# version 08
# baseline (speedup 1.0000x reference)
import contextlib
import numpy as np
import ml_dtypes
import concourse.bass as bass
import concourse.mybir as mybir
from concourse.bass_utils import run_bass_kernel_spmd

F32 = mybir.dt.float32
BF16 = mybir.dt.bfloat16
ALU = mybir.AluOpType
AF = mybir.ActivationFunctionType
AX = mybir.AxisListType

B, SEQ, D = 4, 8192, 1024
DC = D // 128
H = 4096
HC = H // 128
NJ = 9
SHIFTS = [0, -32, -16, -4, -1, 1, 4, 16, 32]
LN_EPS = 1e-5

CT = 4096
HALO = 36
WX = CT + 2 * HALO
WOUT = 4104
SUB = 456
NSUB = 9
ST_SUBS = [(0, 2), (2, 4), (4, 6), (6, 8), (8, 9)]
MAXW = 2 * SUB
B2T = 512
NB2 = 8
OUT_OFF = 4
X_OFF = HALO - OUT_OFF
HR = 458

M1C = float(1.5 * 2.0 ** 23)
M2C = float(1.5 * 2.0 ** 52)
WEPS = float(2.0 ** -40)
WINV = float(2.0 ** 40)

ENGS = ("sync", "gpsimd", "vector", "scalar", "tensor")


class Sched:
    def __init__(self, nc, stack):
        self.nc = nc
        self.q = {e: [] for e in ENGS}
        self.sem = {}
        self.cnt = {}
        self.waited = {e: {} for e in ENGS}
        self.dirty_w = {e: set() for e in ENGS}
        self.dirty_r = {e: set() for e in ENGS}
        self.last_tinst = None
        self._stack = stack
        for name in ("f_gpsimd", "f_vector", "f_scalar", "f_tensor"):
            self.sem[name] = stack.enter_context(nc.semaphore(name))
            self.cnt[name] = 0

    def emit(self, e, fn):
        self.q[e].append(fn)
        if e == "tensor":
            self.last_tinst = len(self.q[e]) - 1

    def wait(self, e, tok):
        if tok is None:
            return
        name, val = tok
        if val <= 0:
            return
        if self.waited[e].get(name, 0) >= val:
            return
        self.waited[e][name] = val
        sem = self.sem[name]
        self.q[e].append(lambda E, s=sem, v=val: E.wait_ge(s, v))

    def waits(self, e, toks):
        for t in toks:
            self.wait(e, t)

    def _hazard(self, e, reads, writes):
        dw, dr = self.dirty_w[e], self.dirty_r[e]
        if any(r in dw for r in reads) or any(w in dw or w in dr for w in writes):
            self.q[e].append(lambda E: E.drain())
            dw.clear()
            dr.clear()
        dr.update(reads)
        dw.update(writes)

    def op(self, e, fn, reads=(), writes=()):
        self._hazard(e, reads, writes)
        self.q[e].append(fn)

    def fence(self, e):
        name = "f_" + e
        self.cnt[name] += 1
        val = self.cnt[name]
        sem = self.sem[name]
        if e == "tensor":
            if self.last_tinst is None:
                self.cnt[name] -= 1
                return (name, self.cnt[name])
            prev = self.q[e][self.last_tinst]
            self.q[e][self.last_tinst] = (
                lambda E, f=prev, s=sem: f(E).then_inc(s, 1))
            self.last_tinst = None
        else:
            self.q[e].append(lambda E, s=sem: E.drain().then_inc(s, 1))
        self.dirty_w[e].clear()
        self.dirty_r[e].clear()
        return (name, val)

    def dma(self, e, out, in_, group="g", **kw):
        name = f"d_{e}_{group}"
        if name not in self.sem:
            self.sem[name] = self._stack.enter_context(self.nc.semaphore(name))
            self.cnt[name] = 0
        self.cnt[name] += 16
        val = self.cnt[name]
        sem = self.sem[name]
        self.q[e].append(
            lambda E, s=sem, o=out, i=in_, k=kw: E.dma_start(out=o, in_=i, **k).then_inc(s, 16))
        return (name, val)


def build_program(debug=()):
    nc = bass.Bass("TRN2", target_bir_lowering=False)
    stack = contextlib.ExitStack()

    x_in = nc.dram_tensor("x_fm", [128, DC, WX], F32, kind="ExternalInput").ap()
    gate_w = nc.dram_tensor("gate_prep", [DC, 128, NJ, D], F32, kind="ExternalInput").ap()
    value_w = nc.dram_tensor("value_prep", [128, DC, D], F32, kind="ExternalInput").ap()
    w1_w = nc.dram_tensor("w1_prep", [HC, 128, D], F32, kind="ExternalInput").ap()
    w2_w = nc.dram_tensor("w2_prep", [DC, 128, H], F32, kind="ExternalInput").ap()
    convw = nc.dram_tensor("conv_prep", [128, 3 * HC], F32, kind="ExternalInput").ap()
    convb = nc.dram_tensor("convb_prep", [128, HC], F32, kind="ExternalInput").ap()
    gamma_d = nc.dram_tensor("gamma_prep", [128, DC], F32, kind="ExternalInput").ap()
    beta_d = nc.dram_tensor("beta_prep", [128, DC], F32, kind="ExternalInput").ap()
    ident_d = nc.dram_tensor("ident", [128, 128], F32, kind="ExternalInput").ap()
    onesb_d = nc.dram_tensor("onesb", [128, 2], BF16, kind="ExternalInput").ap()
    onesf_d = nc.dram_tensor("onesf", [128, 2], F32, kind="ExternalInput").ap()
    masks_d = nc.dram_tensor("masks", [128, 2], F32, kind="ExternalInput").ap()

    out_d = nc.dram_tensor("out_fm", [128, DC, CT], F32, kind="ExternalOutput").ap()

    x2_spill = nc.dram_tensor("x2_spill", [128, DC, WOUT], F32, kind="Internal").ap()
    k2_spill = nc.dram_tensor("k2_spill", [128, HC, CT], BF16, kind="Internal").ap()
    c2_dram = nc.dram_tensor("c2_dram", [1, WOUT + 64], F32, kind="Internal").ap()
    m1_dram = nc.dram_tensor("m1_dram", [128, HC, D], BF16, kind="Internal").ap()
    m2_dram = nc.dram_tensor("m2_dram", [128, DC, H], BF16, kind="Internal").ap()

    dbg = {}
    if "x2" in debug:
        dbg["x2"] = nc.dram_tensor("dbg_x2", [128, DC, WOUT], F32, kind="ExternalOutput").ap()
    if "k2" in debug:
        dbg["k2"] = nc.dram_tensor("dbg_k2", [128, HC, CT], BF16, kind="ExternalOutput").ap()
    if "ws" in debug:
        dbg["ws"] = nc.dram_tensor("dbg_ws", [128, 8], F32, kind="ExternalOutput").ap()
    if "x1" in debug:
        dbg["x1"] = nc.dram_tensor("dbg_x1", [128, DC, WX], BF16, kind="ExternalOutput").ap()

    def alloc(name, shape, dtype, off):
        n = int(np.prod(shape[1:])) * mybir.dt.size(dtype)
        t = nc.alloc_sbuf_tensor_at(name, shape, dtype, offset=off)
        return t, off + ((n + 31) & ~31)

    pers = 17408
    ident, pers = alloc("ident_s", [128, 128], F32, pers)
    onesb, pers = alloc("onesb_s", [128, 2], BF16, pers)
    onesf, pers = alloc("onesf_s", [128, 2], F32, pers)
    ones1f, pers = alloc("ones1f", [1, 128], F32, pers)
    alpha, pers = alloc("alpha_s", [128, 1], F32, pers)
    a_tmp, pers = alloc("a_tmp", [128, 2], F32, pers)
    gamma_s, pers = alloc("gamma_s", [128, DC], F32, pers)
    beta_s, pers = alloc("beta_s", [128, DC], F32, pers)
    convw_s, pers = alloc("convw_s", [128, 3 * HC], F32, pers)
    convb_s, pers = alloc("convb_s", [128, HC], F32, pers)
    masks_s, pers = alloc("masks_s", [128, 2], F32, pers)
    wsvec, pers = alloc("wsvec", [128, 4], F32, pers)
    wpart, pers = alloc("wpart", [128, 132], F32, pers)
    strip1, pers = alloc("strip1", [1, 928], F32, pers)
    strip2, pers = alloc("strip2", [1, 928], F32, pers)
    strip3, pers = alloc("strip3", [1, 928], F32, pers)
    tmst, pers = alloc("tmst", [128, 16], F32, pers)
    tmst2, pers = alloc("tmst2", [128, 16], F32, pers)
    tmst3, pers = alloc("tmst3", [128, 16], F32, pers)
    sb4, pers = alloc("sb4", [1, 512], F32, pers)
    sb4b, pers = alloc("sb4b", [1, 512], F32, pers)
    epsv, pers = alloc("epsv", [1, 2], F32, pers)
    PERS_END = pers

    A0 = (PERS_END + 63) & ~63

    o = A0
    wstream, o = alloc("wstream", [128, 6, 512], F32, o)
    mq_b, o = alloc("mq_b", [128, 512], F32, o)
    mq_d, o = alloc("mq_d", [128, 512], F32, o)
    mq_u, o = alloc("mq_u", [128, 512], F32, o)
    mq_m, o = alloc("mq_m", [128, 2, 512], BF16, o)
    WPH_END = o

    o = WPH_END
    x1_t, o = alloc("x1_t", [128, DC, WX], BF16, o)
    acc_t, o = alloc("acc_t", [128, DC, MAXW], F32, o)
    gate_t, o = alloc("gate_t", [128, 2, 3, D], BF16, o)
    val_t, o = alloc("val_t", [128, DC, D], BF16, o)
    v_t, o = alloc("v_t", [128, 2, MAXW + 64], BF16, o)
    g_t, o = alloc("g_t", [128, 3, MAXW], BF16, o)
    prod_t, o = alloc("prod_t", [128, MAXW], BF16, o)
    xin_t, o = alloc("xin_t", [128, DC, SUB], F32, o)
    stat_t, o = alloc("stat_t", [128, 2, 2 * SUB], BF16, o)
    out2_t, o = alloc("out2_t", [128, DC, SUB], F32, o)
    zt_t, o = alloc("zt_t", [128, 2, SUB], F32, o)
    A_END = o
    assert A_END < 223 * 1024, A_END

    o = A0
    m1_t, o = alloc("m1_t", [128, HC, D], BF16, o)
    k3_t, o = alloc("k3_t", [128, 2, DC, HR], BF16, o)
    k2_t, o = alloc("k2_t", [128, 2, SUB], BF16, o)
    cv_t, o = alloc("cv_t", [128, 6, HR], F32, o)
    bstat_t, o = alloc("bstat_t", [128, 2, 2 * HR], BF16, o)
    m8_t, o = alloc("m8_t", [128, 2, 512], F32, o)
    qb_t, o = alloc("qb_t", [128, 2, HR], F32, o)
    qd_t, o = alloc("qd_t", [128, 2, HR], F32, o)
    hq_t, o = alloc("hq_t", [128, 2, HR], F32, o)
    hx2_t, o = alloc("hx2_t", [128, DC, HR], F32, o)
    u_t, o = alloc("u_t", [128, DC, HR], F32, o)
    hs_t, o = alloc("hs_t", [128, HC, HR], BF16, o)
    sb4t, o = alloc("sb4t", [1, 512], F32, o)
    sb4bt, o = alloc("sb4bt", [1, 512], F32, o)
    tmstt, o = alloc("tmstt", [128, 16], F32, o)
    tmstt2, o = alloc("tmstt2", [128, 16], F32, o)
    tmstt3, o = alloc("tmstt3", [128, 16], F32, o)
    B1_END = o
    assert B1_END < 223 * 1024, B1_END

    o = A0
    m2_t, o = alloc("m2_t", [128, DC, H], BF16, o)
    k2in_t, o = alloc("k2in_t", [128, 2, HC, B2T], BF16, o)
    bx2_t, o = alloc("bx2_t", [128, DC, B2T], F32, o)
    y_t2, o = alloc("y_t", [128, 2, DC, B2T], F32, o)
    B2_END = o
    assert B2_END < 229376, B2_END
    o2 = max(B1_END, B2_END)
    b2stat_t, o2 = alloc("b2stat_t", [128, 2, 2 * B2T], BF16, o2)
    c2sb_t, o2 = alloc("c2sb_t", [128, B2T], F32, o2)
    c2strip, o2 = alloc("c2strip", [1, B2T], F32, o2)
    assert o2 < 229376, o2

    pb = [nc.alloc_psum_tensor(f"pb{i}", [128, 512], F32) for i in range(6)]
    ps_s1 = nc.alloc_psum_tensor("ps_s1", [2, 512], F32)
    ps_s2 = nc.alloc_psum_tensor("ps_s2", [2, 512], F32)

    S = Sched(nc, stack)

    free = {"stats": None, "pb0": None, "pb1": None, "pb2": None, "pb3": None,
            "pb4": None, "pb5": None}

    for ap_o, ap_i in [(ident[:], ident_d), (onesb[:], onesb_d), (onesf[:], onesf_d),
                       (gamma_s[:], gamma_d), (beta_s[:], beta_d), (convw_s[:], convw),
                       (convb_s[:], convb), (masks_s[:], masks_d)]:
        t_const = S.dma("sync", ap_o, ap_i, group="const")
    S.wait("vector", t_const)
    S.wait("tensor", t_const)
    S.wait("scalar", t_const)
    S.wait("gpsimd", t_const)
    S.op("vector", lambda E: E.memset(ones1f[:], 1.0), writes=("ones1f",))
    S.op("vector", lambda E: E.memset(strip1[:], 0.0), writes=("strip1",))
    S.op("vector", lambda E: E.memset(strip2[:], 0.0), writes=("strip2",))
    S.op("vector", lambda E: E.memset(strip3[:], 0.0), writes=("strip3",))
    S.op("vector", lambda E: E.memset(tmst[:], 0.0), writes=("tmst",))
    S.op("vector", lambda E: E.memset(tmst2[:], 0.0), writes=("tmst2",))
    S.op("vector", lambda E: E.memset(tmst3[:], 0.0), writes=("tmst3",))
    S.op("vector", lambda E: E.memset(sb4[:], 0.0), writes=("sb4",))
    S.op("vector", lambda E: E.memset(sb4b[:], 0.0), writes=("sb4b",))
    S.op("vector", lambda E: E.memset(epsv[:], LN_EPS), writes=("epsv",))
    S.op("vector", lambda E: E.memset(a_tmp[:, 0:1], 1.0), writes=("a_tmp",))
    S.op("vector", lambda E: E.tensor_scalar(a_tmp[:, 1:2], a_tmp[:, 0:1], WEPS, -1.0, ALU.add, ALU.add),
         reads=("a_tmp",), writes=("a_tmp",))
    S.op("vector", lambda E: E.tensor_scalar(alpha[:], a_tmp[:, 1:2], -WINV, 1.0, ALU.mult, ALU.add),
         reads=("a_tmp",), writes=("alpha",))
    tok_alpha = S.fence("vector")

    def emit_stats(src_fn, n, sbuf, tag):
        S.waits("tensor", [free["stats"]])
        chunk_toks = [None, None]
        for c in range(DC):
            sb = sbuf[:, c % 2]
            S.wait("scalar", chunk_toks[c % 2])
            S.op("scalar", lambda E, c=c, sb=sb: E.copy(sb[:, 0:n], src_fn(c)),
                 reads=(tag,), writes=("statbuf",))
            S.op("scalar", lambda E, c=c, sb=sb: E.activation(sb[:, n:2 * n], src_fn(c), AF.Square),
                 reads=(tag,), writes=("statbuf",))
            tk = S.fence("scalar")
            S.wait("tensor", tk)
            S.emit("tensor", lambda E, c=c, sb=sb: E.matmul(
                ps_s1[0:1, 0:n], onesb[:, 0:1], sb[:, 0:n], start=(c == 0), stop=(c == DC - 1)))
            S.emit("tensor", lambda E, c=c, sb=sb: E.matmul(
                ps_s2[0:1, 0:n], onesb[:, 0:1], sb[:, n:2 * n], start=(c == 0), stop=(c == DC - 1)))
            chunk_toks[c % 2] = S.fence("tensor")
        return chunk_toks[(DC - 1) % 2]

    def emit_ln_strips(n, tok_stats, b_r=3, b_mu=4):
        S.wait("vector", tok_stats)
        S.op("vector", lambda E: E.tensor_scalar(strip1[0:1, 0:n], ps_s1[0:1, 0:n], 1.0 / D, None, ALU.mult),
             writes=("strip1",))
        S.op("vector", lambda E: E.tensor_scalar(strip2[0:1, 0:n], ps_s2[0:1, 0:n], 1.0 / D, None, ALU.mult),
             writes=("strip2",))
        free["stats"] = S.fence("vector")
        S.op("vector", lambda E: E.tensor_tensor(strip3[0:1, 0:n], strip1[0:1, 0:n], strip1[0:1, 0:n], ALU.mult),
             reads=("strip1",), writes=("strip3",))
        S.op("vector", lambda E: E.scalar_tensor_tensor(strip2[0:1, 0:n], strip3[0:1, 0:n], -1.0, strip2[0:1, 0:n], ALU.mult, ALU.add),
             reads=("strip2", "strip3"), writes=("strip2",))
        tk = S.fence("vector")
        S.wait("scalar", tk)
        S.op("scalar", lambda E: E.activation(strip3[0:1, 0:n], strip2[0:1, 0:n], AF.Sqrt, bias=epsv[0:1, 0:1]),
             reads=("strip2",), writes=("strip3",))
        tk = S.fence("scalar")
        S.wait("vector", tk)
        S.op("vector", lambda E: E.reciprocal(strip2[0:1, 0:n], strip3[0:1, 0:n]),
             reads=("strip3",), writes=("strip2",))
        S.op("vector", lambda E: E.tensor_tensor(strip3[0:1, 0:n], strip1[0:1, 0:n], strip2[0:1, 0:n], ALU.mult),
             reads=("strip1", "strip2"), writes=("strip3",))
        S.op("vector", lambda E: E.tensor_scalar(strip3[0:1, 0:n], strip3[0:1, 0:n], -1.0, None, ALU.mult),
             reads=("strip3",), writes=("strip3",))
        tk = S.fence("vector")
        S.waits("tensor", [tk, free[f"pb{b_r}"], free[f"pb{b_mu}"]])
        S.emit("tensor", lambda E, b_r=b_r: E.matmul(pb[b_r][:, 0:n], ones1f[0:1, :], strip2[0:1, 0:n], start=True, stop=True))
        S.emit("tensor", lambda E, b_mu=b_mu: E.matmul(pb[b_mu][:, 0:n], ones1f[0:1, :], strip3[0:1, 0:n], start=True, stop=True))
        return S.fence("tensor")

    tok_val = S.dma("gpsimd", val_t[:], value_w, group="val")

    xf = [None]
    x1s_tok = {}
    p1_open = [False]
    p1_st = {}

    def p1_A(s):
        c0 = s * SUB
        n = SUB if s < 9 else WX - 9 * SUB
        S.waits("sync", [xf[0]])
        t_l = S.dma("sync", xin_t[:, :, 0:n], x_in[:, :, c0:c0 + n], group="xin")
        S.waits("scalar", [t_l])
        tok_st = emit_stats(lambda c, n=n: xin_t[:, c, 0:n], n, stat_t, "xin")
        p1_st[s] = (c0, n, tok_st)
        p1_open[0] = True

    def p1_BC(s):
        c0, n, tok_st = p1_st[s]
        tok_b = emit_ln_strips(n, tok_st, 2, 5)
        S.wait("vector", tok_b)
        for c in range(DC):
            S.op("vector", lambda E, c=c, n=n: E.tensor_tensor(
                zt_t[:, 0, 0:n], xin_t[:, c, 0:n], pb[2][:, 0:n], ALU.mult),
                reads=("xin",), writes=("zt0",))
            S.op("vector", lambda E, c=c, n=n, c0=c0: E.tensor_tensor(
                x1_t[:, c, c0:c0 + n], zt_t[:, 0, 0:n], pb[5][:, 0:n], ALU.add),
                reads=("zt0",), writes=("x1",))
        xf[0] = S.fence("vector")
        free["pb2"] = xf[0]
        free["pb5"] = xf[0]
        x1s_tok[s] = xf[0]
        p1_open[0] = False

    def w_chunk_ap(which, i):
        if which == 0:
            return w1_w[i // 2, :, 512 * (i % 2): 512 * (i % 2 + 1)]
        d, ss = divmod(i, 8)
        return w2_w[d, :, 512 * ss: 512 * (ss + 1)]

    def m_chunk_ap(which, i):
        if which == 0:
            return m1_dram[:, i // 2, 512 * (i % 2): 512 * (i % 2 + 1)]
        d, ss = divmod(i, 8)
        return m2_dram[:, d, 512 * ss: 512 * (ss + 1)]

    wa_ld = {}
    red_tok = {}

    def wa_load(gi):
        if gi >= 128:
            return
        S.waits("scalar", [red_tok.get(gi - 6)])
        which, i = divmod(gi, 64)
        wa_ld[gi] = S.dma("scalar", wstream[:, gi % 6, :], w_chunk_ap(which, i), group=f"wl{gi % 6}")

    def wa_item(gi):
        wa_load(gi + 4)
        S.wait("vector", wa_ld[gi])
        S.op("vector", lambda E, gi=gi: E.reduce_sum(
            wpart[:, gi:gi + 1], wstream[:, gi % 6, :], axis=AX.X, apply_absolute_value=True),
            reads=("wstream",), writes=("wpart",))
        red_tok[gi] = S.fence("vector")

    tok_ws_c = [None]

    def emit_wfin():
        S.op("vector", lambda E: E.reduce_sum(wpart[:, 128:129], wpart[:, 0:64], axis=AX.X),
             reads=("wpart",), writes=("wpart",))
        S.op("vector", lambda E: E.reduce_sum(wpart[:, 129:130], wpart[:, 64:128], axis=AX.X),
             reads=("wpart",), writes=("wpart",))
        tk = S.fence("vector")
        S.waits("tensor", [tk, free["stats"]])
        S.emit("tensor", lambda E: E.matmul(ps_s1[0:1, 0:1], wpart[:, 128:129], onesf[:, 0:1], start=True, stop=True))
        tk = S.fence("tensor")
        S.wait("vector", tk)
        S.op("vector", lambda E: E.tensor_copy(strip1[0:1, 0:1], ps_s1[0:1, 0:1]), writes=("strip1",))
        tk = S.fence("vector")
        S.wait("tensor", tk)
        S.emit("tensor", lambda E: E.matmul(ps_s1[0:1, 0:1], wpart[:, 129:130], onesf[:, 0:1], start=True, stop=True))
        tk = S.fence("tensor")
        S.wait("vector", tk)
        S.op("vector", lambda E: E.tensor_copy(strip1[0:1, 1:2], ps_s1[0:1, 0:1]), writes=("strip1",))
        free["stats"] = S.fence("vector")
        S.op("vector", lambda E: E.tensor_scalar(strip1[0:1, 2:4], strip1[0:1, 0:2], 1.0 / (H * D), 1e-5, ALU.mult, ALU.max),
             reads=("strip1",), writes=("strip1",))
        S.op("vector", lambda E: E.reciprocal(strip1[0:1, 4:6], strip1[0:1, 2:4]),
             reads=("strip1",), writes=("strip1",))
        S.op("vector", lambda E: E.tensor_scalar(strip1[0:1, 6:8], strip1[0:1, 2:4], 1.0 / 127.0, None, ALU.mult),
             reads=("strip1",), writes=("strip1",))
        tk = S.fence("vector")
        S.waits("tensor", [tk, free["pb5"]])
        S.emit("tensor", lambda E: E.matmul(pb[5][:, 0:4], ones1f[0:1, :], strip1[0:1, 4:8], start=True, stop=True))
        tk = S.fence("tensor")
        S.wait("vector", tk)
        S.op("vector", lambda E: E.tensor_copy(wsvec[:, 0:4], pb[5][:, 0:4]), writes=("wsvec",))
        tok_ws_c[0] = S.fence("vector")
        free["pb5"] = tok_ws_c[0]
        if "ws" in dbg:
            S.wait("scalar", tok_ws_c[0])
            S.dma("scalar", dbg["ws"][:, 0:4], wsvec[:, 0:4], group="dbg")

    q_tok = {}
    st_toks = {}
    wq_ld = {}

    def wq_load(gi):
        if gi >= 128:
            return
        S.waits("scalar", [red_tok[127], q_tok.get(gi - 6)])
        which, i = divmod(gi, 64)
        wq_ld[gi] = S.dma("scalar", wstream[:, gi % 6, :], w_chunk_ap(which, i), group=f"wq{gi % 6}")

    def wq_item(gi):
        wq_load(gi + 4)
        which, i = divmod(gi, 64)
        wscol = wsvec[:, which:which + 1]
        buf = wstream[:, gi % 6, :]
        mb = mq_m[:, gi % 2, :]
        S.waits("vector", [wq_ld[gi], tok_alpha, tok_ws_c[0], st_toks.get(gi - 2)])
        S.op("vector", lambda E, b=buf, w=wscol: E.tensor_scalar(mq_u[:], b, w, None, ALU.mult),
             reads=("wstream", "wsvec"), writes=("mq_u",))
        S.op("vector", lambda E: E.tensor_scalar(mq_b[:], mq_u[:], M1C, -M1C, ALU.add, ALU.add),
             reads=("mq_u",), writes=("mq_b",))
        S.op("vector", lambda E: E.tensor_scalar(mq_d[:], mq_u[:], M2C, -M2C, ALU.add, ALU.add),
             reads=("mq_u",), writes=("mq_d",))
        S.op("vector", lambda E: E.scalar_tensor_tensor(mq_u[:], mq_b[:], alpha[:], mq_d[:], ALU.mult, ALU.add),
             reads=("mq_b", "mq_d", "alpha"), writes=("mq_u",))
        S.op("vector", lambda E, m=mb: E.tensor_scalar(m, mq_u[:], 1.0, -1.0, ALU.min, ALU.max),
             reads=("mq_u",), writes=("mq_m",))
        q_tok[gi] = S.fence("vector")
        S.wait("scalar", q_tok[gi])
        st_toks[gi] = S.dma("scalar", m_chunk_ap(which, i), mb, group=f"mst{gi % 2}")

    work_q = []
    for gi in range(56, 128):
        work_q.append(("wa", gi, 1))
    work_q.append(("fin", 0, 4))
    for gi in range(128):
        work_q.append(("wq", gi, 4))

    def run_item(it):
        kind, gi, _ = it
        if kind == "wa":
            wa_item(gi)
        elif kind == "fin":
            emit_wfin()
            for g in range(4):
                wq_load(g)
        else:
            wq_item(gi)

    def sa_hook(budget):
        while budget > 0 and work_q:
            if work_q[0][0] == "fin" and p1_open[0]:
                break
            it = work_q.pop(0)
            run_item(it)
            budget -= it[2]

    for g in range(4):
        wa_load(g)
    p1_A(0)
    for gi in range(0, 8):
        wa_item(gi)
    p1_BC(0)
    p1_A(1)
    for gi in range(8, 24):
        wa_item(gi)
    p1_BC(1)
    p1_A(2)
    for gi in range(24, 40):
        wa_item(gi)
    p1_BC(2)
    for gi in range(40, 56):
        wa_item(gi)

    p1_sched = {}
    for i in range(4):
        subs = [ss for ss in (2 * i + 3, 2 * i + 4) if ss <= 9]
        if subs:
            p1_sched[(i, 1)] = ("A", subs[0])
            p1_sched[(i, 3)] = ("BC", subs[0])
            if len(subs) > 1:
                p1_sched[(i, 5)] = ("A", subs[1])
                p1_sched[(i, 7)] = ("BC", subs[1])

    x2_sub_store = {}
    acc_free_scalar = None
    prev_ST_out2_v = None
    v_half_free = [None, None]
    n_dc_total = len(ST_SUBS) * DC
    gthird_seq = []
    for sti in range(len(ST_SUBS)):
        for dc in range(DC):
            for third in range(3):
                gthird_seq.append((sti, dc, third))
    g_loaded = {}
    g_slot_free = {}
    g_next = [0]

    def prefetch_gates(upto):
        while g_next[0] < min(upto, len(gthird_seq)):
            gi = g_next[0]
            sti_, dc_, third_ = gthird_seq[gi]
            slot = gi % 2
            S.waits("gpsimd", [g_slot_free.get(gi - 2)])
            g_loaded[gi] = S.dma("gpsimd", gate_t[:, slot, :, :],
                                 gate_w[dc_, :, 3 * third_: 3 * third_ + 3, :],
                                 group=f"gate{slot}")
            g_next[0] += 1

    gate_banks = [0, 1, 3, 4]
    g_free = {0: None, 1: None, 2: None}

    for sti, (s_lo, s_hi) in enumerate(ST_SUBS):
        W = (s_hi - s_lo) * SUB
        o_lo = s_lo * SUB
        xv_lo = o_lo + X_OFF - 32
        WV = W + 64
        bank_i = [0]
        for dc in range(DC):
            half = dc % 2
            vb = v_t[:, half, :]
            S.waits("tensor", [tok_val, x1s_tok[s_hi]])
            S.waits("scalar", [v_half_free[half]])
            nvs = (WV + 511) // 512
            vban = [2, 5]
            for sv in range(nvs):
                a = sv * 512
                nn = min(512, WV - a)
                bk = vban[sv % 2]
                S.waits("tensor", [free[f"pb{bk}"]])
                for kc in range(DC):
                    S.emit("tensor", lambda E, dc=dc, kc=kc, a=a, nn=nn, xv_lo=xv_lo, bk=bk: E.matmul(
                        pb[bk][:, 0:nn], val_t[:, kc, dc * 128:(dc + 1) * 128],
                        x1_t[:, kc, xv_lo + a: xv_lo + a + nn],
                        start=(kc == 0), stop=(kc == DC - 1)))
                tk = S.fence("tensor")
                S.waits("scalar", [tk])
                S.op("scalar", lambda E, a=a, nn=nn, vb=vb, bk=bk: E.copy(vb[:, a:a + nn], pb[bk][:, 0:nn]),
                     writes=("v",))
                free[f"pb{bk}"] = S.fence("scalar")
            tok_v = free[f"pb{vban[(nvs - 1) % 2]}"]
            g_base = (sti * DC + dc) * 3
            prefetch_gates(g_base + 2)
            mac_tok = {}
            for j in range(NJ):
                gthird = g_base + j // 3
                slot = gthird % 2
                jj = j % 3
                gb = g_t[:, j % 3, :]
                S.waits("tensor", [g_loaded[gthird]])
                nsub = s_hi - s_lo
                for sv in range(nsub):
                    a = sv * SUB
                    gsub = j * nsub + sv
                    bk = gate_banks[bank_i[0] % 4]
                    bank_i[0] += 1
                    S.waits("tensor", [free[f"pb{bk}"]])
                    for kc in range(DC):
                        S.emit("tensor", lambda E, slot=slot, jj=jj, kc=kc, a=a, bk=bk, o_lo=o_lo: E.matmul(
                            pb[bk][:, 0:SUB], gate_t[:, slot, jj, kc * 128:(kc + 1) * 128],
                            x1_t[:, kc, o_lo + X_OFF + a: o_lo + X_OFF + a + SUB],
                            start=(kc == 0), stop=(kc == DC - 1)))
                    tk = S.fence("tensor")
                    S.waits("scalar", [tk, g_free[j % 3]])
                    S.op("scalar", lambda E, a=a, gb=gb, bk=bk: E.activation(gb[:, a:a + SUB], pb[bk][:, 0:SUB], AF.Sigmoid),
                         writes=("g",))
                    free[f"pb{bk}"] = S.fence("scalar")
                tok_g = free[f"pb{gate_banks[(bank_i[0] - 1) % 4]}"]
                sh = SHIFTS[j]
                S.waits("vector", [tok_g, tok_v, acc_free_scalar])
                S.op("vector", lambda E, gb=gb, vb=vb, sh=sh, W=W: E.tensor_tensor(
                    prod_t[:, 0:W], gb[:, 0:W], vb[:, 32 + sh: 32 + sh + W], ALU.mult),
                    reads=("g", "v"), writes=("prod",))
                if j == 0:
                    S.op("vector", lambda E, dc=dc, W=W: E.tensor_copy(acc_t[:, dc, 0:W], prod_t[:, 0:W]),
                         reads=("prod",), writes=("acc",))
                else:
                    S.op("vector", lambda E, dc=dc, W=W: E.tensor_tensor(
                        acc_t[:, dc, 0:W], acc_t[:, dc, 0:W], prod_t[:, 0:W], ALU.add),
                        reads=("prod", "acc"), writes=("acc",))
                mac_tok[j] = S.fence("vector")
                g_free[j % 3] = mac_tok[j]
                if j % 3 == 2:
                    g_slot_free[gthird] = mac_tok[j]
                    prefetch_gates(gthird + 3)
            v_half_free[half] = mac_tok[8]
            piece = p1_sched.get((sti, dc))
            if piece is not None:
                (p1_A if piece[0] == "A" else p1_BC)(piece[1])
                sa_hook(6)
            else:
                sa_hook(16)
        tok_acc_done = S.fence("vector")

        for ssub in range(s_lo, s_hi):
            a = (ssub - s_lo) * SUB
            n = SUB
            S.waits("sync", [xf[0]])
            t_xr = S.dma("sync", xin_t[:, :, 0:n], x_in[:, :, ssub * SUB + X_OFF: ssub * SUB + X_OFF + n], group="xin")
            S.waits("scalar", [tok_acc_done])
            tok_st = emit_stats(lambda c, a=a, n=n: acc_t[:, c, a:a + n], n, stat_t, "acc")
            tok_b = emit_ln_strips(n, tok_st)
            S.waits("vector", [tok_b, prev_ST_out2_v, x2_sub_store.get(ssub - 1), x2_sub_store.get((ssub - 1, "dbg"))])
            for c in range(DC):
                S.op("vector", lambda E, c=c, a=a, n=n: E.tensor_tensor(
                    zt_t[:, 0, 0:n], acc_t[:, c, a:a + n], pb[3][:, 0:n], ALU.mult),
                    reads=("acc",), writes=("zt0",))
                S.op("vector", lambda E, n=n: E.tensor_tensor(
                    zt_t[:, 1, 0:n], zt_t[:, 0, 0:n], pb[4][:, 0:n], ALU.add),
                    reads=("zt0",), writes=("zt1",))
                S.op("vector", lambda E, c=c, n=n: E.tensor_scalar(
                    out2_t[:, c, 0:n], zt_t[:, 1, 0:n], gamma_s[:, c:c + 1], beta_s[:, c:c + 1], ALU.mult, ALU.add),
                    reads=("zt1",), writes=("out2",))
            tok_out2 = S.fence("vector")
            free["pb3"] = tok_out2
            free["pb4"] = tok_out2
            S.waits("scalar", [tok_out2])
            tok_st = emit_stats(lambda c, n=n: out2_t[:, c, 0:n], n, stat_t, "out2")
            tok_b = emit_ln_strips(n, tok_st)
            S.waits("vector", [tok_b, t_xr])
            for c in range(DC):
                S.op("vector", lambda E, c=c, n=n: E.tensor_tensor(
                    zt_t[:, 0, 0:n], out2_t[:, c, 0:n], pb[3][:, 0:n], ALU.mult),
                    reads=("out2",), writes=("zt0",))
                S.op("vector", lambda E, c=c, n=n: E.tensor_tensor(
                    zt_t[:, 1, 0:n], xin_t[:, c, 0:n], pb[4][:, 0:n], ALU.add),
                    reads=("xin",), writes=("zt1",))
                S.op("vector", lambda E, c=c, n=n: E.tensor_tensor(
                    out2_t[:, c, 0:n], zt_t[:, 0, 0:n], zt_t[:, 1, 0:n], ALU.add),
                    reads=("zt0", "zt1"), writes=("out2",))
            tok_x2 = S.fence("vector")
            free["pb3"] = tok_x2
            free["pb4"] = tok_x2
            xf[0] = tok_x2
            S.waits("scalar", [tok_x2])
            tsp = S.dma("scalar", x2_spill[:, :, ssub * SUB:(ssub + 1) * SUB], out2_t[:, :, 0:n], group="x2sp")
            x2_sub_store[ssub] = tsp
            if "x2" in dbg:
                x2_sub_store[(ssub, "dbg")] = S.dma("scalar", dbg["x2"][:, :, ssub * SUB:(ssub + 1) * SUB], out2_t[:, :, 0:n], group="dbgx2")
            sa_hook(4)
        acc_free_scalar = S.fence("scalar")
        prev_ST_out2_v = tok_x2
    sa_hook(10 ** 9)
    tok_mq_done = st_toks[127]
    tok_mq_done2 = st_toks[126]
    tok_x1dbg = None
    if "x1" in dbg:
        S.wait("scalar", x1s_tok[9])
        tok_x1dbg = S.dma("scalar", dbg["x1"][:], x1_t[:], group="dbgx1")
    tokA_v = S.fence("vector")
    tokA_s = S.fence("scalar")
    tokA_t = S.fence("tensor")
    tok_spill_all = x2_sub_store[NSUB - 1]
    tok_spill_dbg = x2_sub_store.get((NSUB - 1, 'dbg'))

    S.waits("gpsimd", [tok_mq_done, tok_mq_done2, tokA_v, tokA_s, tokA_t, tok_x1dbg])
    tok_m1 = S.dma("gpsimd", m1_t[:], m1_dram, group="m1")
    S.waits("vector", [tokA_v, tokA_s, tokA_t, tok_spill_all, tok_spill_dbg, tok_x1dbg])
    S.op("vector", lambda E: E.memset(m8_t[:], 0.0), writes=("m8",))
    S.op("vector", lambda E: E.memset(tmstt[:], 0.0), writes=("tmstt",))
    S.op("vector", lambda E: E.memset(tmstt2[:], 0.0), writes=("tmstt2",))
    S.op("vector", lambda E: E.memset(tmstt3[:], 0.0), writes=("tmstt3",))
    S.op("vector", lambda E: E.memset(sb4t[:], 0.0), writes=("sb4t",))
    S.op("vector", lambda E: E.memset(sb4bt[:], 0.0), writes=("sb4bt",))
    tok_b1_init = S.fence("vector")
    S.waits("scalar", [tok_spill_all, tok_spill_dbg, tok_x1dbg, tok_b1_init])
    S.waits("sync", [tok_spill_all, tok_spill_dbg, tok_x1dbg, tokA_v, tokA_s, tokA_t, tok_b1_init])
    S.waits("tensor", [tok_b1_init])

    n = HR
    nq = SUB
    free["strips"] = None
    free["m8h"] = None
    free["sb4h"] = None
    free["sb4tl"] = None
    ST8 = {}

    def H_step(t, k, st):
        if k == 0:
            lo = t * SUB - 1
            llo = max(lo, 0)
            lhi = min(lo + HR, WOUT)
            n_load = lhi - llo
            dst0 = llo - lo
            prev = ST8.get(t - 1, {})
            S.waits("sync", [prev.get("hx2_free"), prev.get("hx2_free_s")])
            st["t_x2"] = S.dma("sync", hx2_t[:, :, dst0:dst0 + n_load], x2_spill[:, :, llo:lhi], group="hx2")
            S.waits("vector", [st["t_x2"]])
            if dst0 > 0:
                S.op("vector", lambda E: E.memset(hx2_t[:, :, 0:1], 0.0), writes=("hx2",))
            if lo + HR > WOUT:
                S.op("vector", lambda E, e0=dst0 + n_load: E.memset(hx2_t[:, :, e0:HR], 0.0), writes=("hx2",))
            if t == 0:
                S.op("vector", lambda E: E.tensor_scalar(hx2_t[:, :, 1:5], hx2_t[:, :, 1:5], masks_s[:, 0:1], None, ALU.mult),
                     reads=("hx2",), writes=("hx2",))
            if t == NSUB - 1:
                S.op("vector", lambda E: E.tensor_scalar(hx2_t[:, :, 453:457], hx2_t[:, :, 453:457], masks_s[:, 1:2], None, ALU.mult),
                     reads=("hx2",), writes=("hx2",))
            st["tok_x2in"] = S.fence("vector")
        elif k == 1:
            S.waits("scalar", [st["tok_x2in"]])
            st["tok_stats"] = emit_stats(lambda c: hx2_t[:, c, 0:n], n, bstat_t, "hx2")
            st["stats_sc"] = S.fence("scalar")
        elif k == 2:
            S.waits("vector", [st["tok_stats"], free["strips"]])
            S.op("vector", lambda E: E.tensor_scalar(strip1[0:1, 0:n], ps_s1[0:1, 0:n], 1.0 / D, None, ALU.mult),
                 writes=("strip1",))
            S.op("vector", lambda E: E.tensor_scalar(strip2[0:1, 0:n], ps_s2[0:1, 0:n], 1.0 / D, None, ALU.mult),
                 writes=("strip2",))
            free["stats"] = S.fence("vector")
            S.op("vector", lambda E: E.tensor_tensor(strip3[0:1, 0:n], strip1[0:1, 0:n], strip1[0:1, 0:n], ALU.mult),
                 reads=("strip1",), writes=("strip3",))
            S.op("vector", lambda E: E.scalar_tensor_tensor(strip2[0:1, 0:n], strip3[0:1, 0:n], -1.0, strip2[0:1, 0:n], ALU.mult, ALU.add),
                 reads=("strip2", "strip3"), writes=("strip2",))
            tk = S.fence("vector")
            S.wait("scalar", tk)
            S.op("scalar", lambda E: E.activation(strip3[0:1, 0:n], strip2[0:1, 0:n], AF.Sqrt, bias=epsv[0:1, 0:1]),
                 reads=("strip2",), writes=("strip3",))
            st["sq_done"] = S.fence("scalar")
        elif k == 3:
            S.wait("vector", st["sq_done"])
            S.op("vector", lambda E: E.reciprocal(strip2[0:1, 0:n], strip3[0:1, 0:n]),
                 reads=("strip3",), writes=("strip2",))
            S.op("vector", lambda E: E.tensor_scalar(strip3[0:1, 0:n], strip1[0:1, 0:n], -1.0, None, ALU.mult),
                 reads=("strip1",), writes=("strip3",))
            tk = S.fence("vector")
            S.waits("tensor", [tk, free["pb3"]])
            S.emit("tensor", lambda E: E.matmul(pb[3][:, 0:n], ones1f[0:1, :], strip3[0:1, 0:n], start=True, stop=True))
            st["mu_bc"] = S.fence("tensor")
        elif k == 4:
            S.waits("vector", [st["mu_bc"], ST8.get(t - 1, {}).get("tok_k3")])
            for c in range(DC):
                S.op("vector", lambda E, c=c: E.tensor_tensor(u_t[:, c, 0:n], hx2_t[:, c, 0:n], pb[3][:, 0:n], ALU.add),
                     reads=("hx2",), writes=("u",))
            st["tok_u"] = S.fence("vector")
            free["pb3"] = st["tok_u"]
            st["hx2_free"] = st["tok_u"]
            st["hx2_free_s"] = st["stats_sc"]
        elif k == 5:
            S.waits("vector", [free["m8h"]])
            S.op("vector", lambda E: E.reduce_max(m8_t[:, 0, 0:n], u_t[:].rearrange("p c n -> p n c"), axis=AX.X, apply_absolute_value=True),
                 reads=("u",), writes=("m8",))
            tk = S.fence("vector")
            S.waits("tensor", [tk, free["pb2"]])
            for bkk in range(4):
                S.emit("tensor", lambda E, bkk=bkk: E.transpose(
                    pb[2][:, bkk * 128:(bkk + 1) * 128], m8_t[:, 0, bkk * 128:(bkk + 1) * 128], ident[:]))
            S.waits("tensor", [free["pb5"], free["stats"]])
            for bkk in range(4):
                S.emit("tensor", lambda E, bkk=bkk: E.transpose(
                    pb[5][:, bkk:bkk + 1], strip2[0:1, bkk * 128:(bkk + 1) * 128], ident[0:1, 0:1]))
            tk_tr = S.fence("tensor")
            S.wait("vector", tk_tr)
            S.op("vector", lambda E: E.reduce_max(
                tmst[:, 0:4], pb[2][:, 0:512].rearrange("p (b q) -> p b q", q=128), axis=AX.X),
                writes=("tmst",))
            S.op("vector", lambda E: E.tensor_copy(tmst2[:, 0:4], pb[5][:, 0:4]), writes=("tmst2",))
            st["tm1"] = S.fence("vector")
            free["m8h"] = st["tm1"]
            free["pb2"] = st["tm1"]
            free["pb5"] = st["tm1"]
        elif k == 6:
            S.op("vector", lambda E: E.tensor_tensor(tmst[:, 4:8], tmst[:, 0:4], tmst2[:, 0:4], ALU.mult),
                 reads=("tmst", "tmst2"), writes=("tmst",))
            S.op("vector", lambda E: E.tensor_scalar(tmst[:, 8:12], tmst[:, 4:8], 1e-5, None, ALU.max),
                 reads=("tmst",), writes=("tmst",))
            S.op("vector", lambda E: E.reciprocal(tmst3[:, 0:4], tmst[:, 8:12]),
                 reads=("tmst",), writes=("tmst3",))
            S.op("vector", lambda E: E.tensor_scalar(tmst3[:, 4:8], tmst3[:, 0:4], 127.0, None, ALU.mult),
                 reads=("tmst3",), writes=("tmst3",))
            S.op("vector", lambda E: E.tensor_tensor(tmst3[:, 8:12], tmst3[:, 4:8], tmst2[:, 0:4], ALU.mult),
                 reads=("tmst3", "tmst2"), writes=("tmst3",))
            S.op("vector", lambda E: E.tensor_scalar(tmst2[:, 4:8], tmst[:, 8:12], wsvec[:, 2:3], None, ALU.mult),
                 reads=("tmst", "wsvec"), writes=("tmst2",))
            tk = S.fence("vector")
            S.waits("tensor", [tk, free["pb5"], free["stats"], free["sb4h"]])
            for bkk in range(4):
                S.emit("tensor", lambda E, bkk=bkk: E.transpose(
                    pb[5][0:1, bkk * 128:(bkk + 1) * 128], tmst3[:, 8 + bkk:9 + bkk], ident[:]))
                S.emit("tensor", lambda E, bkk=bkk: E.transpose(
                    ps_s2[0:1, bkk * 128:(bkk + 1) * 128], tmst2[:, 4 + bkk:5 + bkk], ident[:]))
            tk_tr = S.fence("tensor")
            S.wait("vector", tk_tr)
            S.op("vector", lambda E: E.tensor_copy(sb4[0:1, 0:512], pb[5][0:1, 0:512]), writes=("sb4",))
            S.op("vector", lambda E: E.tensor_copy(sb4b[0:1, 0:512], ps_s2[0:1, 0:512]), writes=("sb4b",))
            tk = S.fence("vector")
            free["pb5"] = tk
            free["stats"] = tk
            S.waits("tensor", [tk, free["pb4"], free["pb5"]])
            S.emit("tensor", lambda E: E.matmul(pb[4][:, 0:512], ones1f[0:1, :], sb4[0:1, 0:512], start=True, stop=True))
            S.emit("tensor", lambda E: E.matmul(pb[5][:, 0:512], ones1f[0:1, :], sb4b[0:1, 0:512], start=True, stop=True))
            st["bc_xq"] = S.fence("tensor")
            free["sb4h"] = st["bc_xq"]
            free["strips"] = st["bc_xq"]
        elif k == 7:
            kslot = t % 2
            S.waits("vector", [st["bc_xq"], ST8.get(t - 1, {}).get("w1_done")])
            for c in range(DC):
                S.op("vector", lambda E, c=c: E.tensor_tensor(hq_t[:, 0, 0:n], u_t[:, c, 0:n], pb[4][:, 0:n], ALU.mult),
                     reads=("u",), writes=("hq0",))
                S.op("vector", lambda E: E.tensor_scalar(qb_t[:, 0, 0:n], hq_t[:, 0, 0:n], M1C, -M1C, ALU.add, ALU.add),
                     reads=("hq0",), writes=("qb0",))
                S.op("vector", lambda E: E.tensor_scalar(qd_t[:, 0, 0:n], hq_t[:, 0, 0:n], M2C, -M2C, ALU.add, ALU.add),
                     reads=("hq0",), writes=("qd0",))
                S.op("vector", lambda E: E.scalar_tensor_tensor(hq_t[:, 1, 0:n], qb_t[:, 0, 0:n], alpha[:], qd_t[:, 0, 0:n], ALU.mult, ALU.add),
                     reads=("qb0", "qd0", "alpha"), writes=("hq1",))
                S.op("vector", lambda E, c=c, kslot=kslot: E.tensor_tensor(k3_t[:, kslot, c, 0:n], hq_t[:, 1, 0:n], pb[5][:, 0:n], ALU.mult),
                     reads=("hq1",), writes=("k3",))
            st["tok_k3"] = S.fence("vector")
            free["pb4"] = st["tok_k3"]
            free["pb5"] = st["tok_k3"]

    NHEAD = 8

    def T_step(t, k, st):
        kslot = t % 2
        if k < 8:
            if k == 0:
                S.waits("tensor", [tok_m1, st["tok_k3"]])
                st["silu_pending"] = []
            for hc in range(4 * k, 4 * k + 4):
                bkid = hc % 2
                pbk = pb[bkid]
                pp = 3 * (hc % 2)
                S.waits("tensor", [free[f"pb{bkid}"]])
                for kc in range(DC):
                    S.emit("tensor", lambda E, hc=hc, kc=kc, pbk=pbk, kslot=kslot: E.matmul(
                        pbk[:, 0:n], m1_t[:, hc, kc * 128:(kc + 1) * 128], k3_t[:, kslot, kc, 0:n],
                        start=(kc == 0), stop=(kc == DC - 1)))
                tk_h = S.fence("tensor")
                S.waits("scalar", [tk_h])
                S.op("scalar", lambda E, hc=hc, pbk=pbk, pp=pp: E.activation(
                    cv_t[:, pp + 0, 0:n - 2], pbk[:, 0:n - 2], AF.Copy,
                    scale=convw_s[:, 3 * hc:3 * hc + 1]),
                    writes=(f"cv0{pp}",))
                tk0 = S.fence("scalar")
                if st["silu_pending"]:
                    phc, ppp, ptk2 = st["silu_pending"].pop()
                    S.waits("scalar", [ptk2])
                    S.op("scalar", lambda E, phc=phc, ppp=ppp: E.activation(
                        hs_t[:, phc, 0:n - 2], cv_t[:, ppp + 2, 0:n - 2], AF.Silu,
                        bias=convb_s[:, phc:phc + 1]),
                        reads=(f"cv2{ppp}",), writes=("hs",))
                S.waits("vector", [tk0, tk_h])
                S.op("vector", lambda E, hc=hc, pbk=pbk, pp=pp: E.scalar_tensor_tensor(
                    cv_t[:, pp + 1, 0:n - 2], pbk[:, 1:n - 1], convw_s[:, 3 * hc + 1:3 * hc + 2], cv_t[:, pp + 0, 0:n - 2], ALU.mult, ALU.add),
                    reads=(f"cv0{pp}",), writes=(f"cv1{pp}",))
                S.op("vector", lambda E, hc=hc, pbk=pbk, pp=pp: E.scalar_tensor_tensor(
                    cv_t[:, pp + 2, 0:n - 2], pbk[:, 2:n], convw_s[:, 3 * hc + 2:3 * hc + 3], cv_t[:, pp + 1, 0:n - 2], ALU.mult, ALU.add),
                    reads=(f"cv1{pp}",), writes=(f"cv2{pp}",))
                tk2 = S.fence("vector")
                free[f"pb{bkid}"] = tk2
                st["silu_pending"].append((hc, pp, tk2))
            if k == 7:
                st["w1_done"] = tk2
                for phc, ppp, ptk2 in st["silu_pending"]:
                    S.waits("scalar", [ptk2])
                    S.op("scalar", lambda E, phc=phc, ppp=ppp: E.activation(
                        hs_t[:, phc, 0:n - 2], cv_t[:, ppp + 2, 0:n - 2], AF.Silu,
                        bias=convb_s[:, phc:phc + 1]),
                        reads=(f"cv2{ppp}",), writes=("hs",))
                st["tok_hs"] = S.fence("scalar")
        elif k == 8:
            S.waits("vector", [st["tok_hs"], free.get("m8t")])
            S.op("vector", lambda E: E.memset(m8_t[:, 1, nq:512], 0.0), writes=("m8t",))
            S.op("vector", lambda E: E.reduce_max(m8_t[:, 1, 0:nq], hs_t[:, :, 0:nq].rearrange("p c n -> p n c"), axis=AX.X, apply_absolute_value=True),
                 reads=("hs",), writes=("m8t",))
            st["m8b"] = S.fence("vector")
        elif k == 9:
            S.waits("tensor", [st["m8b"], free["pb2"]])
            for bkk in range(4):
                S.emit("tensor", lambda E, bkk=bkk: E.transpose(
                    pb[2][:, bkk * 128:(bkk + 1) * 128], m8_t[:, 1, bkk * 128:(bkk + 1) * 128], ident[:]))
            tk_tr = S.fence("tensor")
            S.wait("vector", tk_tr)
            S.op("vector", lambda E: E.reduce_max(
                tmstt[:, 0:4], pb[2][:, 0:512].rearrange("p (b q) -> p b q", q=128), axis=AX.X),
                writes=("tmstt",))
            free["pb2"] = S.fence("vector")
            free["m8t"] = free["pb2"]
            S.op("vector", lambda E: E.tensor_scalar(tmstt[:, 4:8], tmstt[:, 0:4], 1e-5, None, ALU.max),
                 reads=("tmstt",), writes=("tmstt",))
            S.op("vector", lambda E: E.reciprocal(tmstt2[:, 0:4], tmstt[:, 4:8]),
                 reads=("tmstt",), writes=("tmstt2",))
            S.op("vector", lambda E: E.tensor_scalar(tmstt2[:, 4:8], tmstt2[:, 0:4], 127.0, None, ALU.mult),
                 reads=("tmstt2",), writes=("tmstt2",))
            S.op("vector", lambda E: E.tensor_scalar(tmstt3[:, 0:4], tmstt[:, 4:8], wsvec[:, 3:4], None, ALU.mult),
                 reads=("tmstt", "wsvec"), writes=("tmstt3",))
            tk = S.fence("vector")
            S.waits("tensor", [tk, free["pb5"], free["stats"], free["sb4tl"]])
            for bkk in range(4):
                S.emit("tensor", lambda E, bkk=bkk: E.transpose(
                    pb[5][0:1, bkk * 128:(bkk + 1) * 128], tmstt2[:, 4 + bkk:5 + bkk], ident[:]))
                S.emit("tensor", lambda E, bkk=bkk: E.transpose(
                    ps_s2[0:1, bkk * 128:(bkk + 1) * 128], tmstt3[:, bkk:bkk + 1], ident[:]))
            tk_tr = S.fence("tensor")
            S.waits("vector", [tk_tr, ST8.get(t - 1, {}).get("c2st")])
            S.op("vector", lambda E: E.tensor_copy(sb4t[0:1, 0:512], pb[5][0:1, 0:512]), writes=("sb4t",))
            S.op("vector", lambda E: E.tensor_copy(sb4bt[0:1, 0:512], ps_s2[0:1, 0:512]), writes=("sb4bt",))
            tk = S.fence("vector")
            free["pb5"] = tk
            free["stats"] = tk
            S.waits("scalar", [tk, ST8.get(t - 1, {}).get("c2st")])
            st["c2st"] = S.dma("scalar", c2_dram[0:1, t * SUB: t * SUB + 512], sb4bt[0:1, 0:512], group="c2st")
            S.waits("tensor", [tk, free["pb3"]])
            S.emit("tensor", lambda E: E.matmul(pb[3][:, 0:512], ones1f[0:1, :], sb4t[0:1, 0:512], start=True, stop=True))
            st["bc_s2"] = S.fence("tensor")
            free["sb4tl"] = st["bc_s2"]
        elif k == 10:
            o_lo2 = t * SUB
            klo = max(o_lo2, OUT_OFF)
            khi = min(o_lo2 + SUB, OUT_OFF + CT)
            a0 = klo - o_lo2
            S.waits("vector", [st["bc_s2"]])
            prev = ST8.get(t - 1, {})
            k2st = {-2: prev.get("k2_last0", []), -1: prev.get("k2_last1", [])}
            for hc in range(HC):
                S.waits("vector", list(k2st.get(hc - 2, [])))
                S.op("vector", lambda E, hc=hc: E.tensor_tensor(hq_t[:, 1, 0:nq], hs_t[:, hc, 0:nq], pb[3][:, 0:nq], ALU.mult),
                     reads=("hs",), writes=("hq1",))
                S.op("vector", lambda E: E.tensor_scalar(qb_t[:, 1, 0:nq], hq_t[:, 1, 0:nq], M1C, -M1C, ALU.add, ALU.add),
                     reads=("hq1",), writes=("qb1",))
                S.op("vector", lambda E: E.tensor_scalar(qd_t[:, 1, 0:nq], hq_t[:, 1, 0:nq], M2C, -M2C, ALU.add, ALU.add),
                     reads=("hq1",), writes=("qd1",))
                S.op("vector", lambda E, hc=hc: E.scalar_tensor_tensor(k2_t[:, hc % 2, 0:nq], qb_t[:, 1, 0:nq], alpha[:], qd_t[:, 1, 0:nq], ALU.mult, ALU.add),
                     reads=("qb1", "qd1", "alpha"), writes=(f"k2{hc % 2}",))
                tkk = S.fence("vector")
                S.waits("scalar", [tkk])
                toks = [S.dma("scalar", k2_spill[:, hc, klo - OUT_OFF:khi - OUT_OFF],
                              k2_t[:, hc % 2, a0:a0 + (khi - klo)], group=f"k2st{hc % 2}")]
                if "k2" in dbg:
                    toks.append(S.dma("scalar", dbg["k2"][:, hc, klo - OUT_OFF:khi - OUT_OFF],
                                      k2_t[:, hc % 2, a0:a0 + (khi - klo)], group=f"dbgk2{hc % 2}"))
                k2st[hc] = toks
            st["k2_last0"] = k2st[30]
            st["k2_last1"] = k2st[31]
            free["pb3"] = S.fence("vector")

    NTAIL = 11
    ST8[0] = {}
    for k in range(NHEAD):
        H_step(0, k, ST8[0])
    for t in range(NSUB):
        st = ST8[t]
        if t + 1 < NSUB:
            ST8[t + 1] = {}
            for k in range(NTAIL):
                if k < NHEAD:
                    H_step(t + 1, k, ST8[t + 1])
                T_step(t, k, st)
        else:
            for k in range(NTAIL):
                T_step(t, k, st)

    k2_final = list(ST8[NSUB - 1]["k2_last0"]) + list(ST8[NSUB - 1]["k2_last1"])
    tok_k2_dbg = None
    c2st_tok = ST8[NSUB - 1]["c2st"]
    tokB1_v = S.fence("vector")
    tokB1_s = S.fence("scalar")
    tokB1_t = S.fence("tensor")

    S.waits("gpsimd", [ST8[NSUB - 1]["w1_done"], tok_m1])
    m2c = {}
    for j in range(4):
        m2c[j] = S.dma("gpsimd", m2_t[:, :, j * 1024:(j + 1) * 1024],
                       m2_dram[:, :, j * 1024:(j + 1) * 1024], group=f"m2{j}")

    st2 = {}
    w2_banks = [0, 1, 2]

    def b2_loads(t):
        st = st2.setdefault(t, {})
        kb = k2in_t[:, t % 2]
        prev2 = st2.get(t - 2, {})
        if t == 0:
            S.waits("sync", k2_final + [c2st_tok, tokB1_v, tokB1_s, tokB1_t])
        else:
            S.waits("sync", [prev2.get("mm_done"), st2[t - 1].get("c2bc_t")])
        st["k2c"] = {}
        for j in range(4):
            st["k2c"][j] = S.dma(
                "sync", kb[:, j * 8:(j + 1) * 8, 0:B2T],
                k2_spill[:, j * 8:(j + 1) * 8, t * B2T:(t + 1) * B2T], group=f"k2i{t % 2}{j}")
        olo = OUT_OFF + t * B2T
        st["c2"] = S.dma("sync", c2strip[0:1, 0:B2T], c2_dram[0:1, olo:olo + B2T], group="c2ld")

    def b2_head(t):
        st = st2[t]
        kb = k2in_t[:, t % 2]
        yb = y_t2[:, t % 2]
        olo = OUT_OFF + t * B2T
        if t == 0:
            st["x2"] = S.dma("sync", bx2_t[:], x2_spill[:, :, olo:olo + B2T], group="bx2")
        S.waits("tensor", [st["c2"], free["pb5"]])
        S.emit("tensor", lambda E: E.matmul(pb[5][:, 0:B2T], ones1f[0:1, :], c2strip[0:1, 0:B2T], start=True, stop=True))
        st["c2bc_t"] = S.fence("tensor")
        S.wait("vector", st["c2bc_t"])
        S.op("vector", lambda E: E.tensor_copy(c2sb_t[:], pb[5][:, 0:B2T]), writes=("c2sb",))
        tok_c2bc = S.fence("vector")
        free["pb5"] = tok_c2bc
        for d in range(DC):
            bkid = w2_banks[d % 3]
            pbk = pb[bkid]
            S.waits("tensor", [free[f"pb{bkid}"]])
            for kc in range(HC):
                S.waits("tensor", [st["k2c"][kc // 8]])
                if t == 0:
                    S.waits("tensor", [m2c[kc // 8]])
                S.emit("tensor", lambda E, d=d, kc=kc, kb=kb, pbk=pbk: E.matmul(
                    pbk[:, 0:B2T], m2_t[:, d, kc * 128:(kc + 1) * 128], kb[:, kc, 0:B2T],
                    start=(kc == 0), stop=(kc == HC - 1)))
            tk_y = S.fence("tensor")
            st["mm_done"] = tk_y
            S.waits("vector", [tk_y, tok_c2bc, st2.get(t - 2, {}).get("res_done")])
            S.op("vector", lambda E, d=d, pbk=pbk, yb=yb: E.tensor_tensor(yb[:, d, 0:B2T], pbk[:, 0:B2T], c2sb_t[:], ALU.mult),
                 reads=("c2sb",), writes=("y",))
            free[f"pb{bkid}"] = S.fence("vector")
        st["deq_done"] = free[f"pb{w2_banks[(DC - 1) % 3]}"]

    def b2_tail(t):
        st = st2[t]
        yb = y_t2[:, t % 2]
        bx = bx2_t
        S.waits("scalar", [st["deq_done"]])
        tok_st = emit_stats(lambda c, yb=yb: yb[:, c, 0:B2T], B2T, b2stat_t, "y")
        tok_b = emit_ln_strips(B2T, tok_st)
        S.waits("vector", [tok_b, st["x2"]])
        for c in range(DC):
            S.op("vector", lambda E, c=c, yb=yb: E.tensor_tensor(yb[:, c, 0:B2T], yb[:, c, 0:B2T], pb[3][:, 0:B2T], ALU.mult),
                 reads=("y",), writes=("y",))
            S.op("vector", lambda E, c=c, bx=bx: E.tensor_tensor(bx[:, c, 0:B2T], bx[:, c, 0:B2T], pb[4][:, 0:B2T], ALU.add),
                 reads=("bx2",), writes=("bx2",))
            S.op("vector", lambda E, c=c, yb=yb, bx=bx: E.tensor_tensor(bx[:, c, 0:B2T], yb[:, c, 0:B2T], bx[:, c, 0:B2T], ALU.add),
                 reads=("y", "bx2"), writes=("bx2",))
        st["res_done"] = S.fence("vector")
        free["pb3"] = st["res_done"]
        free["pb4"] = st["res_done"]
        S.waits("scalar", [st["res_done"]])
        st["store"] = S.dma("scalar", out_d[:, :, t * B2T:(t + 1) * B2T], bx[:, :, 0:B2T], group="outst")
        if t + 1 < NB2:
            nst = st2.setdefault(t + 1, {})
            S.waits("sync", [st["store"]])
            olo2 = OUT_OFF + (t + 1) * B2T
            nst["x2"] = S.dma("sync", bx2_t[:], x2_spill[:, :, olo2:olo2 + B2T], group="bx2")

    b2_loads(0)
    b2_head(0)
    b2_loads(1)
    for t in range(1, NB2):
        b2_head(t)
        if t + 1 < NB2:
            b2_loads(t + 1)
        b2_tail(t - 1)
    b2_tail(NB2 - 1)

    S.wait("gpsimd", ("d_scalar_outst", S.cnt["d_scalar_outst"]))

    with nc.Block() as block:
        @block.sync
        def _(E):
            for fn in S.q["sync"]:
                fn(E)

        @block.gpsimd
        def _(E):
            for fn in S.q["gpsimd"]:
                fn(E)

        @block.vector
        def _(E):
            for fn in S.q["vector"]:
                fn(E)

        @block.scalar
        def _(E):
            for fn in S.q["scalar"]:
                fn(E)

        @block.tensor
        def _(E):
            for fn in S.q["tensor"]:
                fn(E)

    stack.close()
    return nc



def _prep_inputs(x, gate_W, value_W, gamma, beta, w1, conv_w, conv_b, w2):
    gate_prep = np.ascontiguousarray(
        gate_W.reshape(NJ, DC, 128, DC, 128).transpose(1, 4, 0, 3, 2).reshape(DC, 128, NJ, D))
    value_prep = np.ascontiguousarray(
        value_W.reshape(DC, 128, DC, 128).transpose(3, 2, 0, 1).reshape(128, DC, D))
    w1_prep = np.ascontiguousarray(
        w1.reshape(HC, 128, DC, 128).transpose(0, 3, 2, 1).reshape(HC, 128, D))
    w2_prep = np.ascontiguousarray(
        w2.reshape(DC, 128, HC, 128).transpose(0, 3, 2, 1).reshape(DC, 128, H))
    conv_prep = np.ascontiguousarray(
        conv_w.reshape(HC, 128, 3).transpose(1, 0, 2).reshape(128, 3 * HC))
    convb_prep = np.ascontiguousarray(conv_b.reshape(HC, 128).T)
    gamma_prep = np.ascontiguousarray(gamma.reshape(DC, 128).T)
    beta_prep = np.ascontiguousarray(beta.reshape(DC, 128).T)
    ident = np.eye(128, dtype=np.float32)
    onesb = np.ones((128, 2), dtype=ml_dtypes.bfloat16)
    onesf = np.ones((128, 2), dtype=np.float32)

    shared = dict(gate_prep=gate_prep, value_prep=value_prep, w1_prep=w1_prep,
                  w2_prep=w2_prep, conv_prep=conv_prep, convb_prep=convb_prep,
                  gamma_prep=gamma_prep, beta_prep=beta_prep, ident=ident,
                  onesb=onesb, onesf=onesf)

    in_maps = []
    for core in range(8):
        b, half = core >> 1, core & 1
        g0 = half * (SEQ // 2)
        idx = (np.arange(g0 - HALO, g0 + CT + HALO)) % SEQ
        xs = np.ascontiguousarray(x[b][idx].T.reshape(DC, 128, WX).transpose(1, 0, 2))
        masks = np.ones((128, 2), dtype=np.float32)
        if half == 0:
            masks[:, 0] = 0.0
        else:
            masks[:, 1] = 0.0
        m = dict(shared)
        m["x_fm"] = xs
        m["masks"] = masks
        in_maps.append(m)
    return in_maps


def kernel(L0_attn_x, L0_attn_gate_W, L0_attn_value_W, L0_attn_gamma,
           L0_attn_beta, L0_ffn_w1, L0_ffn_conv_w, L0_ffn_conv_b, L0_ffn_w2,
           _debug=(), _core_ids=None):
    inp = [np.asarray(a, dtype=np.float32) for a in
           (L0_attn_x, L0_attn_gate_W, L0_attn_value_W, L0_attn_gamma,
            L0_attn_beta, L0_ffn_w1, L0_ffn_conv_w, L0_ffn_conv_b, L0_ffn_w2)]
    in_maps = _prep_inputs(*inp)
    nc = build_program(debug=_debug)
    core_ids = list(range(8)) if _core_ids is None else _core_ids
    res = run_bass_kernel_spmd(nc, in_maps[: len(core_ids)], core_ids=core_ids)
    out = np.empty((B, SEQ, D), dtype=np.float32)
    for core in range(len(core_ids)):
        b, half = core >> 1, core & 1
        g0 = half * (SEQ // 2)
        o = res.results[core]["out_fm"]
        out[b, g0:g0 + CT, :] = np.asarray(o).transpose(2, 1, 0).reshape(CT, D)
    if _debug:
        return out, res
    return out


# version 09
# speedup vs baseline: 1.0190x; 1.0190x over previous
import contextlib
import numpy as np
import ml_dtypes
import concourse.bass as bass
import concourse.mybir as mybir
from concourse.bass_utils import run_bass_kernel_spmd

F32 = mybir.dt.float32
BF16 = mybir.dt.bfloat16
ALU = mybir.AluOpType
AF = mybir.ActivationFunctionType
AX = mybir.AxisListType

B, SEQ, D = 4, 8192, 1024
DC = D // 128
H = 4096
HC = H // 128
NJ = 9
SHIFTS = [0, -32, -16, -4, -1, 1, 4, 16, 32]
LN_EPS = 1e-5

CT = 4096
HALO = 36
WX = CT + 2 * HALO
WOUT = 4104
SUB = 456
NSUB = 9
ST_SUBS = [(0, 2), (2, 4), (4, 6), (6, 8), (8, 9)]
MAXW = 2 * SUB
B2T = 512
NB2 = 8
OUT_OFF = 4
X_OFF = HALO - OUT_OFF
HR = 458

M1C = float(1.5 * 2.0 ** 23)
M2C = float(1.5 * 2.0 ** 52)
WEPS = float(2.0 ** -40)
WINV = float(2.0 ** 40)

ENGS = ("sync", "gpsimd", "vector", "scalar", "tensor")


class Sched:
    def __init__(self, nc, stack):
        self.nc = nc
        self.q = {e: [] for e in ENGS}
        self.sem = {}
        self.cnt = {}
        self.waited = {e: {} for e in ENGS}
        self.dirty_w = {e: set() for e in ENGS}
        self.dirty_r = {e: set() for e in ENGS}
        self.last_tinst = None
        self._stack = stack
        for name in ("f_gpsimd", "f_vector", "f_scalar", "f_tensor"):
            self.sem[name] = stack.enter_context(nc.semaphore(name))
            self.cnt[name] = 0

    def emit(self, e, fn):
        self.q[e].append(fn)
        if e == "tensor":
            self.last_tinst = len(self.q[e]) - 1

    def wait(self, e, tok):
        if tok is None:
            return
        name, val = tok
        if val <= 0:
            return
        if self.waited[e].get(name, 0) >= val:
            return
        self.waited[e][name] = val
        sem = self.sem[name]
        self.q[e].append(lambda E, s=sem, v=val: E.wait_ge(s, v))

    def waits(self, e, toks):
        for t in toks:
            self.wait(e, t)

    def _hazard(self, e, reads, writes):
        dw, dr = self.dirty_w[e], self.dirty_r[e]
        if any(r in dw for r in reads) or any(w in dw or w in dr for w in writes):
            self.q[e].append(lambda E: E.drain())
            dw.clear()
            dr.clear()
        dr.update(reads)
        dw.update(writes)

    def op(self, e, fn, reads=(), writes=()):
        self._hazard(e, reads, writes)
        self.q[e].append(fn)

    def fence(self, e):
        name = "f_" + e
        self.cnt[name] += 1
        val = self.cnt[name]
        sem = self.sem[name]
        if e == "tensor":
            if self.last_tinst is None:
                self.cnt[name] -= 1
                return (name, self.cnt[name])
            prev = self.q[e][self.last_tinst]
            self.q[e][self.last_tinst] = (
                lambda E, f=prev, s=sem: f(E).then_inc(s, 1))
            self.last_tinst = None
        else:
            self.q[e].append(lambda E, s=sem: E.drain().then_inc(s, 1))
        self.dirty_w[e].clear()
        self.dirty_r[e].clear()
        return (name, val)

    def dma(self, e, out, in_, group="g", **kw):
        name = f"d_{e}_{group}"
        if name not in self.sem:
            self.sem[name] = self._stack.enter_context(self.nc.semaphore(name))
            self.cnt[name] = 0
        self.cnt[name] += 16
        val = self.cnt[name]
        sem = self.sem[name]
        self.q[e].append(
            lambda E, s=sem, o=out, i=in_, k=kw: E.dma_start(out=o, in_=i, **k).then_inc(s, 16))
        return (name, val)


def build_program(debug=()):
    nc = bass.Bass("TRN2", target_bir_lowering=False)
    stack = contextlib.ExitStack()

    x_in = nc.dram_tensor("x_fm", [128, DC, WX], F32, kind="ExternalInput").ap()
    gate_w = nc.dram_tensor("gate_prep", [DC, 128, NJ, D], F32, kind="ExternalInput").ap()
    value_w = nc.dram_tensor("value_prep", [128, DC, D], F32, kind="ExternalInput").ap()
    w1_w = nc.dram_tensor("w1_prep", [HC, 128, D], F32, kind="ExternalInput").ap()
    w2_w = nc.dram_tensor("w2_prep", [DC, 128, H], F32, kind="ExternalInput").ap()
    convw = nc.dram_tensor("conv_prep", [128, 3 * HC], F32, kind="ExternalInput").ap()
    convb = nc.dram_tensor("convb_prep", [128, HC], F32, kind="ExternalInput").ap()
    gamma_d = nc.dram_tensor("gamma_prep", [128, DC], F32, kind="ExternalInput").ap()
    beta_d = nc.dram_tensor("beta_prep", [128, DC], F32, kind="ExternalInput").ap()
    ident_d = nc.dram_tensor("ident", [128, 128], F32, kind="ExternalInput").ap()
    onesb_d = nc.dram_tensor("onesb", [128, 2], BF16, kind="ExternalInput").ap()
    onesf_d = nc.dram_tensor("onesf", [128, 2], F32, kind="ExternalInput").ap()
    masks_d = nc.dram_tensor("masks", [128, 2], F32, kind="ExternalInput").ap()

    out_d = nc.dram_tensor("out_fm", [128, DC, CT], F32, kind="ExternalOutput").ap()

    x2_spill = nc.dram_tensor("x2_spill", [128, DC, WOUT], F32, kind="Internal").ap()
    k2_spill = nc.dram_tensor("k2_spill", [128, HC, CT], BF16, kind="Internal").ap()
    c2_dram = nc.dram_tensor("c2_dram", [1, WOUT + 64], F32, kind="Internal").ap()
    m1_dram = nc.dram_tensor("m1_dram", [128, HC, D], BF16, kind="Internal").ap()
    m2_dram = nc.dram_tensor("m2_dram", [128, DC, H], BF16, kind="Internal").ap()

    dbg = {}
    if "x2" in debug:
        dbg["x2"] = nc.dram_tensor("dbg_x2", [128, DC, WOUT], F32, kind="ExternalOutput").ap()
    if "k2" in debug:
        dbg["k2"] = nc.dram_tensor("dbg_k2", [128, HC, CT], BF16, kind="ExternalOutput").ap()
    if "ws" in debug:
        dbg["ws"] = nc.dram_tensor("dbg_ws", [128, 8], F32, kind="ExternalOutput").ap()
    if "x1" in debug:
        dbg["x1"] = nc.dram_tensor("dbg_x1", [128, DC, WX], BF16, kind="ExternalOutput").ap()

    def alloc(name, shape, dtype, off):
        n = int(np.prod(shape[1:])) * mybir.dt.size(dtype)
        t = nc.alloc_sbuf_tensor_at(name, shape, dtype, offset=off)
        return t, off + ((n + 31) & ~31)

    pers = 17408
    ident, pers = alloc("ident_s", [128, 128], F32, pers)
    onesb, pers = alloc("onesb_s", [128, 2], BF16, pers)
    onesf, pers = alloc("onesf_s", [128, 2], F32, pers)
    ones1f, pers = alloc("ones1f", [1, 128], F32, pers)
    alpha, pers = alloc("alpha_s", [128, 1], F32, pers)
    a_tmp, pers = alloc("a_tmp", [128, 2], F32, pers)
    gamma_s, pers = alloc("gamma_s", [128, DC], F32, pers)
    beta_s, pers = alloc("beta_s", [128, DC], F32, pers)
    convw_s, pers = alloc("convw_s", [128, 3 * HC], F32, pers)
    convb_s, pers = alloc("convb_s", [128, HC], F32, pers)
    masks_s, pers = alloc("masks_s", [128, 2], F32, pers)
    wsvec, pers = alloc("wsvec", [128, 4], F32, pers)
    wpart, pers = alloc("wpart", [128, 132], F32, pers)
    strip1, pers = alloc("strip1", [1, 928], F32, pers)
    strip2, pers = alloc("strip2", [1, 928], F32, pers)
    strip3, pers = alloc("strip3", [1, 928], F32, pers)
    tmst, pers = alloc("tmst", [128, 16], F32, pers)
    tmst2, pers = alloc("tmst2", [128, 16], F32, pers)
    tmst3, pers = alloc("tmst3", [128, 16], F32, pers)
    sb4, pers = alloc("sb4", [1, 512], F32, pers)
    sb4b, pers = alloc("sb4b", [1, 512], F32, pers)
    epsv, pers = alloc("epsv", [1, 2], F32, pers)
    PERS_END = pers

    A0 = (PERS_END + 63) & ~63

    o = A0
    wstream, o = alloc("wstream", [128, 6, 512], F32, o)
    mq_b, o = alloc("mq_b", [128, 512], F32, o)
    mq_d, o = alloc("mq_d", [128, 512], F32, o)
    mq_u, o = alloc("mq_u", [128, 512], F32, o)
    mq_m, o = alloc("mq_m", [128, 2, 512], BF16, o)
    WPH_END = o

    o = WPH_END
    x1_t, o = alloc("x1_t", [128, DC, WX], BF16, o)
    acc_t, o = alloc("acc_t", [128, DC, MAXW], F32, o)
    gate_t, o = alloc("gate_t", [128, 2, 3, D], BF16, o)
    val_t, o = alloc("val_t", [128, DC, D], BF16, o)
    v_t, o = alloc("v_t", [128, 2, MAXW + 64], BF16, o)
    g_t, o = alloc("g_t", [128, 3, MAXW], BF16, o)
    prod_t, o = alloc("prod_t", [128, MAXW], BF16, o)
    xin_t, o = alloc("xin_t", [128, DC, SUB], F32, o)
    stat_t, o = alloc("stat_t", [128, 2, 2 * SUB], BF16, o)
    out2_t, o = alloc("out2_t", [128, DC, SUB], F32, o)
    zt_t, o = alloc("zt_t", [128, 2, SUB], F32, o)
    A_END = o
    assert A_END < 223 * 1024, A_END

    o = A0
    m1_t, o = alloc("m1_t", [128, HC, D], BF16, o)
    k3_t, o = alloc("k3_t", [128, 2, DC, HR], BF16, o)
    k2_t, o = alloc("k2_t", [128, 2, SUB], BF16, o)
    cv_t, o = alloc("cv_t", [128, 6, HR], F32, o)
    bstat_t, o = alloc("bstat_t", [128, 2, 2 * HR], BF16, o)
    m8_t, o = alloc("m8_t", [128, 2, 512], F32, o)
    qb_t, o = alloc("qb_t", [128, 2, HR], F32, o)
    qd_t, o = alloc("qd_t", [128, 2, HR], F32, o)
    hq_t, o = alloc("hq_t", [128, 2, HR], F32, o)
    hx2_t, o = alloc("hx2_t", [128, DC, HR], F32, o)
    u_t, o = alloc("u_t", [128, DC, HR], F32, o)
    hs_t, o = alloc("hs_t", [128, HC, HR], BF16, o)
    sb4t, o = alloc("sb4t", [1, 512], F32, o)
    sb4bt, o = alloc("sb4bt", [1, 512], F32, o)
    tmstt, o = alloc("tmstt", [128, 16], F32, o)
    tmstt2, o = alloc("tmstt2", [128, 16], F32, o)
    tmstt3, o = alloc("tmstt3", [128, 16], F32, o)
    B1_END = o
    assert B1_END < 223 * 1024, B1_END

    o = A0
    m2_t, o = alloc("m2_t", [128, DC, H], BF16, o)
    k2in_t, o = alloc("k2in_t", [128, 2, HC, B2T], BF16, o)
    bx2_t, o = alloc("bx2_t", [128, DC, B2T], F32, o)
    y_t2, o = alloc("y_t", [128, 2, DC, B2T], F32, o)
    B2_END = o
    assert B2_END < 229376, B2_END
    o2 = max(B1_END, B2_END)
    b2stat_t, o2 = alloc("b2stat_t", [128, 2, 2 * B2T], BF16, o2)
    c2sb_t, o2 = alloc("c2sb_t", [128, B2T], F32, o2)
    c2strip, o2 = alloc("c2strip", [1, B2T], F32, o2)
    assert o2 < 229376, o2

    pb = [nc.alloc_psum_tensor(f"pb{i}", [128, 512], F32) for i in range(6)]
    ps_s1 = nc.alloc_psum_tensor("ps_s1", [2, 512], F32)
    ps_s2 = nc.alloc_psum_tensor("ps_s2", [2, 512], F32)

    S = Sched(nc, stack)

    free = {"stats": None, "pb0": None, "pb1": None, "pb2": None, "pb3": None,
            "pb4": None, "pb5": None}

    for ap_o, ap_i in [(ident[:], ident_d), (onesb[:], onesb_d), (onesf[:], onesf_d),
                       (gamma_s[:], gamma_d), (beta_s[:], beta_d), (convw_s[:], convw),
                       (convb_s[:], convb), (masks_s[:], masks_d)]:
        t_const = S.dma("sync", ap_o, ap_i, group="const")
    S.wait("vector", t_const)
    S.wait("tensor", t_const)
    S.wait("scalar", t_const)
    S.wait("gpsimd", t_const)
    S.op("vector", lambda E: E.memset(ones1f[:], 1.0), writes=("ones1f",))
    S.op("vector", lambda E: E.memset(strip1[:], 0.0), writes=("strip1",))
    S.op("vector", lambda E: E.memset(strip2[:], 0.0), writes=("strip2",))
    S.op("vector", lambda E: E.memset(strip3[:], 0.0), writes=("strip3",))
    S.op("vector", lambda E: E.memset(tmst[:], 0.0), writes=("tmst",))
    S.op("vector", lambda E: E.memset(tmst2[:], 0.0), writes=("tmst2",))
    S.op("vector", lambda E: E.memset(tmst3[:], 0.0), writes=("tmst3",))
    S.op("vector", lambda E: E.memset(sb4[:], 0.0), writes=("sb4",))
    S.op("vector", lambda E: E.memset(sb4b[:], 0.0), writes=("sb4b",))
    S.op("vector", lambda E: E.memset(epsv[:], LN_EPS), writes=("epsv",))
    S.op("vector", lambda E: E.memset(a_tmp[:, 0:1], 1.0), writes=("a_tmp",))
    S.op("vector", lambda E: E.tensor_scalar(a_tmp[:, 1:2], a_tmp[:, 0:1], WEPS, -1.0, ALU.add, ALU.add),
         reads=("a_tmp",), writes=("a_tmp",))
    S.op("vector", lambda E: E.tensor_scalar(alpha[:], a_tmp[:, 1:2], -WINV, 1.0, ALU.mult, ALU.add),
         reads=("a_tmp",), writes=("alpha",))
    tok_alpha = S.fence("vector")

    def emit_stats(src_fn, n, sbuf, tag):
        S.waits("tensor", [free["stats"]])
        chunk_toks = [None, None]
        for c in range(DC):
            sb = sbuf[:, c % 2]
            S.wait("scalar", chunk_toks[c % 2])
            S.op("scalar", lambda E, c=c, sb=sb: E.copy(sb[:, 0:n], src_fn(c)),
                 reads=(tag,), writes=("statbuf",))
            S.op("scalar", lambda E, c=c, sb=sb: E.activation(sb[:, n:2 * n], src_fn(c), AF.Square),
                 reads=(tag,), writes=("statbuf",))
            tk = S.fence("scalar")
            S.wait("tensor", tk)
            S.emit("tensor", lambda E, c=c, sb=sb: E.matmul(
                ps_s1[0:1, 0:n], onesb[:, 0:1], sb[:, 0:n], start=(c == 0), stop=(c == DC - 1)))
            S.emit("tensor", lambda E, c=c, sb=sb: E.matmul(
                ps_s2[0:1, 0:n], onesb[:, 0:1], sb[:, n:2 * n], start=(c == 0), stop=(c == DC - 1)))
            chunk_toks[c % 2] = S.fence("tensor")
        return chunk_toks[(DC - 1) % 2]

    def emit_ln_strips(n, tok_stats, b_r=3, b_mu=4):
        S.wait("vector", tok_stats)
        S.op("vector", lambda E: E.tensor_scalar(strip1[0:1, 0:n], ps_s1[0:1, 0:n], 1.0 / D, None, ALU.mult),
             writes=("strip1",))
        S.op("vector", lambda E: E.tensor_scalar(strip2[0:1, 0:n], ps_s2[0:1, 0:n], 1.0 / D, None, ALU.mult),
             writes=("strip2",))
        free["stats"] = S.fence("vector")
        S.op("vector", lambda E: E.tensor_tensor(strip3[0:1, 0:n], strip1[0:1, 0:n], strip1[0:1, 0:n], ALU.mult),
             reads=("strip1",), writes=("strip3",))
        S.op("vector", lambda E: E.scalar_tensor_tensor(strip2[0:1, 0:n], strip3[0:1, 0:n], -1.0, strip2[0:1, 0:n], ALU.mult, ALU.add),
             reads=("strip2", "strip3"), writes=("strip2",))
        tk = S.fence("vector")
        S.wait("scalar", tk)
        S.op("scalar", lambda E: E.activation(strip3[0:1, 0:n], strip2[0:1, 0:n], AF.Sqrt, bias=epsv[0:1, 0:1]),
             reads=("strip2",), writes=("strip3",))
        tk = S.fence("scalar")
        S.wait("vector", tk)
        S.op("vector", lambda E: E.reciprocal(strip2[0:1, 0:n], strip3[0:1, 0:n]),
             reads=("strip3",), writes=("strip2",))
        S.op("vector", lambda E: E.tensor_tensor(strip3[0:1, 0:n], strip1[0:1, 0:n], strip2[0:1, 0:n], ALU.mult),
             reads=("strip1", "strip2"), writes=("strip3",))
        S.op("vector", lambda E: E.tensor_scalar(strip3[0:1, 0:n], strip3[0:1, 0:n], -1.0, None, ALU.mult),
             reads=("strip3",), writes=("strip3",))
        tk = S.fence("vector")
        S.waits("tensor", [tk, free[f"pb{b_r}"], free[f"pb{b_mu}"]])
        S.emit("tensor", lambda E, b_r=b_r: E.matmul(pb[b_r][:, 0:n], ones1f[0:1, :], strip2[0:1, 0:n], start=True, stop=True))
        S.emit("tensor", lambda E, b_mu=b_mu: E.matmul(pb[b_mu][:, 0:n], ones1f[0:1, :], strip3[0:1, 0:n], start=True, stop=True))
        return S.fence("tensor")

    tok_val = S.dma("gpsimd", val_t[:], value_w, group="val")

    xf = [None]
    x1s_tok = {}
    p1_open = [False]
    p1_st = {}

    def p1_L(s):
        c0 = s * SUB
        n = SUB if s < 9 else WX - 9 * SUB
        S.waits("sync", [xf[0]])
        t_l = S.dma("sync", xin_t[:, :, 0:n], x_in[:, :, c0:c0 + n], group="xin")
        p1_st[s] = [c0, n, t_l]

    def p1_S(s):
        c0, n, t_l = p1_st[s]
        p1_open[0] = True
        S.waits("scalar", [t_l])
        tok_st = emit_stats(lambda c, n=n: xin_t[:, c, 0:n], n, stat_t, "xin")
        p1_st[s] = (c0, n, tok_st)

    def p1_A(s):
        p1_L(s)
        p1_S(s)

    def p1_BC(s):
        c0, n, tok_st = p1_st[s]
        tok_b = emit_ln_strips(n, tok_st, 2, 5)
        S.wait("vector", tok_b)
        for c in range(DC):
            S.op("vector", lambda E, c=c, n=n: E.tensor_tensor(
                zt_t[:, 0, 0:n], xin_t[:, c, 0:n], pb[2][:, 0:n], ALU.mult),
                reads=("xin",), writes=("zt0",))
            S.op("vector", lambda E, c=c, n=n, c0=c0: E.tensor_tensor(
                x1_t[:, c, c0:c0 + n], zt_t[:, 0, 0:n], pb[5][:, 0:n], ALU.add),
                reads=("zt0",), writes=("x1",))
        xf[0] = S.fence("vector")
        free["pb2"] = xf[0]
        free["pb5"] = xf[0]
        x1s_tok[s] = xf[0]
        p1_open[0] = False

    def w_chunk_ap(which, i):
        if which == 0:
            return w1_w[i // 2, :, 512 * (i % 2): 512 * (i % 2 + 1)]
        d, ss = divmod(i, 8)
        return w2_w[d, :, 512 * ss: 512 * (ss + 1)]

    def m_chunk_ap(which, i):
        if which == 0:
            return m1_dram[:, i // 2, 512 * (i % 2): 512 * (i % 2 + 1)]
        d, ss = divmod(i, 8)
        return m2_dram[:, d, 512 * ss: 512 * (ss + 1)]

    wa_ld = {}
    red_tok = {}

    def wa_load(gi):
        if gi >= 128:
            return
        S.waits("scalar", [red_tok.get(gi - 6)])
        which, i = divmod(gi, 64)
        wa_ld[gi] = S.dma("scalar", wstream[:, gi % 6, :], w_chunk_ap(which, i), group=f"wl{gi % 6}")

    def wa_item(gi):
        wa_load(gi + 4)
        S.wait("vector", wa_ld[gi])
        S.op("vector", lambda E, gi=gi: E.reduce_sum(
            wpart[:, gi:gi + 1], wstream[:, gi % 6, :], axis=AX.X, apply_absolute_value=True),
            reads=("wstream",), writes=("wpart",))
        red_tok[gi] = S.fence("vector")

    tok_ws_c = [None]

    def emit_wfin():
        S.op("vector", lambda E: E.reduce_sum(wpart[:, 128:129], wpart[:, 0:64], axis=AX.X),
             reads=("wpart",), writes=("wpart",))
        S.op("vector", lambda E: E.reduce_sum(wpart[:, 129:130], wpart[:, 64:128], axis=AX.X),
             reads=("wpart",), writes=("wpart",))
        tk = S.fence("vector")
        S.waits("tensor", [tk, free["stats"]])
        S.emit("tensor", lambda E: E.matmul(ps_s1[0:1, 0:1], wpart[:, 128:129], onesf[:, 0:1], start=True, stop=True))
        tk = S.fence("tensor")
        S.wait("vector", tk)
        S.op("vector", lambda E: E.tensor_copy(strip1[0:1, 0:1], ps_s1[0:1, 0:1]), writes=("strip1",))
        tk = S.fence("vector")
        S.wait("tensor", tk)
        S.emit("tensor", lambda E: E.matmul(ps_s1[0:1, 0:1], wpart[:, 129:130], onesf[:, 0:1], start=True, stop=True))
        tk = S.fence("tensor")
        S.wait("vector", tk)
        S.op("vector", lambda E: E.tensor_copy(strip1[0:1, 1:2], ps_s1[0:1, 0:1]), writes=("strip1",))
        free["stats"] = S.fence("vector")
        S.op("vector", lambda E: E.tensor_scalar(strip1[0:1, 2:4], strip1[0:1, 0:2], 1.0 / (H * D), 1e-5, ALU.mult, ALU.max),
             reads=("strip1",), writes=("strip1",))
        S.op("vector", lambda E: E.reciprocal(strip1[0:1, 4:6], strip1[0:1, 2:4]),
             reads=("strip1",), writes=("strip1",))
        S.op("vector", lambda E: E.tensor_scalar(strip1[0:1, 6:8], strip1[0:1, 2:4], 1.0 / 127.0, None, ALU.mult),
             reads=("strip1",), writes=("strip1",))
        tk = S.fence("vector")
        S.waits("tensor", [tk, free["pb5"]])
        S.emit("tensor", lambda E: E.matmul(pb[5][:, 0:4], ones1f[0:1, :], strip1[0:1, 4:8], start=True, stop=True))
        tk = S.fence("tensor")
        S.wait("vector", tk)
        S.op("vector", lambda E: E.tensor_copy(wsvec[:, 0:4], pb[5][:, 0:4]), writes=("wsvec",))
        tok_ws_c[0] = S.fence("vector")
        free["pb5"] = tok_ws_c[0]
        if "ws" in dbg:
            S.wait("scalar", tok_ws_c[0])
            S.dma("scalar", dbg["ws"][:, 0:4], wsvec[:, 0:4], group="dbg")

    q_tok = {}
    st_toks = {}
    wq_ld = {}

    def wq_load(gi):
        if gi >= 128:
            return
        S.waits("scalar", [red_tok[127], q_tok.get(gi - 6)])
        which, i = divmod(gi, 64)
        wq_ld[gi] = S.dma("scalar", wstream[:, gi % 6, :], w_chunk_ap(which, i), group=f"wq{gi % 6}")

    def wq_item(gi):
        wq_load(gi + 4)
        which, i = divmod(gi, 64)
        wscol = wsvec[:, which:which + 1]
        buf = wstream[:, gi % 6, :]
        mb = mq_m[:, gi % 2, :]
        S.waits("vector", [wq_ld[gi], tok_alpha, tok_ws_c[0], st_toks.get(gi - 2)])
        S.op("vector", lambda E, b=buf, w=wscol: E.tensor_scalar(mq_u[:], b, w, None, ALU.mult),
             reads=("wstream", "wsvec"), writes=("mq_u",))
        S.op("vector", lambda E: E.tensor_scalar(mq_b[:], mq_u[:], M1C, -M1C, ALU.add, ALU.add),
             reads=("mq_u",), writes=("mq_b",))
        S.op("vector", lambda E: E.tensor_scalar(mq_d[:], mq_u[:], M2C, -M2C, ALU.add, ALU.add),
             reads=("mq_u",), writes=("mq_d",))
        S.op("vector", lambda E: E.scalar_tensor_tensor(mq_u[:], mq_b[:], alpha[:], mq_d[:], ALU.mult, ALU.add),
             reads=("mq_b", "mq_d", "alpha"), writes=("mq_u",))
        S.op("vector", lambda E, m=mb: E.tensor_scalar(m, mq_u[:], 1.0, -1.0, ALU.min, ALU.max),
             reads=("mq_u",), writes=("mq_m",))
        q_tok[gi] = S.fence("vector")
        S.wait("scalar", q_tok[gi])
        st_toks[gi] = S.dma("scalar", m_chunk_ap(which, i), mb, group=f"mst{gi % 2}")

    work_q = []
    for gi in range(56, 128):
        work_q.append(("wa", gi, 1))
    work_q.append(("fin", 0, 4))
    for gi in range(128):
        work_q.append(("wq", gi, 4))

    def run_item(it):
        kind, gi, _ = it
        if kind == "wa":
            wa_item(gi)
        elif kind == "fin":
            emit_wfin()
            for g in range(4):
                wq_load(g)
        else:
            wq_item(gi)

    def sa_hook(budget):
        while budget > 0 and work_q:
            if work_q[0][0] == "fin" and p1_open[0]:
                break
            it = work_q.pop(0)
            run_item(it)
            budget -= it[2]

    for g in range(4):
        wa_load(g)
    p1_A(0)
    for gi in range(0, 8):
        wa_item(gi)
    p1_BC(0)
    p1_A(1)
    for gi in range(8, 24):
        wa_item(gi)
    p1_BC(1)
    p1_A(2)
    for gi in range(24, 40):
        wa_item(gi)
    p1_BC(2)
    for gi in range(40, 56):
        wa_item(gi)

    p1_sched = {}
    for i in range(4):
        subs = [ss for ss in (2 * i + 3, 2 * i + 4) if ss <= 9]
        if subs:
            p1_sched[(i, 0)] = ("L", subs[0])
            p1_sched[(i, 1)] = ("S", subs[0])
            p1_sched[(i, 3)] = ("BC", subs[0])
            if len(subs) > 1:
                p1_sched[(i, 4)] = ("L", subs[1])
                p1_sched[(i, 5)] = ("S", subs[1])
                p1_sched[(i, 7)] = ("BC", subs[1])

    x2_sub_store = {}
    acc_free_scalar = None
    prev_ST_out2_v = None
    v_half_free = [None, None]
    n_dc_total = len(ST_SUBS) * DC
    gthird_seq = []
    for sti in range(len(ST_SUBS)):
        for dc in range(DC):
            for third in range(3):
                gthird_seq.append((sti, dc, third))
    g_loaded = {}
    g_slot_free = {}
    g_next = [0]

    def prefetch_gates(upto):
        while g_next[0] < min(upto, len(gthird_seq)):
            gi = g_next[0]
            sti_, dc_, third_ = gthird_seq[gi]
            slot = gi % 2
            S.waits("gpsimd", [g_slot_free.get(gi - 2)])
            g_loaded[gi] = S.dma("gpsimd", gate_t[:, slot, :, :],
                                 gate_w[dc_, :, 3 * third_: 3 * third_ + 3, :],
                                 group=f"gate{slot}")
            g_next[0] += 1

    gate_banks = [0, 1, 3, 4]
    g_free = {0: None, 1: None, 2: None}

    for sti, (s_lo, s_hi) in enumerate(ST_SUBS):
        W = (s_hi - s_lo) * SUB
        o_lo = s_lo * SUB
        xv_lo = o_lo + X_OFF - 32
        WV = W + 64
        bank_i = [0]
        for dc in range(DC):
            half = dc % 2
            vb = v_t[:, half, :]
            S.waits("tensor", [tok_val, x1s_tok[s_hi]])
            S.waits("scalar", [v_half_free[half]])
            nvs = (WV + 511) // 512
            vban = [2, 5]
            for sv in range(nvs):
                a = sv * 512
                nn = min(512, WV - a)
                bk = vban[sv % 2]
                S.waits("tensor", [free[f"pb{bk}"]])
                for kc in range(DC):
                    S.emit("tensor", lambda E, dc=dc, kc=kc, a=a, nn=nn, xv_lo=xv_lo, bk=bk: E.matmul(
                        pb[bk][:, 0:nn], val_t[:, kc, dc * 128:(dc + 1) * 128],
                        x1_t[:, kc, xv_lo + a: xv_lo + a + nn],
                        start=(kc == 0), stop=(kc == DC - 1)))
                tk = S.fence("tensor")
                S.waits("scalar", [tk])
                S.op("scalar", lambda E, a=a, nn=nn, vb=vb, bk=bk: E.copy(vb[:, a:a + nn], pb[bk][:, 0:nn]),
                     writes=("v",))
                free[f"pb{bk}"] = S.fence("scalar")
            tok_v = free[f"pb{vban[(nvs - 1) % 2]}"]
            g_base = (sti * DC + dc) * 3
            prefetch_gates(g_base + 2)
            mac_tok = {}
            for j in range(NJ):
                gthird = g_base + j // 3
                slot = gthird % 2
                jj = j % 3
                gb = g_t[:, j % 3, :]
                S.waits("tensor", [g_loaded[gthird]])
                nsub = s_hi - s_lo
                for sv in range(nsub):
                    a = sv * SUB
                    gsub = j * nsub + sv
                    bk = gate_banks[bank_i[0] % 4]
                    bank_i[0] += 1
                    S.waits("tensor", [free[f"pb{bk}"]])
                    for kc in range(DC):
                        S.emit("tensor", lambda E, slot=slot, jj=jj, kc=kc, a=a, bk=bk, o_lo=o_lo: E.matmul(
                            pb[bk][:, 0:SUB], gate_t[:, slot, jj, kc * 128:(kc + 1) * 128],
                            x1_t[:, kc, o_lo + X_OFF + a: o_lo + X_OFF + a + SUB],
                            start=(kc == 0), stop=(kc == DC - 1)))
                    tk = S.fence("tensor")
                    last_mm_tk = tk
                    S.waits("scalar", [tk, g_free[j % 3]])
                    S.op("scalar", lambda E, a=a, gb=gb, bk=bk: E.activation(gb[:, a:a + SUB], pb[bk][:, 0:SUB], AF.Sigmoid),
                         writes=("g",))
                    free[f"pb{bk}"] = S.fence("scalar")
                tok_g = free[f"pb{gate_banks[(bank_i[0] - 1) % 4]}"]
                sh = SHIFTS[j]
                S.waits("vector", [tok_g, tok_v, acc_free_scalar])
                S.op("vector", lambda E, gb=gb, vb=vb, sh=sh, W=W: E.tensor_tensor(
                    prod_t[:, 0:W], gb[:, 0:W], vb[:, 32 + sh: 32 + sh + W], ALU.mult),
                    reads=("g", "v"), writes=("prod",))
                if j == 0:
                    S.op("vector", lambda E, dc=dc, W=W: E.tensor_copy(acc_t[:, dc, 0:W], prod_t[:, 0:W]),
                         reads=("prod",), writes=("acc",))
                else:
                    S.op("vector", lambda E, dc=dc, W=W: E.tensor_tensor(
                        acc_t[:, dc, 0:W], acc_t[:, dc, 0:W], prod_t[:, 0:W], ALU.add),
                        reads=("prod", "acc"), writes=("acc",))
                mac_tok[j] = S.fence("vector")
                g_free[j % 3] = mac_tok[j]
                if j % 3 == 2:
                    g_slot_free[gthird] = last_mm_tk
                    prefetch_gates(gthird + 3)
            v_half_free[half] = mac_tok[8]
            piece = p1_sched.get((sti, dc))
            if piece is not None:
                fn = {"L": p1_L, "S": p1_S, "BC": p1_BC}[piece[0]]
                fn(piece[1])
                sa_hook(14 if piece[0] == "L" else 6)
            else:
                sa_hook(16)
        tok_acc_done = S.fence("vector")

        for ssub in range(s_lo, s_hi):
            a = (ssub - s_lo) * SUB
            n = SUB
            S.waits("sync", [xf[0]])
            t_xr = S.dma("sync", xin_t[:, :, 0:n], x_in[:, :, ssub * SUB + X_OFF: ssub * SUB + X_OFF + n], group="xin")
            S.waits("scalar", [tok_acc_done])
            tok_st = emit_stats(lambda c, a=a, n=n: acc_t[:, c, a:a + n], n, stat_t, "acc")
            tok_b = emit_ln_strips(n, tok_st)
            S.waits("vector", [tok_b, prev_ST_out2_v, x2_sub_store.get(ssub - 1), x2_sub_store.get((ssub - 1, "dbg"))])
            for c in range(DC):
                S.op("vector", lambda E, c=c, a=a, n=n: E.tensor_tensor(
                    zt_t[:, 0, 0:n], acc_t[:, c, a:a + n], pb[3][:, 0:n], ALU.mult),
                    reads=("acc",), writes=("zt0",))
                S.op("vector", lambda E, n=n: E.tensor_tensor(
                    zt_t[:, 1, 0:n], zt_t[:, 0, 0:n], pb[4][:, 0:n], ALU.add),
                    reads=("zt0",), writes=("zt1",))
                S.op("vector", lambda E, c=c, n=n: E.tensor_scalar(
                    out2_t[:, c, 0:n], zt_t[:, 1, 0:n], gamma_s[:, c:c + 1], beta_s[:, c:c + 1], ALU.mult, ALU.add),
                    reads=("zt1",), writes=("out2",))
            tok_out2 = S.fence("vector")
            free["pb3"] = tok_out2
            free["pb4"] = tok_out2
            S.waits("scalar", [tok_out2])
            tok_st = emit_stats(lambda c, n=n: out2_t[:, c, 0:n], n, stat_t, "out2")
            tok_b = emit_ln_strips(n, tok_st)
            S.waits("vector", [tok_b, t_xr])
            for c in range(DC):
                S.op("vector", lambda E, c=c, n=n: E.tensor_tensor(
                    zt_t[:, 0, 0:n], out2_t[:, c, 0:n], pb[3][:, 0:n], ALU.mult),
                    reads=("out2",), writes=("zt0",))
                S.op("vector", lambda E, c=c, n=n: E.tensor_tensor(
                    zt_t[:, 1, 0:n], xin_t[:, c, 0:n], pb[4][:, 0:n], ALU.add),
                    reads=("xin",), writes=("zt1",))
                S.op("vector", lambda E, c=c, n=n: E.tensor_tensor(
                    out2_t[:, c, 0:n], zt_t[:, 0, 0:n], zt_t[:, 1, 0:n], ALU.add),
                    reads=("zt0", "zt1"), writes=("out2",))
            tok_x2 = S.fence("vector")
            free["pb3"] = tok_x2
            free["pb4"] = tok_x2
            xf[0] = tok_x2
            S.waits("scalar", [tok_x2])
            tsp = S.dma("scalar", x2_spill[:, :, ssub * SUB:(ssub + 1) * SUB], out2_t[:, :, 0:n], group="x2sp")
            x2_sub_store[ssub] = tsp
            if "x2" in dbg:
                x2_sub_store[(ssub, "dbg")] = S.dma("scalar", dbg["x2"][:, :, ssub * SUB:(ssub + 1) * SUB], out2_t[:, :, 0:n], group="dbgx2")
            sa_hook(4)
        acc_free_scalar = S.fence("scalar")
        prev_ST_out2_v = tok_x2
    sa_hook(10 ** 9)
    tok_mq_done = st_toks[127]
    tok_mq_done2 = st_toks[126]
    tok_x1dbg = None
    if "x1" in dbg:
        S.wait("scalar", x1s_tok[9])
        tok_x1dbg = S.dma("scalar", dbg["x1"][:], x1_t[:], group="dbgx1")
    tokA_v = S.fence("vector")
    tokA_s = S.fence("scalar")
    tokA_t = S.fence("tensor")
    tok_spill_all = x2_sub_store[NSUB - 1]
    tok_spill_dbg = x2_sub_store.get((NSUB - 1, 'dbg'))

    S.waits("gpsimd", [tok_mq_done, tok_mq_done2, tokA_v, tokA_s, tokA_t, tok_x1dbg])
    tok_m1 = S.dma("gpsimd", m1_t[:], m1_dram, group="m1")
    S.waits("vector", [tokA_v, tokA_s, tokA_t, tok_spill_all, tok_spill_dbg, tok_x1dbg])
    S.op("vector", lambda E: E.memset(m8_t[:], 0.0), writes=("m8",))
    S.op("vector", lambda E: E.memset(tmstt[:], 0.0), writes=("tmstt",))
    S.op("vector", lambda E: E.memset(tmstt2[:], 0.0), writes=("tmstt2",))
    S.op("vector", lambda E: E.memset(tmstt3[:], 0.0), writes=("tmstt3",))
    S.op("vector", lambda E: E.memset(sb4t[:], 0.0), writes=("sb4t",))
    S.op("vector", lambda E: E.memset(sb4bt[:], 0.0), writes=("sb4bt",))
    tok_b1_init = S.fence("vector")
    S.waits("scalar", [tok_spill_all, tok_spill_dbg, tok_x1dbg, tok_b1_init])
    S.waits("sync", [tok_spill_all, tok_spill_dbg, tok_x1dbg, tokA_v, tokA_s, tokA_t, tok_b1_init])
    S.waits("tensor", [tok_b1_init])

    n = HR
    nq = SUB
    free["strips"] = None
    free["m8h"] = None
    free["sb4h"] = None
    free["sb4tl"] = None
    ST8 = {}

    def H_step(t, k, st):
        if k == 0:
            lo = t * SUB - 1
            llo = max(lo, 0)
            lhi = min(lo + HR, WOUT)
            n_load = lhi - llo
            dst0 = llo - lo
            prev = ST8.get(t - 1, {})
            S.waits("sync", [prev.get("hx2_free"), prev.get("hx2_free_s")])
            st["t_x2"] = S.dma("sync", hx2_t[:, :, dst0:dst0 + n_load], x2_spill[:, :, llo:lhi], group="hx2")
            S.waits("vector", [st["t_x2"]])
            if dst0 > 0:
                S.op("vector", lambda E: E.memset(hx2_t[:, :, 0:1], 0.0), writes=("hx2",))
            if lo + HR > WOUT:
                S.op("vector", lambda E, e0=dst0 + n_load: E.memset(hx2_t[:, :, e0:HR], 0.0), writes=("hx2",))
            if t == 0:
                S.op("vector", lambda E: E.tensor_scalar(hx2_t[:, :, 1:5], hx2_t[:, :, 1:5], masks_s[:, 0:1], None, ALU.mult),
                     reads=("hx2",), writes=("hx2",))
            if t == NSUB - 1:
                S.op("vector", lambda E: E.tensor_scalar(hx2_t[:, :, 453:457], hx2_t[:, :, 453:457], masks_s[:, 1:2], None, ALU.mult),
                     reads=("hx2",), writes=("hx2",))
            st["tok_x2in"] = S.fence("vector")
        elif k == 1:
            S.waits("scalar", [st["tok_x2in"]])
            st["tok_stats"] = emit_stats(lambda c: hx2_t[:, c, 0:n], n, bstat_t, "hx2")
            st["stats_sc"] = S.fence("scalar")
        elif k == 2:
            S.waits("vector", [st["tok_stats"], free["strips"]])
            S.op("vector", lambda E: E.tensor_scalar(strip1[0:1, 0:n], ps_s1[0:1, 0:n], 1.0 / D, None, ALU.mult),
                 writes=("strip1",))
            S.op("vector", lambda E: E.tensor_scalar(strip2[0:1, 0:n], ps_s2[0:1, 0:n], 1.0 / D, None, ALU.mult),
                 writes=("strip2",))
            free["stats"] = S.fence("vector")
            S.op("vector", lambda E: E.tensor_tensor(strip3[0:1, 0:n], strip1[0:1, 0:n], strip1[0:1, 0:n], ALU.mult),
                 reads=("strip1",), writes=("strip3",))
            S.op("vector", lambda E: E.scalar_tensor_tensor(strip2[0:1, 0:n], strip3[0:1, 0:n], -1.0, strip2[0:1, 0:n], ALU.mult, ALU.add),
                 reads=("strip2", "strip3"), writes=("strip2",))
            tk = S.fence("vector")
            S.wait("scalar", tk)
            S.op("scalar", lambda E: E.activation(strip3[0:1, 0:n], strip2[0:1, 0:n], AF.Sqrt, bias=epsv[0:1, 0:1]),
                 reads=("strip2",), writes=("strip3",))
            st["sq_done"] = S.fence("scalar")
        elif k == 3:
            S.wait("vector", st["sq_done"])
            S.op("vector", lambda E: E.reciprocal(strip2[0:1, 0:n], strip3[0:1, 0:n]),
                 reads=("strip3",), writes=("strip2",))
            S.op("vector", lambda E: E.tensor_scalar(strip3[0:1, 0:n], strip1[0:1, 0:n], -1.0, None, ALU.mult),
                 reads=("strip1",), writes=("strip3",))
            tk = S.fence("vector")
            S.waits("tensor", [tk, free["pb3"]])
            S.emit("tensor", lambda E: E.matmul(pb[3][:, 0:n], ones1f[0:1, :], strip3[0:1, 0:n], start=True, stop=True))
            st["mu_bc"] = S.fence("tensor")
        elif k == 4:
            S.waits("vector", [st["mu_bc"], ST8.get(t - 1, {}).get("tok_k3")])
            for c in range(DC):
                S.op("vector", lambda E, c=c: E.tensor_tensor(u_t[:, c, 0:n], hx2_t[:, c, 0:n], pb[3][:, 0:n], ALU.add),
                     reads=("hx2",), writes=("u",))
            st["tok_u"] = S.fence("vector")
            free["pb3"] = st["tok_u"]
            st["hx2_free"] = st["tok_u"]
            st["hx2_free_s"] = st["stats_sc"]
        elif k == 5:
            S.waits("vector", [free["m8h"]])
            S.op("vector", lambda E: E.reduce_max(m8_t[:, 0, 0:n], u_t[:].rearrange("p c n -> p n c"), axis=AX.X, apply_absolute_value=True),
                 reads=("u",), writes=("m8",))
            tk = S.fence("vector")
            S.waits("tensor", [tk, free["pb2"]])
            for bkk in range(4):
                S.emit("tensor", lambda E, bkk=bkk: E.transpose(
                    pb[2][:, bkk * 128:(bkk + 1) * 128], m8_t[:, 0, bkk * 128:(bkk + 1) * 128], ident[:]))
            S.waits("tensor", [free["pb5"], free["stats"]])
            for bkk in range(4):
                S.emit("tensor", lambda E, bkk=bkk: E.transpose(
                    pb[5][:, bkk:bkk + 1], strip2[0:1, bkk * 128:(bkk + 1) * 128], ident[0:1, 0:1]))
            tk_tr = S.fence("tensor")
            S.wait("vector", tk_tr)
            S.op("vector", lambda E: E.reduce_max(
                tmst[:, 0:4], pb[2][:, 0:512].rearrange("p (b q) -> p b q", q=128), axis=AX.X),
                writes=("tmst",))
            S.op("vector", lambda E: E.tensor_copy(tmst2[:, 0:4], pb[5][:, 0:4]), writes=("tmst2",))
            st["tm1"] = S.fence("vector")
            free["m8h"] = st["tm1"]
            free["pb2"] = st["tm1"]
            free["pb5"] = st["tm1"]
        elif k == 6:
            S.op("vector", lambda E: E.tensor_tensor(tmst[:, 4:8], tmst[:, 0:4], tmst2[:, 0:4], ALU.mult),
                 reads=("tmst", "tmst2"), writes=("tmst",))
            S.op("vector", lambda E: E.tensor_scalar(tmst[:, 8:12], tmst[:, 4:8], 1e-5, None, ALU.max),
                 reads=("tmst",), writes=("tmst",))
            S.op("vector", lambda E: E.reciprocal(tmst3[:, 0:4], tmst[:, 8:12]),
                 reads=("tmst",), writes=("tmst3",))
            S.op("vector", lambda E: E.tensor_scalar(tmst3[:, 4:8], tmst3[:, 0:4], 127.0, None, ALU.mult),
                 reads=("tmst3",), writes=("tmst3",))
            S.op("vector", lambda E: E.tensor_tensor(tmst3[:, 8:12], tmst3[:, 4:8], tmst2[:, 0:4], ALU.mult),
                 reads=("tmst3", "tmst2"), writes=("tmst3",))
            S.op("vector", lambda E: E.tensor_scalar(tmst2[:, 4:8], tmst[:, 8:12], wsvec[:, 2:3], None, ALU.mult),
                 reads=("tmst", "wsvec"), writes=("tmst2",))
            tk = S.fence("vector")
            S.waits("tensor", [tk, free["pb5"], free["stats"], free["sb4h"]])
            for bkk in range(4):
                S.emit("tensor", lambda E, bkk=bkk: E.transpose(
                    pb[5][0:1, bkk * 128:(bkk + 1) * 128], tmst3[:, 8 + bkk:9 + bkk], ident[:]))
                S.emit("tensor", lambda E, bkk=bkk: E.transpose(
                    ps_s2[0:1, bkk * 128:(bkk + 1) * 128], tmst2[:, 4 + bkk:5 + bkk], ident[:]))
            tk_tr = S.fence("tensor")
            S.wait("vector", tk_tr)
            S.op("vector", lambda E: E.tensor_copy(sb4[0:1, 0:512], pb[5][0:1, 0:512]), writes=("sb4",))
            S.op("vector", lambda E: E.tensor_copy(sb4b[0:1, 0:512], ps_s2[0:1, 0:512]), writes=("sb4b",))
            tk = S.fence("vector")
            free["pb5"] = tk
            free["stats"] = tk
            S.waits("tensor", [tk, free["pb4"], free["pb5"]])
            S.emit("tensor", lambda E: E.matmul(pb[4][:, 0:512], ones1f[0:1, :], sb4[0:1, 0:512], start=True, stop=True))
            S.emit("tensor", lambda E: E.matmul(pb[5][:, 0:512], ones1f[0:1, :], sb4b[0:1, 0:512], start=True, stop=True))
            st["bc_xq"] = S.fence("tensor")
            free["sb4h"] = st["bc_xq"]
            free["strips"] = st["bc_xq"]
        elif k == 7:
            kslot = t % 2
            S.waits("vector", [st["bc_xq"], ST8.get(t - 1, {}).get("w1_done")])
            for c in range(DC):
                S.op("vector", lambda E, c=c: E.tensor_tensor(hq_t[:, 0, 0:n], u_t[:, c, 0:n], pb[4][:, 0:n], ALU.mult),
                     reads=("u",), writes=("hq0",))
                S.op("vector", lambda E: E.tensor_scalar(qb_t[:, 0, 0:n], hq_t[:, 0, 0:n], M1C, -M1C, ALU.add, ALU.add),
                     reads=("hq0",), writes=("qb0",))
                S.op("vector", lambda E: E.tensor_scalar(qd_t[:, 0, 0:n], hq_t[:, 0, 0:n], M2C, -M2C, ALU.add, ALU.add),
                     reads=("hq0",), writes=("qd0",))
                S.op("vector", lambda E: E.scalar_tensor_tensor(hq_t[:, 1, 0:n], qb_t[:, 0, 0:n], alpha[:], qd_t[:, 0, 0:n], ALU.mult, ALU.add),
                     reads=("qb0", "qd0", "alpha"), writes=("hq1",))
                S.op("vector", lambda E, c=c, kslot=kslot: E.tensor_tensor(k3_t[:, kslot, c, 0:n], hq_t[:, 1, 0:n], pb[5][:, 0:n], ALU.mult),
                     reads=("hq1",), writes=("k3",))
            st["tok_k3"] = S.fence("vector")
            free["pb4"] = st["tok_k3"]
            free["pb5"] = st["tok_k3"]

    NHEAD = 8

    def T_step(t, k, st):
        kslot = t % 2
        if k < 8:
            if k == 0:
                S.waits("tensor", [tok_m1, st["tok_k3"]])
                st["silu_pending"] = []
            for hc in range(4 * k, 4 * k + 4):
                bkid = hc % 2
                pbk = pb[bkid]
                pp = 3 * (hc % 2)
                S.waits("tensor", [free[f"pb{bkid}"]])
                for kc in range(DC):
                    S.emit("tensor", lambda E, hc=hc, kc=kc, pbk=pbk, kslot=kslot: E.matmul(
                        pbk[:, 0:n], m1_t[:, hc, kc * 128:(kc + 1) * 128], k3_t[:, kslot, kc, 0:n],
                        start=(kc == 0), stop=(kc == DC - 1)))
                tk_h = S.fence("tensor")
                S.waits("scalar", [tk_h])
                S.op("scalar", lambda E, hc=hc, pbk=pbk, pp=pp: E.activation(
                    cv_t[:, pp + 0, 0:n - 2], pbk[:, 0:n - 2], AF.Copy,
                    scale=convw_s[:, 3 * hc:3 * hc + 1]),
                    writes=(f"cv0{pp}",))
                tk0 = S.fence("scalar")
                if st["silu_pending"]:
                    phc, ppp, ptk2 = st["silu_pending"].pop()
                    S.waits("scalar", [ptk2])
                    S.op("scalar", lambda E, phc=phc, ppp=ppp: E.activation(
                        hs_t[:, phc, 0:n - 2], cv_t[:, ppp + 2, 0:n - 2], AF.Silu,
                        bias=convb_s[:, phc:phc + 1]),
                        reads=(f"cv2{ppp}",), writes=("hs",))
                S.waits("vector", [tk0, tk_h])
                S.op("vector", lambda E, hc=hc, pbk=pbk, pp=pp: E.scalar_tensor_tensor(
                    cv_t[:, pp + 1, 0:n - 2], pbk[:, 1:n - 1], convw_s[:, 3 * hc + 1:3 * hc + 2], cv_t[:, pp + 0, 0:n - 2], ALU.mult, ALU.add),
                    reads=(f"cv0{pp}",), writes=(f"cv1{pp}",))
                S.op("vector", lambda E, hc=hc, pbk=pbk, pp=pp: E.scalar_tensor_tensor(
                    cv_t[:, pp + 2, 0:n - 2], pbk[:, 2:n], convw_s[:, 3 * hc + 2:3 * hc + 3], cv_t[:, pp + 1, 0:n - 2], ALU.mult, ALU.add),
                    reads=(f"cv1{pp}",), writes=(f"cv2{pp}",))
                tk2 = S.fence("vector")
                free[f"pb{bkid}"] = tk2
                st["silu_pending"].append((hc, pp, tk2))
            if k == 7:
                st["w1_done"] = tk2
                for phc, ppp, ptk2 in st["silu_pending"]:
                    S.waits("scalar", [ptk2])
                    S.op("scalar", lambda E, phc=phc, ppp=ppp: E.activation(
                        hs_t[:, phc, 0:n - 2], cv_t[:, ppp + 2, 0:n - 2], AF.Silu,
                        bias=convb_s[:, phc:phc + 1]),
                        reads=(f"cv2{ppp}",), writes=("hs",))
                st["tok_hs"] = S.fence("scalar")
        elif k == 8:
            S.waits("vector", [st["tok_hs"], free.get("m8t")])
            S.op("vector", lambda E: E.memset(m8_t[:, 1, nq:512], 0.0), writes=("m8t",))
            S.op("vector", lambda E: E.reduce_max(m8_t[:, 1, 0:nq], hs_t[:, :, 0:nq].rearrange("p c n -> p n c"), axis=AX.X, apply_absolute_value=True),
                 reads=("hs",), writes=("m8t",))
            st["m8b"] = S.fence("vector")
        elif k == 9:
            S.waits("tensor", [st["m8b"], free["pb2"]])
            for bkk in range(4):
                S.emit("tensor", lambda E, bkk=bkk: E.transpose(
                    pb[2][:, bkk * 128:(bkk + 1) * 128], m8_t[:, 1, bkk * 128:(bkk + 1) * 128], ident[:]))
            tk_tr = S.fence("tensor")
            S.wait("vector", tk_tr)
            S.op("vector", lambda E: E.reduce_max(
                tmstt[:, 0:4], pb[2][:, 0:512].rearrange("p (b q) -> p b q", q=128), axis=AX.X),
                writes=("tmstt",))
            free["pb2"] = S.fence("vector")
            free["m8t"] = free["pb2"]
            S.op("vector", lambda E: E.tensor_scalar(tmstt[:, 4:8], tmstt[:, 0:4], 1e-5, None, ALU.max),
                 reads=("tmstt",), writes=("tmstt",))
            S.op("vector", lambda E: E.reciprocal(tmstt2[:, 0:4], tmstt[:, 4:8]),
                 reads=("tmstt",), writes=("tmstt2",))
            S.op("vector", lambda E: E.tensor_scalar(tmstt2[:, 4:8], tmstt2[:, 0:4], 127.0, None, ALU.mult),
                 reads=("tmstt2",), writes=("tmstt2",))
            S.op("vector", lambda E: E.tensor_scalar(tmstt3[:, 0:4], tmstt[:, 4:8], wsvec[:, 3:4], None, ALU.mult),
                 reads=("tmstt", "wsvec"), writes=("tmstt3",))
            tk = S.fence("vector")
            S.waits("tensor", [tk, free["pb5"], free["stats"], free["sb4tl"]])
            for bkk in range(4):
                S.emit("tensor", lambda E, bkk=bkk: E.transpose(
                    pb[5][0:1, bkk * 128:(bkk + 1) * 128], tmstt2[:, 4 + bkk:5 + bkk], ident[:]))
                S.emit("tensor", lambda E, bkk=bkk: E.transpose(
                    ps_s2[0:1, bkk * 128:(bkk + 1) * 128], tmstt3[:, bkk:bkk + 1], ident[:]))
            tk_tr = S.fence("tensor")
            S.waits("vector", [tk_tr, ST8.get(t - 1, {}).get("c2st")])
            S.op("vector", lambda E: E.tensor_copy(sb4t[0:1, 0:512], pb[5][0:1, 0:512]), writes=("sb4t",))
            S.op("vector", lambda E: E.tensor_copy(sb4bt[0:1, 0:512], ps_s2[0:1, 0:512]), writes=("sb4bt",))
            tk = S.fence("vector")
            free["pb5"] = tk
            free["stats"] = tk
            S.waits("scalar", [tk, ST8.get(t - 1, {}).get("c2st")])
            st["c2st"] = S.dma("scalar", c2_dram[0:1, t * SUB: t * SUB + 512], sb4bt[0:1, 0:512], group="c2st")
            S.waits("tensor", [tk, free["pb3"]])
            S.emit("tensor", lambda E: E.matmul(pb[3][:, 0:512], ones1f[0:1, :], sb4t[0:1, 0:512], start=True, stop=True))
            st["bc_s2"] = S.fence("tensor")
            free["sb4tl"] = st["bc_s2"]
        elif k == 10:
            o_lo2 = t * SUB
            klo = max(o_lo2, OUT_OFF)
            khi = min(o_lo2 + SUB, OUT_OFF + CT)
            a0 = klo - o_lo2
            S.waits("vector", [st["bc_s2"]])
            prev = ST8.get(t - 1, {})
            k2st = {-2: prev.get("k2_last0", []), -1: prev.get("k2_last1", [])}
            for hc in range(HC):
                S.waits("vector", list(k2st.get(hc - 2, [])))
                S.op("vector", lambda E, hc=hc: E.tensor_tensor(hq_t[:, 1, 0:nq], hs_t[:, hc, 0:nq], pb[3][:, 0:nq], ALU.mult),
                     reads=("hs",), writes=("hq1",))
                S.op("vector", lambda E: E.tensor_scalar(qb_t[:, 1, 0:nq], hq_t[:, 1, 0:nq], M1C, -M1C, ALU.add, ALU.add),
                     reads=("hq1",), writes=("qb1",))
                S.op("vector", lambda E: E.tensor_scalar(qd_t[:, 1, 0:nq], hq_t[:, 1, 0:nq], M2C, -M2C, ALU.add, ALU.add),
                     reads=("hq1",), writes=("qd1",))
                S.op("vector", lambda E, hc=hc: E.scalar_tensor_tensor(k2_t[:, hc % 2, 0:nq], qb_t[:, 1, 0:nq], alpha[:], qd_t[:, 1, 0:nq], ALU.mult, ALU.add),
                     reads=("qb1", "qd1", "alpha"), writes=(f"k2{hc % 2}",))
                tkk = S.fence("vector")
                S.waits("scalar", [tkk])
                toks = [S.dma("scalar", k2_spill[:, hc, klo - OUT_OFF:khi - OUT_OFF],
                              k2_t[:, hc % 2, a0:a0 + (khi - klo)], group=f"k2st{hc % 2}")]
                if "k2" in dbg:
                    toks.append(S.dma("scalar", dbg["k2"][:, hc, klo - OUT_OFF:khi - OUT_OFF],
                                      k2_t[:, hc % 2, a0:a0 + (khi - klo)], group=f"dbgk2{hc % 2}"))
                k2st[hc] = toks
            st["k2_last0"] = k2st[30]
            st["k2_last1"] = k2st[31]
            free["pb3"] = S.fence("vector")

    NTAIL = 11
    ST8[0] = {}
    for k in range(NHEAD):
        H_step(0, k, ST8[0])
    for t in range(NSUB):
        st = ST8[t]
        if t + 1 < NSUB:
            ST8[t + 1] = {}
            for k in range(NTAIL):
                T_step(t, k, st)
                if k < NHEAD:
                    H_step(t + 1, k, ST8[t + 1])
        else:
            for k in range(NTAIL):
                T_step(t, k, st)

    k2_final = list(ST8[NSUB - 1]["k2_last0"]) + list(ST8[NSUB - 1]["k2_last1"])
    tok_k2_dbg = None
    c2st_tok = ST8[NSUB - 1]["c2st"]
    tokB1_v = S.fence("vector")
    tokB1_s = S.fence("scalar")
    tokB1_t = S.fence("tensor")

    S.waits("gpsimd", [ST8[NSUB - 1]["w1_done"], tok_m1])
    m2c = {}
    for j in range(4):
        m2c[j] = S.dma("gpsimd", m2_t[:, :, j * 1024:(j + 1) * 1024],
                       m2_dram[:, :, j * 1024:(j + 1) * 1024], group=f"m2{j}")

    st2 = {}
    w2_banks = [0, 1, 2]

    def b2_loads(t):
        st = st2.setdefault(t, {})
        kb = k2in_t[:, t % 2]
        prev2 = st2.get(t - 2, {})
        if t == 0:
            S.waits("sync", k2_final + [c2st_tok, tokB1_v, tokB1_s, tokB1_t])
        else:
            S.waits("sync", [prev2.get("mm_done"), st2[t - 1].get("c2bc_t")])
        olo = OUT_OFF + t * B2T
        st["c2"] = S.dma("sync", c2strip[0:1, 0:B2T], c2_dram[0:1, olo:olo + B2T], group="c2ld")
        if t == 0:
            st["x2"] = S.dma("sync", bx2_t[:], x2_spill[:, :, olo:olo + B2T], group="bx2")
        st["k2c"] = {}
        for j in range(4):
            st["k2c"][j] = S.dma(
                "sync", kb[:, j * 8:(j + 1) * 8, 0:B2T],
                k2_spill[:, j * 8:(j + 1) * 8, t * B2T:(t + 1) * B2T], group=f"k2i{t % 2}{j}")

    def b2_head(t):
        st = st2[t]
        kb = k2in_t[:, t % 2]
        yb = y_t2[:, t % 2]
        olo = OUT_OFF + t * B2T
        S.waits("tensor", [st["c2"], free["pb5"]])
        S.emit("tensor", lambda E: E.matmul(pb[5][:, 0:B2T], ones1f[0:1, :], c2strip[0:1, 0:B2T], start=True, stop=True))
        st["c2bc_t"] = S.fence("tensor")
        S.wait("vector", st["c2bc_t"])
        S.op("vector", lambda E: E.tensor_copy(c2sb_t[:], pb[5][:, 0:B2T]), writes=("c2sb",))
        tok_c2bc = S.fence("vector")
        free["pb5"] = tok_c2bc
        for d in range(DC):
            bkid = w2_banks[d % 3]
            pbk = pb[bkid]
            S.waits("tensor", [free[f"pb{bkid}"]])
            for kc in range(HC):
                S.waits("tensor", [st["k2c"][kc // 8]])
                if t == 0:
                    S.waits("tensor", [m2c[kc // 8]])
                S.emit("tensor", lambda E, d=d, kc=kc, kb=kb, pbk=pbk: E.matmul(
                    pbk[:, 0:B2T], m2_t[:, d, kc * 128:(kc + 1) * 128], kb[:, kc, 0:B2T],
                    start=(kc == 0), stop=(kc == HC - 1)))
            tk_y = S.fence("tensor")
            st["mm_done"] = tk_y
            S.waits("vector", [tk_y, tok_c2bc, st2.get(t - 2, {}).get("res_done")])
            S.op("vector", lambda E, d=d, pbk=pbk, yb=yb: E.tensor_tensor(yb[:, d, 0:B2T], pbk[:, 0:B2T], c2sb_t[:], ALU.mult),
                 reads=("c2sb",), writes=("y",))
            free[f"pb{bkid}"] = S.fence("vector")
        st["deq_done"] = free[f"pb{w2_banks[(DC - 1) % 3]}"]

    def b2_tail(t):
        st = st2[t]
        yb = y_t2[:, t % 2]
        bx = bx2_t
        S.waits("scalar", [st["deq_done"]])
        tok_st = emit_stats(lambda c, yb=yb: yb[:, c, 0:B2T], B2T, b2stat_t, "y")
        tok_b = emit_ln_strips(B2T, tok_st)
        S.waits("vector", [tok_b, st["x2"]])
        for c in range(DC):
            S.op("vector", lambda E, c=c, yb=yb: E.tensor_tensor(yb[:, c, 0:B2T], yb[:, c, 0:B2T], pb[3][:, 0:B2T], ALU.mult),
                 reads=("y",), writes=("y",))
            S.op("vector", lambda E, c=c, bx=bx: E.tensor_tensor(bx[:, c, 0:B2T], bx[:, c, 0:B2T], pb[4][:, 0:B2T], ALU.add),
                 reads=("bx2",), writes=("bx2",))
            S.op("vector", lambda E, c=c, yb=yb, bx=bx: E.tensor_tensor(bx[:, c, 0:B2T], yb[:, c, 0:B2T], bx[:, c, 0:B2T], ALU.add),
                 reads=("y", "bx2"), writes=("bx2",))
        st["res_done"] = S.fence("vector")
        free["pb3"] = st["res_done"]
        free["pb4"] = st["res_done"]
        S.waits("scalar", [st["res_done"]])
        st["store"] = S.dma("scalar", out_d[:, :, t * B2T:(t + 1) * B2T], bx[:, :, 0:B2T], group="outst")
        if t + 1 < NB2:
            nst = st2.setdefault(t + 1, {})
            S.waits("sync", [st["store"]])
            olo2 = OUT_OFF + (t + 1) * B2T
            nst["x2"] = S.dma("sync", bx2_t[:], x2_spill[:, :, olo2:olo2 + B2T], group="bx2")

    b2_loads(0)
    b2_head(0)
    b2_loads(1)
    for t in range(1, NB2):
        b2_head(t)
        if t + 1 < NB2:
            b2_loads(t + 1)
        b2_tail(t - 1)
    b2_tail(NB2 - 1)

    S.wait("gpsimd", ("d_scalar_outst", S.cnt["d_scalar_outst"]))

    with nc.Block() as block:
        @block.sync
        def _(E):
            for fn in S.q["sync"]:
                fn(E)

        @block.gpsimd
        def _(E):
            for fn in S.q["gpsimd"]:
                fn(E)

        @block.vector
        def _(E):
            for fn in S.q["vector"]:
                fn(E)

        @block.scalar
        def _(E):
            for fn in S.q["scalar"]:
                fn(E)

        @block.tensor
        def _(E):
            for fn in S.q["tensor"]:
                fn(E)

    stack.close()
    return nc



def _prep_inputs(x, gate_W, value_W, gamma, beta, w1, conv_w, conv_b, w2):
    gate_prep = np.ascontiguousarray(
        gate_W.reshape(NJ, DC, 128, DC, 128).transpose(1, 4, 0, 3, 2).reshape(DC, 128, NJ, D))
    value_prep = np.ascontiguousarray(
        value_W.reshape(DC, 128, DC, 128).transpose(3, 2, 0, 1).reshape(128, DC, D))
    w1_prep = np.ascontiguousarray(
        w1.reshape(HC, 128, DC, 128).transpose(0, 3, 2, 1).reshape(HC, 128, D))
    w2_prep = np.ascontiguousarray(
        w2.reshape(DC, 128, HC, 128).transpose(0, 3, 2, 1).reshape(DC, 128, H))
    conv_prep = np.ascontiguousarray(
        conv_w.reshape(HC, 128, 3).transpose(1, 0, 2).reshape(128, 3 * HC))
    convb_prep = np.ascontiguousarray(conv_b.reshape(HC, 128).T)
    gamma_prep = np.ascontiguousarray(gamma.reshape(DC, 128).T)
    beta_prep = np.ascontiguousarray(beta.reshape(DC, 128).T)
    ident = np.eye(128, dtype=np.float32)
    onesb = np.ones((128, 2), dtype=ml_dtypes.bfloat16)
    onesf = np.ones((128, 2), dtype=np.float32)

    shared = dict(gate_prep=gate_prep, value_prep=value_prep, w1_prep=w1_prep,
                  w2_prep=w2_prep, conv_prep=conv_prep, convb_prep=convb_prep,
                  gamma_prep=gamma_prep, beta_prep=beta_prep, ident=ident,
                  onesb=onesb, onesf=onesf)

    in_maps = []
    for core in range(8):
        b, half = core >> 1, core & 1
        g0 = half * (SEQ // 2)
        idx = (np.arange(g0 - HALO, g0 + CT + HALO)) % SEQ
        xs = np.ascontiguousarray(x[b][idx].T.reshape(DC, 128, WX).transpose(1, 0, 2))
        masks = np.ones((128, 2), dtype=np.float32)
        if half == 0:
            masks[:, 0] = 0.0
        else:
            masks[:, 1] = 0.0
        m = dict(shared)
        m["x_fm"] = xs
        m["masks"] = masks
        in_maps.append(m)
    return in_maps


def kernel(L0_attn_x, L0_attn_gate_W, L0_attn_value_W, L0_attn_gamma,
           L0_attn_beta, L0_ffn_w1, L0_ffn_conv_w, L0_ffn_conv_b, L0_ffn_w2,
           _debug=(), _core_ids=None):
    inp = [np.asarray(a, dtype=np.float32) for a in
           (L0_attn_x, L0_attn_gate_W, L0_attn_value_W, L0_attn_gamma,
            L0_attn_beta, L0_ffn_w1, L0_ffn_conv_w, L0_ffn_conv_b, L0_ffn_w2)]
    in_maps = _prep_inputs(*inp)
    nc = build_program(debug=_debug)
    core_ids = list(range(8)) if _core_ids is None else _core_ids
    res = run_bass_kernel_spmd(nc, in_maps[: len(core_ids)], core_ids=core_ids)
    out = np.empty((B, SEQ, D), dtype=np.float32)
    for core in range(len(core_ids)):
        b, half = core >> 1, core & 1
        g0 = half * (SEQ // 2)
        o = res.results[core]["out_fm"]
        out[b, g0:g0 + CT, :] = np.asarray(o).transpose(2, 1, 0).reshape(CT, D)
    if _debug:
        return out, res
    return out


# version 10 (final)
# speedup vs baseline: 1.2280x; 1.2050x over previous
import contextlib
import numpy as np
import ml_dtypes
import concourse.bass as bass
import concourse.mybir as mybir
from concourse.bass_utils import run_bass_kernel_spmd

F32 = mybir.dt.float32
BF16 = mybir.dt.bfloat16
ALU = mybir.AluOpType
AF = mybir.ActivationFunctionType
AX = mybir.AxisListType

B, SEQ, D = 4, 8192, 1024
DC = D // 128
H = 4096
HC = H // 128
NJ = 9
SHIFTS = [0, -32, -16, -4, -1, 1, 4, 16, 32]
LN_EPS = 1e-5

CT = 4096
HALO = 36
WX = CT + 2 * HALO
WOUT = 4104
SUB = 456
NSUB = 9
ST_SUBS = [(0, 2), (2, 4), (4, 6), (6, 8), (8, 9)]
MAXW = 2 * SUB
B2T = 512
NB2 = 8
OUT_OFF = 4
X_OFF = HALO - OUT_OFF
HR = 458

M1C = float(1.5 * 2.0 ** 23)
M2C = float(1.5 * 2.0 ** 52)
WEPS = float(2.0 ** -40)
WINV = float(2.0 ** 40)

ENGS = ("sync", "gpsimd", "vector", "scalar", "tensor")


class Sched:
    def __init__(self, nc, stack):
        self.nc = nc
        self.q = {e: [] for e in ENGS}
        self.sem = {}
        self.cnt = {}
        self.waited = {e: {} for e in ENGS}
        self.dirty_w = {e: set() for e in ENGS}
        self.dirty_r = {e: set() for e in ENGS}
        self.last_tinst = None
        self._stack = stack
        for name in ("f_gpsimd", "f_vector", "f_scalar", "f_tensor"):
            self.sem[name] = stack.enter_context(nc.semaphore(name))
            self.cnt[name] = 0

    def emit(self, e, fn):
        self.q[e].append(fn)
        if e == "tensor":
            self.last_tinst = len(self.q[e]) - 1

    def wait(self, e, tok):
        if tok is None:
            return
        name, val = tok
        if val <= 0:
            return
        if self.waited[e].get(name, 0) >= val:
            return
        self.waited[e][name] = val
        sem = self.sem[name]
        self.q[e].append(lambda E, s=sem, v=val: E.wait_ge(s, v))

    def waits(self, e, toks):
        for t in toks:
            self.wait(e, t)

    def _hazard(self, e, reads, writes):
        dw, dr = self.dirty_w[e], self.dirty_r[e]
        if any(r in dw for r in reads) or any(w in dw or w in dr for w in writes):
            self.q[e].append(lambda E: E.drain())
            dw.clear()
            dr.clear()
        dr.update(reads)
        dw.update(writes)

    def op(self, e, fn, reads=(), writes=()):
        self._hazard(e, reads, writes)
        self.q[e].append(fn)

    def fence(self, e):
        name = "f_" + e
        self.cnt[name] += 1
        val = self.cnt[name]
        sem = self.sem[name]
        if e == "tensor":
            if self.last_tinst is None:
                self.cnt[name] -= 1
                return (name, self.cnt[name])
            prev = self.q[e][self.last_tinst]
            self.q[e][self.last_tinst] = (
                lambda E, f=prev, s=sem: f(E).then_inc(s, 1))
            self.last_tinst = None
        else:
            self.q[e].append(lambda E, s=sem: E.drain().then_inc(s, 1))
        self.dirty_w[e].clear()
        self.dirty_r[e].clear()
        return (name, val)

    def dma(self, e, out, in_, group="g", **kw):
        name = f"d_{e}_{group}"
        if name not in self.sem:
            self.sem[name] = self._stack.enter_context(self.nc.semaphore(name))
            self.cnt[name] = 0
        self.cnt[name] += 16
        val = self.cnt[name]
        sem = self.sem[name]
        self.q[e].append(
            lambda E, s=sem, o=out, i=in_, k=kw: E.dma_start(out=o, in_=i, **k).then_inc(s, 16))
        return (name, val)


def build_program(debug=()):
    nc = bass.Bass("TRN2", target_bir_lowering=False)
    stack = contextlib.ExitStack()

    x_in = nc.dram_tensor("x_fm", [128, DC, WX], F32, kind="ExternalInput").ap()
    gate_w = nc.dram_tensor("gate_prep", [DC, 128, NJ, D], F32, kind="ExternalInput").ap()
    value_w = nc.dram_tensor("value_prep", [128, DC, D], F32, kind="ExternalInput").ap()
    w1_w = nc.dram_tensor("w1_prep", [HC, 128, D], F32, kind="ExternalInput").ap()
    w2_w = nc.dram_tensor("w2_prep", [DC, 128, H], F32, kind="ExternalInput").ap()
    convw = nc.dram_tensor("conv_prep", [128, 3 * HC], F32, kind="ExternalInput").ap()
    convb = nc.dram_tensor("convb_prep", [128, HC], F32, kind="ExternalInput").ap()
    gamma_d = nc.dram_tensor("gamma_prep", [128, DC], F32, kind="ExternalInput").ap()
    beta_d = nc.dram_tensor("beta_prep", [128, DC], F32, kind="ExternalInput").ap()
    ident_d = nc.dram_tensor("ident", [128, 128], F32, kind="ExternalInput").ap()
    onesb_d = nc.dram_tensor("onesb", [128, 2], BF16, kind="ExternalInput").ap()
    onesf_d = nc.dram_tensor("onesf", [128, 2], F32, kind="ExternalInput").ap()
    masks_d = nc.dram_tensor("masks", [128, 2], F32, kind="ExternalInput").ap()

    out_d = nc.dram_tensor("out_fm", [128, DC, CT], F32, kind="ExternalOutput").ap()

    x2_spill = nc.dram_tensor("x2_spill", [128, DC, WOUT], F32, kind="Internal").ap()
    k2_spill = nc.dram_tensor("k2_spill", [128, HC, CT], BF16, kind="Internal").ap()
    c2_dram = nc.dram_tensor("c2_dram", [1, WOUT + 64], F32, kind="Internal").ap()
    m1_dram = nc.dram_tensor("m1_dram", [128, HC, D], BF16, kind="Internal").ap()
    m2_dram = nc.dram_tensor("m2_dram", [128, DC, H], BF16, kind="Internal").ap()

    dbg = {}
    if "x2" in debug:
        dbg["x2"] = nc.dram_tensor("dbg_x2", [128, DC, WOUT], F32, kind="ExternalOutput").ap()
    if "k2" in debug:
        dbg["k2"] = nc.dram_tensor("dbg_k2", [128, HC, CT], BF16, kind="ExternalOutput").ap()
    if "ws" in debug:
        dbg["ws"] = nc.dram_tensor("dbg_ws", [128, 8], F32, kind="ExternalOutput").ap()
    if "x1" in debug:
        dbg["x1"] = nc.dram_tensor("dbg_x1", [128, DC, WX], BF16, kind="ExternalOutput").ap()

    def alloc(name, shape, dtype, off):
        n = int(np.prod(shape[1:])) * mybir.dt.size(dtype)
        t = nc.alloc_sbuf_tensor_at(name, shape, dtype, offset=off)
        return t, off + ((n + 31) & ~31)

    pers = 17408
    ident, pers = alloc("ident_s", [128, 128], F32, pers)
    onesb, pers = alloc("onesb_s", [128, 2], BF16, pers)
    onesf, pers = alloc("onesf_s", [128, 2], F32, pers)
    ones1f, pers = alloc("ones1f", [1, 128], F32, pers)
    alpha, pers = alloc("alpha_s", [128, 1], F32, pers)
    a_tmp, pers = alloc("a_tmp", [128, 2], F32, pers)
    gamma_s, pers = alloc("gamma_s", [128, DC], F32, pers)
    beta_s, pers = alloc("beta_s", [128, DC], F32, pers)
    convw_s, pers = alloc("convw_s", [128, 3 * HC], F32, pers)
    convb_s, pers = alloc("convb_s", [128, HC], F32, pers)
    masks_s, pers = alloc("masks_s", [128, 2], F32, pers)
    wsvec, pers = alloc("wsvec", [128, 4], F32, pers)
    wpart, pers = alloc("wpart", [128, 132], F32, pers)
    strip1, pers = alloc("strip1", [1, 928], F32, pers)
    strip2, pers = alloc("strip2", [1, 928], F32, pers)
    strip3, pers = alloc("strip3", [1, 928], F32, pers)
    tmst, pers = alloc("tmst", [128, 16], F32, pers)
    tmst2, pers = alloc("tmst2", [128, 16], F32, pers)
    tmst3, pers = alloc("tmst3", [128, 16], F32, pers)
    sb4, pers = alloc("sb4", [1, 512], F32, pers)
    sb4b, pers = alloc("sb4b", [1, 512], F32, pers)
    epsv, pers = alloc("epsv", [1, 2], F32, pers)
    PERS_END = pers

    A0 = (PERS_END + 63) & ~63

    o = A0
    wstream, o = alloc("wstream", [128, 6, 512], F32, o)
    mq_b, o = alloc("mq_b", [128, 512], F32, o)
    mq_d, o = alloc("mq_d", [128, 512], F32, o)
    mq_u, o = alloc("mq_u", [128, 512], F32, o)
    mq_m, o = alloc("mq_m", [128, 2, 512], BF16, o)
    WPH_END = o

    o = WPH_END
    x1_t, o = alloc("x1_t", [128, DC, WX], BF16, o)
    acc_t, o = alloc("acc_t", [128, DC, MAXW], F32, o)
    gate_t, o = alloc("gate_t", [128, 2, 3, D], BF16, o)
    val_t, o = alloc("val_t", [128, DC, D], BF16, o)
    v_t, o = alloc("v_t", [128, 2, MAXW + 64], BF16, o)
    g_t, o = alloc("g_t", [128, 3, MAXW], BF16, o)
    prod_t, o = alloc("prod_t", [128, MAXW], BF16, o)
    xin_t, o = alloc("xin_t", [128, DC, SUB], F32, o)
    stat_t, o = alloc("stat_t", [128, 2, 2 * SUB], BF16, o)
    out2_t, o = alloc("out2_t", [128, DC, SUB], F32, o)
    zt_t, o = alloc("zt_t", [128, 2, SUB], F32, o)
    A_END = o
    assert A_END < 223 * 1024, A_END

    o = A0
    m1_t, o = alloc("m1_t", [128, HC, D], BF16, o)
    k3_t, o = alloc("k3_t", [128, 2, DC, HR], BF16, o)
    k2_t, o = alloc("k2_t", [128, 2, SUB], BF16, o)
    cv_t, o = alloc("cv_t", [128, 6, HR], F32, o)
    bstat_t, o = alloc("bstat_t", [128, 2, 2 * HR], BF16, o)
    m8_t, o = alloc("m8_t", [128, 2, 512], F32, o)
    qb_t, o = alloc("qb_t", [128, 2, HR], F32, o)
    qd_t, o = alloc("qd_t", [128, 2, HR], F32, o)
    hq_t, o = alloc("hq_t", [128, 2, HR], F32, o)
    hx2_t, o = alloc("hx2_t", [128, DC, HR], F32, o)
    u_t, o = alloc("u_t", [128, DC, HR], F32, o)
    hs_t, o = alloc("hs_t", [128, HC, HR], BF16, o)
    sb4t, o = alloc("sb4t", [1, 512], F32, o)
    sb4bt, o = alloc("sb4bt", [1, 512], F32, o)
    tmstt, o = alloc("tmstt", [128, 16], F32, o)
    tmstt2, o = alloc("tmstt2", [128, 16], F32, o)
    tmstt3, o = alloc("tmstt3", [128, 16], F32, o)
    B1_END = o
    assert B1_END < 223 * 1024, B1_END

    o = A0
    m2_t, o = alloc("m2_t", [128, DC, H], BF16, o)
    k2in_t, o = alloc("k2in_t", [128, 2, HC, B2T], BF16, o)
    bx2_t, o = alloc("bx2_t", [128, DC, B2T], F32, o)
    y_t2, o = alloc("y_t", [128, 2, DC, B2T], F32, o)
    B2_END = o
    assert B2_END < 229376, B2_END
    o2 = max(B1_END, B2_END)
    b2stat_t, o2 = alloc("b2stat_t", [128, 2, 2 * B2T], BF16, o2)
    c2sb_t, o2 = alloc("c2sb_t", [128, B2T], F32, o2)
    c2strip, o2 = alloc("c2strip", [1, B2T], F32, o2)
    assert o2 < 229376, o2

    pb = [nc.alloc_psum_tensor(f"pb{i}", [128, 512], F32) for i in range(6)]
    ps_s1 = nc.alloc_psum_tensor("ps_s1", [2, 512], F32)
    ps_s2 = nc.alloc_psum_tensor("ps_s2", [2, 512], F32)

    S = Sched(nc, stack)

    free = {"stats": None, "pb0": None, "pb1": None, "pb2": None, "pb3": None,
            "pb4": None, "pb5": None}

    for ap_o, ap_i in [(ident[:], ident_d), (onesb[:], onesb_d), (onesf[:], onesf_d),
                       (gamma_s[:], gamma_d), (beta_s[:], beta_d), (convw_s[:], convw),
                       (convb_s[:], convb), (masks_s[:], masks_d)]:
        t_const = S.dma("sync", ap_o, ap_i, group="const")
    S.wait("vector", t_const)
    S.wait("tensor", t_const)
    S.wait("scalar", t_const)
    S.wait("gpsimd", t_const)
    S.op("vector", lambda E: E.memset(ones1f[:], 1.0), writes=("ones1f",))
    S.op("vector", lambda E: E.memset(strip1[:], 0.0), writes=("strip1",))
    S.op("vector", lambda E: E.memset(strip2[:], 0.0), writes=("strip2",))
    S.op("vector", lambda E: E.memset(strip3[:], 0.0), writes=("strip3",))
    S.op("vector", lambda E: E.memset(tmst[:], 0.0), writes=("tmst",))
    S.op("vector", lambda E: E.memset(tmst2[:], 0.0), writes=("tmst2",))
    S.op("vector", lambda E: E.memset(tmst3[:], 0.0), writes=("tmst3",))
    S.op("vector", lambda E: E.memset(sb4[:], 0.0), writes=("sb4",))
    S.op("vector", lambda E: E.memset(sb4b[:], 0.0), writes=("sb4b",))
    S.op("vector", lambda E: E.memset(epsv[:], LN_EPS), writes=("epsv",))
    S.op("vector", lambda E: E.memset(a_tmp[:, 0:1], 1.0), writes=("a_tmp",))
    S.op("vector", lambda E: E.tensor_scalar(a_tmp[:, 1:2], a_tmp[:, 0:1], WEPS, -1.0, ALU.add, ALU.add),
         reads=("a_tmp",), writes=("a_tmp",))
    S.op("vector", lambda E: E.tensor_scalar(alpha[:], a_tmp[:, 1:2], -WINV, 1.0, ALU.mult, ALU.add),
         reads=("a_tmp",), writes=("alpha",))
    tok_alpha = S.fence("vector")

    def emit_stats(src_fn, n, sbuf, tag, chunk_tok=None):
        S.waits("tensor", [free["stats"]])
        chunk_toks = [None, None]
        for c in range(DC):
            sb = sbuf[:, c % 2]
            if chunk_tok is not None:
                S.wait("scalar", chunk_tok(c))
            S.wait("scalar", chunk_toks[c % 2])
            S.op("scalar", lambda E, c=c, sb=sb: E.copy(sb[:, 0:n], src_fn(c)),
                 reads=(tag,), writes=("statbuf",))
            S.op("scalar", lambda E, c=c, sb=sb: E.activation(sb[:, n:2 * n], src_fn(c), AF.Square),
                 reads=(tag,), writes=("statbuf",))
            tk = S.fence("scalar")
            S.wait("tensor", tk)
            S.emit("tensor", lambda E, c=c, sb=sb: E.matmul(
                ps_s1[0:1, 0:n], onesb[:, 0:1], sb[:, 0:n], start=(c == 0), stop=(c == DC - 1)))
            S.emit("tensor", lambda E, c=c, sb=sb: E.matmul(
                ps_s2[0:1, 0:n], onesb[:, 0:1], sb[:, n:2 * n], start=(c == 0), stop=(c == DC - 1)))
            chunk_toks[c % 2] = S.fence("tensor")
        return chunk_toks[(DC - 1) % 2]

    def emit_ln_strips(n, tok_stats, b_r=3, b_mu=4):
        S.wait("vector", tok_stats)
        S.op("vector", lambda E: E.tensor_scalar(strip1[0:1, 0:n], ps_s1[0:1, 0:n], 1.0 / D, None, ALU.mult),
             writes=("strip1",))
        S.op("vector", lambda E: E.tensor_scalar(strip2[0:1, 0:n], ps_s2[0:1, 0:n], 1.0 / D, None, ALU.mult),
             writes=("strip2",))
        free["stats"] = S.fence("vector")
        S.op("vector", lambda E: E.tensor_tensor(strip3[0:1, 0:n], strip1[0:1, 0:n], strip1[0:1, 0:n], ALU.mult),
             reads=("strip1",), writes=("strip3",))
        S.op("vector", lambda E: E.scalar_tensor_tensor(strip2[0:1, 0:n], strip3[0:1, 0:n], -1.0, strip2[0:1, 0:n], ALU.mult, ALU.add),
             reads=("strip2", "strip3"), writes=("strip2",))
        tk = S.fence("vector")
        S.wait("scalar", tk)
        S.op("scalar", lambda E: E.activation(strip3[0:1, 0:n], strip2[0:1, 0:n], AF.Sqrt, bias=epsv[0:1, 0:1]),
             reads=("strip2",), writes=("strip3",))
        tk = S.fence("scalar")
        S.wait("vector", tk)
        S.op("vector", lambda E: E.reciprocal(strip2[0:1, 0:n], strip3[0:1, 0:n]),
             reads=("strip3",), writes=("strip2",))
        S.op("vector", lambda E: E.tensor_tensor(strip3[0:1, 0:n], strip1[0:1, 0:n], strip2[0:1, 0:n], ALU.mult),
             reads=("strip1", "strip2"), writes=("strip3",))
        S.op("vector", lambda E: E.tensor_scalar(strip3[0:1, 0:n], strip3[0:1, 0:n], -1.0, None, ALU.mult),
             reads=("strip3",), writes=("strip3",))
        tk = S.fence("vector")
        S.waits("tensor", [tk, free[f"pb{b_r}"], free[f"pb{b_mu}"]])
        S.emit("tensor", lambda E, b_r=b_r: E.matmul(pb[b_r][:, 0:n], ones1f[0:1, :], strip2[0:1, 0:n], start=True, stop=True))
        S.emit("tensor", lambda E, b_mu=b_mu: E.matmul(pb[b_mu][:, 0:n], ones1f[0:1, :], strip3[0:1, 0:n], start=True, stop=True))
        return S.fence("tensor")

    tok_val = S.dma("gpsimd", val_t[:], value_w, group="val")

    xf = [None]
    x1s_tok = {}
    p1_open = [False]
    p1_st = {}

    def p1_L(s):
        c0 = s * SUB
        n = SUB if s < 9 else WX - 9 * SUB
        S.waits("sync", [xf[0]])
        t_l = S.dma("sync", xin_t[:, :, 0:n], x_in[:, :, c0:c0 + n], group="xin")
        p1_st[s] = [c0, n, t_l]

    def p1_S(s):
        c0, n, t_l = p1_st[s]
        p1_open[0] = True
        S.waits("scalar", [t_l])
        tok_st = emit_stats(lambda c, n=n: xin_t[:, c, 0:n], n, stat_t, "xin")
        p1_st[s] = (c0, n, tok_st)

    def p1_A(s):
        p1_L(s)
        p1_S(s)

    def p1_BC(s):
        c0, n, tok_st = p1_st[s]
        tok_b = emit_ln_strips(n, tok_st, 2, 5)
        S.wait("vector", tok_b)
        for c in range(DC):
            S.op("vector", lambda E, c=c, n=n: E.tensor_tensor(
                zt_t[:, 0, 0:n], xin_t[:, c, 0:n], pb[2][:, 0:n], ALU.mult),
                reads=("xin",), writes=("zt0",))
            S.op("vector", lambda E, c=c, n=n, c0=c0: E.tensor_tensor(
                x1_t[:, c, c0:c0 + n], zt_t[:, 0, 0:n], pb[5][:, 0:n], ALU.add),
                reads=("zt0",), writes=("x1",))
        xf[0] = S.fence("vector")
        free["pb2"] = xf[0]
        free["pb5"] = xf[0]
        x1s_tok[s] = xf[0]
        p1_open[0] = False

    def w_chunk_ap(which, i):
        if which == 0:
            return w1_w[i // 2, :, 512 * (i % 2): 512 * (i % 2 + 1)]
        d, ss = divmod(i, 8)
        return w2_w[d, :, 512 * ss: 512 * (ss + 1)]

    def m_chunk_ap(which, i):
        if which == 0:
            return m1_dram[:, i // 2, 512 * (i % 2): 512 * (i % 2 + 1)]
        d, ss = divmod(i, 8)
        return m2_dram[:, d, 512 * ss: 512 * (ss + 1)]

    wa_ld = {}
    red_tok = {}

    def wa_load(gi):
        if gi >= 128:
            return
        S.waits("scalar", [red_tok.get(gi - 6)])
        which, i = divmod(gi, 64)
        wa_ld[gi] = S.dma("scalar", wstream[:, gi % 6, :], w_chunk_ap(which, i), group=f"wl{gi % 6}")

    def wa_item(gi):
        wa_load(gi + 4)
        S.wait("vector", wa_ld[gi])
        S.op("vector", lambda E, gi=gi: E.reduce_sum(
            wpart[:, gi:gi + 1], wstream[:, gi % 6, :], axis=AX.X, apply_absolute_value=True),
            reads=("wstream",), writes=("wpart",))
        red_tok[gi] = S.fence("vector")

    tok_ws_c = [None]

    def emit_wfin():
        S.op("vector", lambda E: E.reduce_sum(wpart[:, 128:129], wpart[:, 0:64], axis=AX.X),
             reads=("wpart",), writes=("wpart",))
        S.op("vector", lambda E: E.reduce_sum(wpart[:, 129:130], wpart[:, 64:128], axis=AX.X),
             reads=("wpart",), writes=("wpart",))
        tk = S.fence("vector")
        S.waits("tensor", [tk, free["stats"]])
        S.emit("tensor", lambda E: E.matmul(ps_s1[0:1, 0:1], wpart[:, 128:129], onesf[:, 0:1], start=True, stop=True))
        tk = S.fence("tensor")
        S.wait("vector", tk)
        S.op("vector", lambda E: E.tensor_copy(strip1[0:1, 0:1], ps_s1[0:1, 0:1]), writes=("strip1",))
        tk = S.fence("vector")
        S.wait("tensor", tk)
        S.emit("tensor", lambda E: E.matmul(ps_s1[0:1, 0:1], wpart[:, 129:130], onesf[:, 0:1], start=True, stop=True))
        tk = S.fence("tensor")
        S.wait("vector", tk)
        S.op("vector", lambda E: E.tensor_copy(strip1[0:1, 1:2], ps_s1[0:1, 0:1]), writes=("strip1",))
        free["stats"] = S.fence("vector")
        S.op("vector", lambda E: E.tensor_scalar(strip1[0:1, 2:4], strip1[0:1, 0:2], 1.0 / (H * D), 1e-5, ALU.mult, ALU.max),
             reads=("strip1",), writes=("strip1",))
        S.op("vector", lambda E: E.reciprocal(strip1[0:1, 4:6], strip1[0:1, 2:4]),
             reads=("strip1",), writes=("strip1",))
        S.op("vector", lambda E: E.tensor_scalar(strip1[0:1, 6:8], strip1[0:1, 2:4], 1.0 / 127.0, None, ALU.mult),
             reads=("strip1",), writes=("strip1",))
        tk = S.fence("vector")
        S.waits("tensor", [tk, free["pb5"]])
        S.emit("tensor", lambda E: E.matmul(pb[5][:, 0:4], ones1f[0:1, :], strip1[0:1, 4:8], start=True, stop=True))
        tk = S.fence("tensor")
        S.wait("vector", tk)
        S.op("vector", lambda E: E.tensor_copy(wsvec[:, 0:4], pb[5][:, 0:4]), writes=("wsvec",))
        tok_ws_c[0] = S.fence("vector")
        free["pb5"] = tok_ws_c[0]
        if "ws" in dbg:
            S.wait("scalar", tok_ws_c[0])
            S.dma("scalar", dbg["ws"][:, 0:4], wsvec[:, 0:4], group="dbg")

    q_tok = {}
    st_toks = {}
    wq_ld = {}

    def wq_load(gi):
        if gi >= 128:
            return
        S.waits("scalar", [red_tok[127], q_tok.get(gi - 6)])
        which, i = divmod(gi, 64)
        wq_ld[gi] = S.dma("scalar", wstream[:, gi % 6, :], w_chunk_ap(which, i), group=f"wq{gi % 6}")

    def wq_item(gi):
        wq_load(gi + 4)
        which, i = divmod(gi, 64)
        wscol = wsvec[:, which:which + 1]
        buf = wstream[:, gi % 6, :]
        mb = mq_m[:, gi % 2, :]
        S.waits("vector", [wq_ld[gi], tok_alpha, tok_ws_c[0], st_toks.get(gi - 2)])
        S.op("vector", lambda E, b=buf, w=wscol: E.tensor_scalar(mq_u[:], b, w, None, ALU.mult),
             reads=("wstream", "wsvec"), writes=("mq_u",))
        S.op("vector", lambda E: E.tensor_scalar(mq_b[:], mq_u[:], M1C, -M1C, ALU.add, ALU.add),
             reads=("mq_u",), writes=("mq_b",))
        S.op("vector", lambda E: E.tensor_scalar(mq_d[:], mq_u[:], M2C, -M2C, ALU.add, ALU.add),
             reads=("mq_u",), writes=("mq_d",))
        S.op("vector", lambda E: E.scalar_tensor_tensor(mq_u[:], mq_b[:], alpha[:], mq_d[:], ALU.mult, ALU.add),
             reads=("mq_b", "mq_d", "alpha"), writes=("mq_u",))
        S.op("vector", lambda E, m=mb: E.tensor_scalar(m, mq_u[:], 1.0, -1.0, ALU.min, ALU.max),
             reads=("mq_u",), writes=("mq_m",))
        q_tok[gi] = S.fence("vector")
        S.wait("scalar", q_tok[gi])
        st_toks[gi] = S.dma("scalar", m_chunk_ap(which, i), mb, group=f"mst{gi % 2}")

    work_q = []
    for gi in range(56, 128):
        work_q.append(("wa", gi, 1))
    work_q.append(("fin", 0, 4))
    for gi in range(128):
        work_q.append(("wq", gi, 4))

    def run_item(it):
        kind, gi, _ = it
        if kind == "wa":
            wa_item(gi)
        elif kind == "fin":
            emit_wfin()
            for g in range(4):
                wq_load(g)
        else:
            wq_item(gi)

    def sa_hook(budget):
        while budget > 0 and work_q:
            if work_q[0][0] == "fin" and p1_open[0]:
                break
            it = work_q.pop(0)
            run_item(it)
            budget -= it[2]

    for g in range(4):
        wa_load(g)
    p1_A(0)
    for gi in range(0, 8):
        wa_item(gi)
    p1_BC(0)
    p1_A(1)
    for gi in range(8, 24):
        wa_item(gi)
    p1_BC(1)
    p1_A(2)
    for gi in range(24, 40):
        wa_item(gi)
    p1_BC(2)
    for gi in range(40, 56):
        wa_item(gi)

    p1_sched = {}
    for i in range(4):
        subs = [ss for ss in (2 * i + 3, 2 * i + 4) if ss <= 9]
        if subs:
            p1_sched[(i, 0)] = ("L", subs[0])
            p1_sched[(i, 1)] = ("S", subs[0])
            p1_sched[(i, 3)] = ("BC", subs[0])
            if len(subs) > 1:
                p1_sched[(i, 4)] = ("L", subs[1])
                p1_sched[(i, 5)] = ("S", subs[1])
                p1_sched[(i, 7)] = ("BC", subs[1])

    x2_sub_store = {}
    acc_free_scalar = None
    prev_ST_out2_v = None
    v_half_free = [None, None]
    n_dc_total = len(ST_SUBS) * DC
    gthird_seq = []
    for sti in range(len(ST_SUBS)):
        for dc in range(DC):
            for third in range(3):
                gthird_seq.append((sti, dc, third))
    g_loaded = {}
    g_slot_free = {}
    g_next = [0]

    def prefetch_gates(upto):
        while g_next[0] < min(upto, len(gthird_seq)):
            gi = g_next[0]
            sti_, dc_, third_ = gthird_seq[gi]
            slot = gi % 2
            S.waits("gpsimd", [g_slot_free.get(gi - 2)])
            g_loaded[gi] = S.dma("gpsimd", gate_t[:, slot, :, :],
                                 gate_w[dc_, :, 3 * third_: 3 * third_ + 3, :],
                                 group=f"gate{slot}")
            g_next[0] += 1

    gate_banks = [0, 1, 3, 4]
    g_free = {0: None, 1: None, 2: None}

    for sti, (s_lo, s_hi) in enumerate(ST_SUBS):
        W = (s_hi - s_lo) * SUB
        o_lo = s_lo * SUB
        xv_lo = o_lo + X_OFF - 32
        WV = W + 64
        bank_i = [0]
        mac_fence_dc = {}
        for dc in range(DC):
            half = dc % 2
            vb = v_t[:, half, :]
            S.waits("tensor", [tok_val, x1s_tok[s_hi]])
            S.waits("scalar", [v_half_free[half]])
            nvs = (WV + 511) // 512
            vban = [2, 5]
            for sv in range(nvs):
                a = sv * 512
                nn = min(512, WV - a)
                bk = vban[sv % 2]
                S.waits("tensor", [free[f"pb{bk}"]])
                for kc in range(DC):
                    S.emit("tensor", lambda E, dc=dc, kc=kc, a=a, nn=nn, xv_lo=xv_lo, bk=bk: E.matmul(
                        pb[bk][:, 0:nn], val_t[:, kc, dc * 128:(dc + 1) * 128],
                        x1_t[:, kc, xv_lo + a: xv_lo + a + nn],
                        start=(kc == 0), stop=(kc == DC - 1)))
                tk = S.fence("tensor")
                S.waits("scalar", [tk])
                S.op("scalar", lambda E, a=a, nn=nn, vb=vb, bk=bk: E.copy(vb[:, a:a + nn], pb[bk][:, 0:nn]),
                     writes=("v",))
                free[f"pb{bk}"] = S.fence("scalar")
            tok_v = free[f"pb{vban[(nvs - 1) % 2]}"]
            g_base = (sti * DC + dc) * 3
            prefetch_gates(g_base + 2)
            mac_tok = {}
            for j in range(NJ):
                gthird = g_base + j // 3
                slot = gthird % 2
                jj = j % 3
                gb = g_t[:, j % 3, :]
                S.waits("tensor", [g_loaded[gthird]])
                nsub = s_hi - s_lo
                for sv in range(nsub):
                    a = sv * SUB
                    gsub = j * nsub + sv
                    bk = gate_banks[bank_i[0] % 4]
                    bank_i[0] += 1
                    S.waits("tensor", [free[f"pb{bk}"]])
                    for kc in range(DC):
                        S.emit("tensor", lambda E, slot=slot, jj=jj, kc=kc, a=a, bk=bk, o_lo=o_lo: E.matmul(
                            pb[bk][:, 0:SUB], gate_t[:, slot, jj, kc * 128:(kc + 1) * 128],
                            x1_t[:, kc, o_lo + X_OFF + a: o_lo + X_OFF + a + SUB],
                            start=(kc == 0), stop=(kc == DC - 1)))
                    tk = S.fence("tensor")
                    last_mm_tk = tk
                    S.waits("scalar", [tk, g_free[j % 3]])
                    S.op("scalar", lambda E, a=a, gb=gb, bk=bk: E.activation(gb[:, a:a + SUB], pb[bk][:, 0:SUB], AF.Sigmoid),
                         writes=("g",))
                    free[f"pb{bk}"] = S.fence("scalar")
                tok_g = free[f"pb{gate_banks[(bank_i[0] - 1) % 4]}"]
                sh = SHIFTS[j]
                S.waits("vector", [tok_g, tok_v, acc_free_scalar])
                S.op("vector", lambda E, gb=gb, vb=vb, sh=sh, W=W: E.tensor_tensor(
                    prod_t[:, 0:W], gb[:, 0:W], vb[:, 32 + sh: 32 + sh + W], ALU.mult),
                    reads=("g", "v"), writes=("prod",))
                if j == 0:
                    S.op("vector", lambda E, dc=dc, W=W: E.tensor_copy(acc_t[:, dc, 0:W], prod_t[:, 0:W]),
                         reads=("prod",), writes=("acc",))
                else:
                    S.op("vector", lambda E, dc=dc, W=W: E.tensor_tensor(
                        acc_t[:, dc, 0:W], acc_t[:, dc, 0:W], prod_t[:, 0:W], ALU.add),
                        reads=("prod", "acc"), writes=("acc",))
                mac_tok[j] = S.fence("vector")
                g_free[j % 3] = mac_tok[j]
                if j % 3 == 2:
                    g_slot_free[gthird] = last_mm_tk
                    prefetch_gates(gthird + 3)
            v_half_free[half] = mac_tok[8]
            mac_fence_dc[dc] = mac_tok[8]
            piece = p1_sched.get((sti, dc))
            if piece is not None:
                fn = {"L": p1_L, "S": p1_S, "BC": p1_BC}[piece[0]]
                fn(piece[1])
                sa_hook(14 if piece[0] == "L" else 6)
            else:
                sa_hook(16)
        tok_acc_done = S.fence("vector")

        for ssub in range(s_lo, s_hi):
            a = (ssub - s_lo) * SUB
            n = SUB
            S.waits("sync", [xf[0]])
            t_xr = S.dma("sync", xin_t[:, :, 0:n], x_in[:, :, ssub * SUB + X_OFF: ssub * SUB + X_OFF + n], group="xin")
            tok_st = emit_stats(lambda c, a=a, n=n: acc_t[:, c, a:a + n], n, stat_t, "acc",
                                chunk_tok=lambda c: mac_fence_dc[c])
            tok_b = emit_ln_strips(n, tok_st)
            S.waits("vector", [tok_b, prev_ST_out2_v, x2_sub_store.get(ssub - 1), x2_sub_store.get((ssub - 1, "dbg"))])
            for c in range(DC):
                S.op("vector", lambda E, c=c, a=a, n=n: E.tensor_tensor(
                    zt_t[:, 0, 0:n], acc_t[:, c, a:a + n], pb[3][:, 0:n], ALU.mult),
                    reads=("acc",), writes=("zt0",))
                S.op("vector", lambda E, n=n: E.tensor_tensor(
                    zt_t[:, 1, 0:n], zt_t[:, 0, 0:n], pb[4][:, 0:n], ALU.add),
                    reads=("zt0",), writes=("zt1",))
                S.op("vector", lambda E, c=c, n=n: E.tensor_scalar(
                    out2_t[:, c, 0:n], zt_t[:, 1, 0:n], gamma_s[:, c:c + 1], beta_s[:, c:c + 1], ALU.mult, ALU.add),
                    reads=("zt1",), writes=("out2",))
            tok_out2 = S.fence("vector")
            free["pb3"] = tok_out2
            free["pb4"] = tok_out2
            S.waits("scalar", [tok_out2])
            tok_st = emit_stats(lambda c, n=n: out2_t[:, c, 0:n], n, stat_t, "out2")
            tok_b = emit_ln_strips(n, tok_st)
            S.waits("vector", [tok_b, t_xr])
            for c in range(DC):
                S.op("vector", lambda E, c=c, n=n: E.tensor_tensor(
                    zt_t[:, 0, 0:n], out2_t[:, c, 0:n], pb[3][:, 0:n], ALU.mult),
                    reads=("out2",), writes=("zt0",))
                S.op("vector", lambda E, c=c, n=n: E.tensor_tensor(
                    zt_t[:, 1, 0:n], xin_t[:, c, 0:n], pb[4][:, 0:n], ALU.add),
                    reads=("xin",), writes=("zt1",))
                S.op("vector", lambda E, c=c, n=n: E.tensor_tensor(
                    out2_t[:, c, 0:n], zt_t[:, 0, 0:n], zt_t[:, 1, 0:n], ALU.add),
                    reads=("zt0", "zt1"), writes=("out2",))
            tok_x2 = S.fence("vector")
            free["pb3"] = tok_x2
            free["pb4"] = tok_x2
            xf[0] = tok_x2
            S.waits("scalar", [tok_x2])
            tsp = S.dma("scalar", x2_spill[:, :, ssub * SUB:(ssub + 1) * SUB], out2_t[:, :, 0:n], group="x2sp")
            x2_sub_store[ssub] = tsp
            if "x2" in dbg:
                x2_sub_store[(ssub, "dbg")] = S.dma("scalar", dbg["x2"][:, :, ssub * SUB:(ssub + 1) * SUB], out2_t[:, :, 0:n], group="dbgx2")
            sa_hook(4)
        acc_free_scalar = S.fence("scalar")
        prev_ST_out2_v = tok_x2
    sa_hook(10 ** 9)
    tok_mq_done = st_toks[127]
    tok_mq_done2 = st_toks[126]
    tok_x1dbg = None
    if "x1" in dbg:
        S.wait("scalar", x1s_tok[9])
        tok_x1dbg = S.dma("scalar", dbg["x1"][:], x1_t[:], group="dbgx1")
    tokA_v = S.fence("vector")
    tokA_s = S.fence("scalar")
    tokA_t = S.fence("tensor")
    tok_spill_all = x2_sub_store[NSUB - 1]
    tok_spill_dbg = x2_sub_store.get((NSUB - 1, 'dbg'))

    S.waits("gpsimd", [tok_mq_done, tok_mq_done2, tokA_v, tokA_s, tokA_t, tok_x1dbg])
    tok_m1 = S.dma("gpsimd", m1_t[:], m1_dram, group="m1")
    S.waits("vector", [tokA_v, tokA_s, tokA_t, tok_spill_all, tok_spill_dbg, tok_x1dbg])
    S.op("vector", lambda E: E.memset(m8_t[:], 0.0), writes=("m8",))
    S.op("vector", lambda E: E.memset(tmstt[:], 0.0), writes=("tmstt",))
    S.op("vector", lambda E: E.memset(tmstt2[:], 0.0), writes=("tmstt2",))
    S.op("vector", lambda E: E.memset(tmstt3[:], 0.0), writes=("tmstt3",))
    S.op("vector", lambda E: E.memset(sb4t[:], 0.0), writes=("sb4t",))
    S.op("vector", lambda E: E.memset(sb4bt[:], 0.0), writes=("sb4bt",))
    tok_b1_init = S.fence("vector")
    S.waits("scalar", [tok_spill_all, tok_spill_dbg, tok_x1dbg, tok_b1_init])
    S.waits("sync", [tok_spill_all, tok_spill_dbg, tok_x1dbg, tokA_v, tokA_s, tokA_t, tok_b1_init])
    S.waits("tensor", [tok_b1_init])

    n = HR
    nq = SUB
    free["strips"] = None
    free["m8h"] = None
    free["sb4h"] = None
    free["sb4tl"] = None
    ST8 = {}

    def H_step(t, k, st):
        if k == 0:
            lo = t * SUB - 1
            llo = max(lo, 0)
            lhi = min(lo + HR, WOUT)
            n_load = lhi - llo
            dst0 = llo - lo
            prev = ST8.get(t - 1, {})
            S.waits("sync", [prev.get("hx2_free"), prev.get("hx2_free_s")])
            st["t_x2"] = S.dma("sync", hx2_t[:, :, dst0:dst0 + n_load], x2_spill[:, :, llo:lhi], group="hx2")
            S.waits("vector", [st["t_x2"]])
            if dst0 > 0:
                S.op("vector", lambda E: E.memset(hx2_t[:, :, 0:1], 0.0), writes=("hx2",))
            if lo + HR > WOUT:
                S.op("vector", lambda E, e0=dst0 + n_load: E.memset(hx2_t[:, :, e0:HR], 0.0), writes=("hx2",))
            if t == 0:
                S.op("vector", lambda E: E.tensor_scalar(hx2_t[:, :, 1:5], hx2_t[:, :, 1:5], masks_s[:, 0:1], None, ALU.mult),
                     reads=("hx2",), writes=("hx2",))
            if t == NSUB - 1:
                S.op("vector", lambda E: E.tensor_scalar(hx2_t[:, :, 453:457], hx2_t[:, :, 453:457], masks_s[:, 1:2], None, ALU.mult),
                     reads=("hx2",), writes=("hx2",))
            st["tok_x2in"] = S.fence("vector")
        elif k == 1:
            S.waits("scalar", [st["tok_x2in"]])
            st["tok_stats"] = emit_stats(lambda c: hx2_t[:, c, 0:n], n, bstat_t, "hx2")
            st["stats_sc"] = S.fence("scalar")
        elif k == 2:
            S.waits("vector", [st["tok_stats"], free["strips"]])
            S.op("vector", lambda E: E.tensor_scalar(strip1[0:1, 0:n], ps_s1[0:1, 0:n], 1.0 / D, None, ALU.mult),
                 writes=("strip1",))
            S.op("vector", lambda E: E.tensor_scalar(strip2[0:1, 0:n], ps_s2[0:1, 0:n], 1.0 / D, None, ALU.mult),
                 writes=("strip2",))
            free["stats"] = S.fence("vector")
            S.op("vector", lambda E: E.tensor_tensor(strip3[0:1, 0:n], strip1[0:1, 0:n], strip1[0:1, 0:n], ALU.mult),
                 reads=("strip1",), writes=("strip3",))
            S.op("vector", lambda E: E.scalar_tensor_tensor(strip2[0:1, 0:n], strip3[0:1, 0:n], -1.0, strip2[0:1, 0:n], ALU.mult, ALU.add),
                 reads=("strip2", "strip3"), writes=("strip2",))
            tk = S.fence("vector")
            S.wait("scalar", tk)
            S.op("scalar", lambda E: E.activation(strip3[0:1, 0:n], strip2[0:1, 0:n], AF.Sqrt, bias=epsv[0:1, 0:1]),
                 reads=("strip2",), writes=("strip3",))
            st["sq_done"] = S.fence("scalar")
        elif k == 3:
            S.wait("vector", st["sq_done"])
            S.op("vector", lambda E: E.reciprocal(strip2[0:1, 0:n], strip3[0:1, 0:n]),
                 reads=("strip3",), writes=("strip2",))
            S.op("vector", lambda E: E.tensor_scalar(strip3[0:1, 0:n], strip1[0:1, 0:n], -1.0, None, ALU.mult),
                 reads=("strip1",), writes=("strip3",))
            tk = S.fence("vector")
            S.waits("tensor", [tk, free["pb3"]])
            S.emit("tensor", lambda E: E.matmul(pb[3][:, 0:n], ones1f[0:1, :], strip3[0:1, 0:n], start=True, stop=True))
            st["mu_bc"] = S.fence("tensor")
        elif k == 4:
            S.waits("vector", [st["mu_bc"], ST8.get(t - 1, {}).get("tok_k3")])
            for c in range(DC):
                S.op("vector", lambda E, c=c: E.tensor_tensor(u_t[:, c, 0:n], hx2_t[:, c, 0:n], pb[3][:, 0:n], ALU.add),
                     reads=("hx2",), writes=("u",))
            st["tok_u"] = S.fence("vector")
            free["pb3"] = st["tok_u"]
            st["hx2_free"] = st["tok_u"]
            st["hx2_free_s"] = st["stats_sc"]
        elif k == 5:
            S.waits("vector", [free["m8h"]])
            S.op("vector", lambda E: E.reduce_max(m8_t[:, 0, 0:n], u_t[:].rearrange("p c n -> p n c"), axis=AX.X, apply_absolute_value=True),
                 reads=("u",), writes=("m8",))
            tk = S.fence("vector")
            S.waits("tensor", [tk, free["pb2"]])
            for bkk in range(4):
                S.emit("tensor", lambda E, bkk=bkk: E.transpose(
                    pb[2][:, bkk * 128:(bkk + 1) * 128], m8_t[:, 0, bkk * 128:(bkk + 1) * 128], ident[:]))
            S.waits("tensor", [free["pb5"], free["stats"]])
            for bkk in range(4):
                S.emit("tensor", lambda E, bkk=bkk: E.transpose(
                    pb[5][:, bkk:bkk + 1], strip2[0:1, bkk * 128:(bkk + 1) * 128], ident[0:1, 0:1]))
            tk_tr = S.fence("tensor")
            S.wait("vector", tk_tr)
            S.op("vector", lambda E: E.reduce_max(
                tmst[:, 0:4], pb[2][:, 0:512].rearrange("p (b q) -> p b q", q=128), axis=AX.X),
                writes=("tmst",))
            S.op("vector", lambda E: E.tensor_copy(tmst2[:, 0:4], pb[5][:, 0:4]), writes=("tmst2",))
            st["tm1"] = S.fence("vector")
            free["m8h"] = st["tm1"]
            free["pb2"] = st["tm1"]
            free["pb5"] = st["tm1"]
        elif k == 6:
            S.op("vector", lambda E: E.tensor_tensor(tmst[:, 4:8], tmst[:, 0:4], tmst2[:, 0:4], ALU.mult),
                 reads=("tmst", "tmst2"), writes=("tmst",))
            S.op("vector", lambda E: E.tensor_scalar(tmst[:, 8:12], tmst[:, 4:8], 1e-5, None, ALU.max),
                 reads=("tmst",), writes=("tmst",))
            S.op("vector", lambda E: E.reciprocal(tmst3[:, 0:4], tmst[:, 8:12]),
                 reads=("tmst",), writes=("tmst3",))
            S.op("vector", lambda E: E.tensor_scalar(tmst3[:, 4:8], tmst3[:, 0:4], 127.0, None, ALU.mult),
                 reads=("tmst3",), writes=("tmst3",))
            S.op("vector", lambda E: E.tensor_tensor(tmst3[:, 8:12], tmst3[:, 4:8], tmst2[:, 0:4], ALU.mult),
                 reads=("tmst3", "tmst2"), writes=("tmst3",))
            S.op("vector", lambda E: E.tensor_scalar(tmst2[:, 4:8], tmst[:, 8:12], wsvec[:, 2:3], None, ALU.mult),
                 reads=("tmst", "wsvec"), writes=("tmst2",))
            tk = S.fence("vector")
            S.waits("tensor", [tk, free["pb5"], free["stats"], free["sb4h"]])
            for bkk in range(4):
                S.emit("tensor", lambda E, bkk=bkk: E.transpose(
                    pb[5][0:1, bkk * 128:(bkk + 1) * 128], tmst3[:, 8 + bkk:9 + bkk], ident[:]))
                S.emit("tensor", lambda E, bkk=bkk: E.transpose(
                    ps_s2[0:1, bkk * 128:(bkk + 1) * 128], tmst2[:, 4 + bkk:5 + bkk], ident[:]))
            tk_tr = S.fence("tensor")
            S.wait("vector", tk_tr)
            S.op("vector", lambda E: E.tensor_copy(sb4[0:1, 0:512], pb[5][0:1, 0:512]), writes=("sb4",))
            S.op("vector", lambda E: E.tensor_copy(sb4b[0:1, 0:512], ps_s2[0:1, 0:512]), writes=("sb4b",))
            tk = S.fence("vector")
            free["pb5"] = tk
            free["stats"] = tk
            S.waits("tensor", [tk, free["pb4"], free["pb5"]])
            S.emit("tensor", lambda E: E.matmul(pb[4][:, 0:512], ones1f[0:1, :], sb4[0:1, 0:512], start=True, stop=True))
            S.emit("tensor", lambda E: E.matmul(pb[5][:, 0:512], ones1f[0:1, :], sb4b[0:1, 0:512], start=True, stop=True))
            st["bc_xq"] = S.fence("tensor")
            free["sb4h"] = st["bc_xq"]
            free["strips"] = st["bc_xq"]
        elif k == 7:
            kslot = t % 2
            S.waits("vector", [st["bc_xq"], ST8.get(t - 1, {}).get("w1_done")])
            for c in range(DC):
                S.op("vector", lambda E, c=c: E.tensor_tensor(hq_t[:, 0, 0:n], u_t[:, c, 0:n], pb[4][:, 0:n], ALU.mult),
                     reads=("u",), writes=("hq0",))
                S.op("vector", lambda E: E.tensor_scalar(qb_t[:, 0, 0:n], hq_t[:, 0, 0:n], M1C, -M1C, ALU.add, ALU.add),
                     reads=("hq0",), writes=("qb0",))
                S.op("vector", lambda E: E.tensor_scalar(qd_t[:, 0, 0:n], hq_t[:, 0, 0:n], M2C, -M2C, ALU.add, ALU.add),
                     reads=("hq0",), writes=("qd0",))
                S.op("vector", lambda E: E.scalar_tensor_tensor(hq_t[:, 1, 0:n], qb_t[:, 0, 0:n], alpha[:], qd_t[:, 0, 0:n], ALU.mult, ALU.add),
                     reads=("qb0", "qd0", "alpha"), writes=("hq1",))
                S.op("vector", lambda E, c=c, kslot=kslot: E.tensor_tensor(k3_t[:, kslot, c, 0:n], hq_t[:, 1, 0:n], pb[5][:, 0:n], ALU.mult),
                     reads=("hq1",), writes=("k3",))
            st["tok_k3"] = S.fence("vector")
            free["pb4"] = st["tok_k3"]
            free["pb5"] = st["tok_k3"]

    NHEAD = 8

    def T_step(t, k, st):
        kslot = t % 2
        if k < 8:
            if k == 0:
                S.waits("tensor", [tok_m1, st["tok_k3"]])
                st["silu_pending"] = []
            for hc in range(4 * k, 4 * k + 4):
                bkid = hc % 3
                pbk = pb[bkid]
                pp = 3 * (hc % 2)
                S.waits("tensor", [free[f"pb{bkid}"]])
                for kc in range(DC):
                    S.emit("tensor", lambda E, hc=hc, kc=kc, pbk=pbk, kslot=kslot: E.matmul(
                        pbk[:, 0:n], m1_t[:, hc, kc * 128:(kc + 1) * 128], k3_t[:, kslot, kc, 0:n],
                        start=(kc == 0), stop=(kc == DC - 1)))
                tk_h = S.fence("tensor")
                S.waits("scalar", [tk_h])
                S.op("scalar", lambda E, hc=hc, pbk=pbk, pp=pp: E.activation(
                    cv_t[:, pp + 0, 0:n - 2], pbk[:, 0:n - 2], AF.Copy,
                    scale=convw_s[:, 3 * hc:3 * hc + 1]),
                    writes=(f"cv0{pp}",))
                tk0 = S.fence("scalar")
                if st["silu_pending"]:
                    phc, ppp, ptk2 = st["silu_pending"].pop()
                    S.waits("scalar", [ptk2])
                    S.op("scalar", lambda E, phc=phc, ppp=ppp: E.activation(
                        hs_t[:, phc, 0:n - 2], cv_t[:, ppp + 2, 0:n - 2], AF.Silu,
                        bias=convb_s[:, phc:phc + 1]),
                        reads=(f"cv2{ppp}",), writes=("hs",))
                S.waits("vector", [tk0, tk_h])
                S.op("vector", lambda E, hc=hc, pbk=pbk, pp=pp: E.scalar_tensor_tensor(
                    cv_t[:, pp + 1, 0:n - 2], pbk[:, 1:n - 1], convw_s[:, 3 * hc + 1:3 * hc + 2], cv_t[:, pp + 0, 0:n - 2], ALU.mult, ALU.add),
                    reads=(f"cv0{pp}",), writes=(f"cv1{pp}",))
                S.op("vector", lambda E, hc=hc, pbk=pbk, pp=pp: E.scalar_tensor_tensor(
                    cv_t[:, pp + 2, 0:n - 2], pbk[:, 2:n], convw_s[:, 3 * hc + 2:3 * hc + 3], cv_t[:, pp + 1, 0:n - 2], ALU.mult, ALU.add),
                    reads=(f"cv1{pp}",), writes=(f"cv2{pp}",))
                tk2 = S.fence("vector")
                free[f"pb{bkid}"] = tk2
                st["silu_pending"].append((hc, pp, tk2))
            if k == 7:
                st["w1_done"] = tk2
                for phc, ppp, ptk2 in st["silu_pending"]:
                    S.waits("scalar", [ptk2])
                    S.op("scalar", lambda E, phc=phc, ppp=ppp: E.activation(
                        hs_t[:, phc, 0:n - 2], cv_t[:, ppp + 2, 0:n - 2], AF.Silu,
                        bias=convb_s[:, phc:phc + 1]),
                        reads=(f"cv2{ppp}",), writes=("hs",))
                st["tok_hs"] = S.fence("scalar")
        elif k == 8:
            S.waits("vector", [st["tok_hs"], free.get("m8t")])
            S.op("vector", lambda E: E.memset(m8_t[:, 1, nq:512], 0.0), writes=("m8t",))
            S.op("vector", lambda E: E.reduce_max(m8_t[:, 1, 0:nq], hs_t[:, :, 0:nq].rearrange("p c n -> p n c"), axis=AX.X, apply_absolute_value=True),
                 reads=("hs",), writes=("m8t",))
            st["m8b"] = S.fence("vector")
        elif k == 9:
            S.waits("tensor", [st["m8b"], free["pb2"]])
            for bkk in range(4):
                S.emit("tensor", lambda E, bkk=bkk: E.transpose(
                    pb[2][:, bkk * 128:(bkk + 1) * 128], m8_t[:, 1, bkk * 128:(bkk + 1) * 128], ident[:]))
            tk_tr = S.fence("tensor")
            S.wait("vector", tk_tr)
            S.op("vector", lambda E: E.reduce_max(
                tmstt[:, 0:4], pb[2][:, 0:512].rearrange("p (b q) -> p b q", q=128), axis=AX.X),
                writes=("tmstt",))
            free["pb2"] = S.fence("vector")
            free["m8t"] = free["pb2"]
            S.op("vector", lambda E: E.tensor_scalar(tmstt[:, 4:8], tmstt[:, 0:4], 1e-5, None, ALU.max),
                 reads=("tmstt",), writes=("tmstt",))
            S.op("vector", lambda E: E.reciprocal(tmstt2[:, 0:4], tmstt[:, 4:8]),
                 reads=("tmstt",), writes=("tmstt2",))
            S.op("vector", lambda E: E.tensor_scalar(tmstt2[:, 4:8], tmstt2[:, 0:4], 127.0, None, ALU.mult),
                 reads=("tmstt2",), writes=("tmstt2",))
            S.op("vector", lambda E: E.tensor_scalar(tmstt3[:, 0:4], tmstt[:, 4:8], wsvec[:, 3:4], None, ALU.mult),
                 reads=("tmstt", "wsvec"), writes=("tmstt3",))
            tk = S.fence("vector")
            S.waits("tensor", [tk, free["pb5"], free["stats"], free["sb4tl"]])
            for bkk in range(4):
                S.emit("tensor", lambda E, bkk=bkk: E.transpose(
                    pb[5][0:1, bkk * 128:(bkk + 1) * 128], tmstt2[:, 4 + bkk:5 + bkk], ident[:]))
                S.emit("tensor", lambda E, bkk=bkk: E.transpose(
                    ps_s2[0:1, bkk * 128:(bkk + 1) * 128], tmstt3[:, bkk:bkk + 1], ident[:]))
            tk_tr = S.fence("tensor")
            S.waits("vector", [tk_tr, ST8.get(t - 1, {}).get("c2st")])
            S.op("vector", lambda E: E.tensor_copy(sb4t[0:1, 0:512], pb[5][0:1, 0:512]), writes=("sb4t",))
            S.op("vector", lambda E: E.tensor_copy(sb4bt[0:1, 0:512], ps_s2[0:1, 0:512]), writes=("sb4bt",))
            tk = S.fence("vector")
            free["pb5"] = tk
            free["stats"] = tk
            S.waits("scalar", [tk, ST8.get(t - 1, {}).get("c2st")])
            st["c2st"] = S.dma("scalar", c2_dram[0:1, t * SUB: t * SUB + 512], sb4bt[0:1, 0:512], group="c2st")
            S.waits("tensor", [tk, free["pb3"]])
            S.emit("tensor", lambda E: E.matmul(pb[3][:, 0:512], ones1f[0:1, :], sb4t[0:1, 0:512], start=True, stop=True))
            st["bc_s2"] = S.fence("tensor")
            free["sb4tl"] = st["bc_s2"]
        elif k == 10:
            o_lo2 = t * SUB
            klo = max(o_lo2, OUT_OFF)
            khi = min(o_lo2 + SUB, OUT_OFF + CT)
            a0 = klo - o_lo2
            S.waits("vector", [st["bc_s2"]])
            prev = ST8.get(t - 1, {})
            k2st = {-2: prev.get("k2_last0", []), -1: prev.get("k2_last1", [])}
            for hc in range(HC):
                S.waits("vector", list(k2st.get(hc - 2, [])))
                S.op("vector", lambda E, hc=hc: E.tensor_tensor(hq_t[:, 1, 0:nq], hs_t[:, hc, 0:nq], pb[3][:, 0:nq], ALU.mult),
                     reads=("hs",), writes=("hq1",))
                S.op("vector", lambda E: E.tensor_scalar(qb_t[:, 1, 0:nq], hq_t[:, 1, 0:nq], M1C, -M1C, ALU.add, ALU.add),
                     reads=("hq1",), writes=("qb1",))
                S.op("vector", lambda E: E.tensor_scalar(qd_t[:, 1, 0:nq], hq_t[:, 1, 0:nq], M2C, -M2C, ALU.add, ALU.add),
                     reads=("hq1",), writes=("qd1",))
                S.op("vector", lambda E, hc=hc: E.scalar_tensor_tensor(k2_t[:, hc % 2, 0:nq], qb_t[:, 1, 0:nq], alpha[:], qd_t[:, 1, 0:nq], ALU.mult, ALU.add),
                     reads=("qb1", "qd1", "alpha"), writes=(f"k2{hc % 2}",))
                tkk = S.fence("vector")
                S.waits("scalar", [tkk])
                toks = [S.dma("scalar", k2_spill[:, hc, klo - OUT_OFF:khi - OUT_OFF],
                              k2_t[:, hc % 2, a0:a0 + (khi - klo)], group=f"k2st{hc % 2}")]
                if "k2" in dbg:
                    toks.append(S.dma("scalar", dbg["k2"][:, hc, klo - OUT_OFF:khi - OUT_OFF],
                                      k2_t[:, hc % 2, a0:a0 + (khi - klo)], group=f"dbgk2{hc % 2}"))
                k2st[hc] = toks
            st["k2_last0"] = k2st[30]
            st["k2_last1"] = k2st[31]
            free["pb3"] = S.fence("vector")

    NTAIL = 11
    ST8[0] = {}
    for k in range(NHEAD):
        H_step(0, k, ST8[0])
    for t in range(NSUB):
        st = ST8[t]
        if t + 1 < NSUB:
            ST8[t + 1] = {}
            for k in range(NTAIL):
                T_step(t, k, st)
                if k < NHEAD:
                    H_step(t + 1, k, ST8[t + 1])
        else:
            for k in range(NTAIL):
                T_step(t, k, st)

    k2_final = list(ST8[NSUB - 1]["k2_last0"]) + list(ST8[NSUB - 1]["k2_last1"])
    tok_k2_dbg = None
    c2st_tok = ST8[NSUB - 1]["c2st"]
    tokB1_v = S.fence("vector")
    tokB1_s = S.fence("scalar")
    tokB1_t = S.fence("tensor")

    S.waits("gpsimd", [ST8[NSUB - 1]["w1_done"], tok_m1])
    m2c = {}
    for j in range(4):
        m2c[j] = S.dma("gpsimd", m2_t[:, :, j * 1024:(j + 1) * 1024],
                       m2_dram[:, :, j * 1024:(j + 1) * 1024], group=f"m2{j}")

    st2 = {}
    w2_banks = [0, 1, 2]

    def b2_loads(t):
        st = st2.setdefault(t, {})
        kb = k2in_t[:, t % 2]
        prev2 = st2.get(t - 2, {})
        if t == 0:
            S.waits("sync", k2_final + [c2st_tok, tokB1_v, tokB1_s, tokB1_t])
        else:
            S.waits("sync", [prev2.get("mm_done"), st2[t - 1].get("c2bc_t")])
        olo = OUT_OFF + t * B2T
        st["c2"] = S.dma("sync", c2strip[0:1, 0:B2T], c2_dram[0:1, olo:olo + B2T], group="c2ld")
        if t == 0:
            st["x2"] = S.dma("sync", bx2_t[:], x2_spill[:, :, olo:olo + B2T], group="bx2")
        st["k2c"] = {}
        for j in range(4):
            st["k2c"][j] = S.dma(
                "sync", kb[:, j * 8:(j + 1) * 8, 0:B2T],
                k2_spill[:, j * 8:(j + 1) * 8, t * B2T:(t + 1) * B2T], group=f"k2i{t % 2}{j}")

    def b2_head(t):
        st = st2[t]
        kb = k2in_t[:, t % 2]
        yb = y_t2[:, t % 2]
        olo = OUT_OFF + t * B2T
        S.waits("tensor", [st["c2"], free["pb5"]])
        S.emit("tensor", lambda E: E.matmul(pb[5][:, 0:B2T], ones1f[0:1, :], c2strip[0:1, 0:B2T], start=True, stop=True))
        st["c2bc_t"] = S.fence("tensor")
        S.wait("vector", st["c2bc_t"])
        S.op("vector", lambda E: E.tensor_copy(c2sb_t[:], pb[5][:, 0:B2T]), writes=("c2sb",))
        tok_c2bc = S.fence("vector")
        free["pb5"] = tok_c2bc
        for d in range(DC):
            bkid = w2_banks[d % 3]
            pbk = pb[bkid]
            S.waits("tensor", [free[f"pb{bkid}"]])
            for kc in range(HC):
                S.waits("tensor", [st["k2c"][kc // 8]])
                if t == 0:
                    S.waits("tensor", [m2c[kc // 8]])
                S.emit("tensor", lambda E, d=d, kc=kc, kb=kb, pbk=pbk: E.matmul(
                    pbk[:, 0:B2T], m2_t[:, d, kc * 128:(kc + 1) * 128], kb[:, kc, 0:B2T],
                    start=(kc == 0), stop=(kc == HC - 1)))
            tk_y = S.fence("tensor")
            st["mm_done"] = tk_y
            S.waits("vector", [tk_y, tok_c2bc, st2.get(t - 2, {}).get("res_done")])
            S.op("vector", lambda E, d=d, pbk=pbk, yb=yb: E.tensor_tensor(yb[:, d, 0:B2T], pbk[:, 0:B2T], c2sb_t[:], ALU.mult),
                 reads=("c2sb",), writes=("y",))
            free[f"pb{bkid}"] = S.fence("vector")
        st["deq_done"] = free[f"pb{w2_banks[(DC - 1) % 3]}"]

    def b2_tail(t):
        st = st2[t]
        yb = y_t2[:, t % 2]
        bx = bx2_t
        S.waits("scalar", [st["deq_done"]])
        tok_st = emit_stats(lambda c, yb=yb: yb[:, c, 0:B2T], B2T, b2stat_t, "y")
        tok_b = emit_ln_strips(B2T, tok_st)
        S.waits("vector", [tok_b, st["x2"]])
        for c in range(DC):
            S.op("vector", lambda E, c=c, yb=yb: E.tensor_tensor(yb[:, c, 0:B2T], yb[:, c, 0:B2T], pb[3][:, 0:B2T], ALU.mult),
                 reads=("y",), writes=("y",))
            S.op("vector", lambda E, c=c, bx=bx: E.tensor_tensor(bx[:, c, 0:B2T], bx[:, c, 0:B2T], pb[4][:, 0:B2T], ALU.add),
                 reads=("bx2",), writes=("bx2",))
            S.op("vector", lambda E, c=c, yb=yb, bx=bx: E.tensor_tensor(bx[:, c, 0:B2T], yb[:, c, 0:B2T], bx[:, c, 0:B2T], ALU.add),
                 reads=("y", "bx2"), writes=("bx2",))
        st["res_done"] = S.fence("vector")
        free["pb3"] = st["res_done"]
        free["pb4"] = st["res_done"]
        S.waits("scalar", [st["res_done"]])
        st["store"] = S.dma("scalar", out_d[:, :, t * B2T:(t + 1) * B2T], bx[:, :, 0:B2T], group="outst")
        if t + 1 < NB2:
            nst = st2.setdefault(t + 1, {})
            S.waits("sync", [st["store"]])
            olo2 = OUT_OFF + (t + 1) * B2T
            nst["x2"] = S.dma("sync", bx2_t[:], x2_spill[:, :, olo2:olo2 + B2T], group="bx2")

    b2_loads(0)
    b2_head(0)
    b2_loads(1)
    for t in range(1, NB2):
        b2_head(t)
        if t + 1 < NB2:
            b2_loads(t + 1)
        b2_tail(t - 1)
    b2_tail(NB2 - 1)

    S.wait("gpsimd", ("d_scalar_outst", S.cnt["d_scalar_outst"]))

    with nc.Block() as block:
        @block.sync
        def _(E):
            for fn in S.q["sync"]:
                fn(E)

        @block.gpsimd
        def _(E):
            for fn in S.q["gpsimd"]:
                fn(E)

        @block.vector
        def _(E):
            for fn in S.q["vector"]:
                fn(E)

        @block.scalar
        def _(E):
            for fn in S.q["scalar"]:
                fn(E)

        @block.tensor
        def _(E):
            for fn in S.q["tensor"]:
                fn(E)

    stack.close()
    return nc



def _prep_inputs(x, gate_W, value_W, gamma, beta, w1, conv_w, conv_b, w2):
    gate_prep = np.ascontiguousarray(
        gate_W.reshape(NJ, DC, 128, DC, 128).transpose(1, 4, 0, 3, 2).reshape(DC, 128, NJ, D))
    value_prep = np.ascontiguousarray(
        value_W.reshape(DC, 128, DC, 128).transpose(3, 2, 0, 1).reshape(128, DC, D))
    w1_prep = np.ascontiguousarray(
        w1.reshape(HC, 128, DC, 128).transpose(0, 3, 2, 1).reshape(HC, 128, D))
    w2_prep = np.ascontiguousarray(
        w2.reshape(DC, 128, HC, 128).transpose(0, 3, 2, 1).reshape(DC, 128, H))
    conv_prep = np.ascontiguousarray(
        conv_w.reshape(HC, 128, 3).transpose(1, 0, 2).reshape(128, 3 * HC))
    convb_prep = np.ascontiguousarray(conv_b.reshape(HC, 128).T)
    gamma_prep = np.ascontiguousarray(gamma.reshape(DC, 128).T)
    beta_prep = np.ascontiguousarray(beta.reshape(DC, 128).T)
    ident = np.eye(128, dtype=np.float32)
    onesb = np.ones((128, 2), dtype=ml_dtypes.bfloat16)
    onesf = np.ones((128, 2), dtype=np.float32)

    shared = dict(gate_prep=gate_prep, value_prep=value_prep, w1_prep=w1_prep,
                  w2_prep=w2_prep, conv_prep=conv_prep, convb_prep=convb_prep,
                  gamma_prep=gamma_prep, beta_prep=beta_prep, ident=ident,
                  onesb=onesb, onesf=onesf)

    in_maps = []
    for core in range(8):
        b, half = core >> 1, core & 1
        g0 = half * (SEQ // 2)
        idx = (np.arange(g0 - HALO, g0 + CT + HALO)) % SEQ
        xs = np.ascontiguousarray(x[b][idx].T.reshape(DC, 128, WX).transpose(1, 0, 2))
        masks = np.ones((128, 2), dtype=np.float32)
        if half == 0:
            masks[:, 0] = 0.0
        else:
            masks[:, 1] = 0.0
        m = dict(shared)
        m["x_fm"] = xs
        m["masks"] = masks
        in_maps.append(m)
    return in_maps


def kernel(L0_attn_x, L0_attn_gate_W, L0_attn_value_W, L0_attn_gamma,
           L0_attn_beta, L0_ffn_w1, L0_ffn_conv_w, L0_ffn_conv_b, L0_ffn_w2,
           _debug=(), _core_ids=None):
    inp = [np.asarray(a, dtype=np.float32) for a in
           (L0_attn_x, L0_attn_gate_W, L0_attn_value_W, L0_attn_gamma,
            L0_attn_beta, L0_ffn_w1, L0_ffn_conv_w, L0_ffn_conv_b, L0_ffn_w2)]
    in_maps = _prep_inputs(*inp)
    nc = build_program(debug=_debug)
    core_ids = list(range(8)) if _core_ids is None else _core_ids
    res = run_bass_kernel_spmd(nc, in_maps[: len(core_ids)], core_ids=core_ids)
    out = np.empty((B, SEQ, D), dtype=np.float32)
    for core in range(len(core_ids)):
        b, half = core >> 1, core & 1
        g0 = half * (SEQ // 2)
        o = res.results[core]["out_fm"]
        out[b, g0:g0 + CT, :] = np.asarray(o).transpose(2, 1, 0).reshape(CT, D)
    if _debug:
        return out, res
    return out
